# version 01
# baseline (speedup 1.0000x reference)
import numpy as np
import ml_dtypes
from contextlib import ExitStack
import concourse.bass as bass
import concourse.mybir as mybir
from concourse.bass_utils import run_bass_kernel_spmd

F32 = mybir.dt.float32
BF16 = mybir.dt.float16
RBF16 = mybir.dt.bfloat16
I32 = mybir.dt.int32
AF = mybir.ActivationFunctionType
ALU = mybir.AluOpType
AX = mybir.AxisListType

BF = np.float16
T = 2048
NFT = 32
NTT = 16

ENGINES = ("sync", "gpsimd", "vector", "scalar", "tensor")
DMA_ENG = {"sync", "gpsimd"}


class Ctx:

    def __init__(self, nc, es):
        self.nc = nc
        self.es = es
        self.block = es.enter_context(nc.Block())
        self.sem = {n: es.enter_context(nc.semaphore("sem_" + n)) for n in ENGINES}
        self.cnt = {n: 0 for n in ENGINES}

    @staticmethod
    def _flat(deps):
        out = []
        for d in deps:
            if d is None:
                continue
            if isinstance(d, list):
                out.extend(x for x in d if x is not None)
            else:
                out.append(d)
        best = {}
        for (s, c) in out:
            best[s] = max(best.get(s, 0), c)
        return sorted(best.items())

    def on(self, eng_name, deps, emit, signal=True):
        step = 16 if eng_name in DMA_ENG else 1
        sem = self.sem[eng_name]
        deps = self._flat(deps)

        def body(eng):
            for (s, c) in deps:
                if c > 0:
                    eng.wait_ge(self.sem[s], c)
            ret = emit(eng)
            if signal:
                if isinstance(ret, list):
                    for ins in ret:
                        ins.then_inc(sem, step)
                        self.cnt[eng_name] += step
                else:
                    ret.then_inc(sem, step)
                    self.cnt[eng_name] += step

        getattr(self.block, eng_name)(body)
        return (eng_name, self.cnt[eng_name])

    def wait_only(self, eng_name, deps):
        deps = self._flat(deps)

        def body(eng):
            for (s, c) in deps:
                if c > 0:
                    eng.wait_ge(self.sem[s], c)

        getattr(self.block, eng_name)(body)



def pack_lhsT(WT):
    Fd, Od = WT.shape
    return np.ascontiguousarray(
        WT.reshape(Fd // 128, 128, Od // 128, 128).transpose(2, 1, 0, 3).reshape(Od // 128, 128, Fd)
    )



def build_launch1(debug=False):
    nc = bass.Bass("TRN2", target_bir_lowering=False)
    xT = nc.dram_tensor("xT", [4096, T], F32, kind="ExternalInput").ap()
    wq_p = nc.dram_tensor("wq_p", [32, 128, 4096], F32, kind="ExternalInput").ap()
    wk_p = nc.dram_tensor("wk_p", [8, 128, 4096], F32, kind="ExternalInput").ap()
    wv_p = nc.dram_tensor("wv_p", [8, 128, 4096], F32, kind="ExternalInput").ap()
    had_p = nc.dram_tensor("had_p", [32, 128, 4096], F32, kind="ExternalInput").ap()
    bq_c = nc.dram_tensor("bq_c", [128, 32], F32, kind="ExternalInput").ap()
    bkv_c = nc.dram_tensor("bkv_c", [128, 16], F32, kind="ExternalInput").ap()
    ident_d = nc.dram_tensor("ident", [128, 128], BF16, kind="ExternalInput").ap()

    kvp_o = nc.dram_tensor("kvp", [8, 128, 129], F32, kind="ExternalOutput").ap()
    qphi_o = nc.dram_tensor("qphiT", [4096, T], BF16, kind="ExternalOutput").ap()
    rot_o = nc.dram_tensor("rotT", [4096, T], BF16, kind="ExternalOutput").ap()
    if debug:
        xn_o = nc.dram_tensor("xn0T", [4096, T], BF16, kind="ExternalOutput").ap()

    es = ExitStack()
    S = lambda *a: es.enter_context(nc.sbuf_tensor(*a))
    P = lambda *a: es.enter_context(nc.psum_tensor(*a))
    xb = S("xb", [128, NFT * T], BF16)
    _xf0 = S("xf0", [128, T], F32)
    xf = [_xf0, _xf0]
    wbuf = [S(f"wbuf{i}", [128, 4096], BF16) for i in range(2)]
    abc = S("abc", [128, T], F32)
    bbc = S("bbc", [128, T], F32)
    stt = S("stt", [128, 16], F32)
    st2 = S("st2", [128, 16], F32)
    st3 = S("st3", [128, 16], F32)
    arow = S("arow", [1, 128], F32)
    brow = S("brow", [1, 128], F32)
    onesc_bf = S("onesc_bf", [128, 1], BF16)
    epsb = S("epsb", [128, 1], F32)
    ones1_f = S("ones1_f", [1, 128], F32)
    ident = S("identS", [128, 128], BF16)
    identf = S("identfS", [128, 128], F32)
    biasq = S("biasq", [128, 32], F32)
    biaskv = S("biaskv", [128, 16], F32)
    kcol = S("kcol", [128, T], BF16)
    vcol = S("vcol", [128, T], BF16)
    outt = kcol
    pa = S("pa", [128, T], F32)
    pex = S("pex", [128, T], BF16)
    prl = S("prl", [128, T], BF16)
    sq = pex
    xc = prl
    ktm = S("ktm", [128, 128], BF16)
    vtm = S("vtm", [128, 128], BF16)
    kvh = abc

    pacc = P("pacc", [128, T], F32)
    pkv = P("pkv", [128, 512], F32)
    ptrk = P("ptrk", [128, 128], BF16)
    ptrv = P("ptrv", [128, 128], BF16)
    pst = P("pst", [128, 64], F32)

    ctx = Ctx(nc, es)

    def phi_evac(psum_ap, out_ap, dep_pe, bias_col=None, extra_out_dep=None):
        if bias_col is not None:
            e1 = ctx.on("vector", [dep_pe, cload], lambda e: e.tensor_scalar(
                pa[:], psum_ap, bias_col, 0.0, op0=ALU.add, op1=ALU.min))
            e3 = ctx.on("vector", [dep_pe, cload], lambda e: e.tensor_scalar(
                prl[:], psum_ap, bias_col, 0.0, op0=ALU.add, op1=ALU.max))
        else:
            e1 = ctx.on("vector", [dep_pe], lambda e: e.tensor_scalar(
                pa[:], psum_ap, 0.0, None, op0=ALU.min))
            e3 = ctx.on("vector", [dep_pe], lambda e: e.tensor_scalar(
                prl[:], psum_ap, 0.0, None, op0=ALU.max))
        e2 = ctx.on("scalar", [e1],
                    lambda e: e.activation(pex[:], pa[:], AF.Exp))
        e4 = ctx.on("vector", [e2, e3, extra_out_dep],
                    lambda e: e.tensor_add(out_ap, pex[:], prl[:]))
        return e4, [e1, e3, e4]

    consts = ctx.on("vector", [], lambda e: [
        e.memset(onesc_bf[:], 1.0),
        e.memset(ones1_f[:], 1.0),
        e.memset(epsb[:], 1e-5),
    ][-1])
    cload = ctx.on("gpsimd", [], lambda e: [
        e.dma_start(out=ident[:], in_=ident_d[:]),
        e.dma_start(out=biasq[:], in_=bq_c[:]),
        e.dma_start(out=biaskv[:], in_=bkv_c[:]),
    ])
    cload = ctx.on("vector", [cload], lambda e: e.tensor_copy(identf[:], ident[:]))

    xf_free = None
    stats_ev = None
    for ft in range(NFT):
        lev = ctx.on("sync", [xf_free],
                     lambda e, ft=ft: e.dma_start(out=xf[ft % 2][:], in_=xT[128 * ft:128 * (ft + 1), :]))
        cast_ev = ctx.on("vector", [lev, stats_ev],
                         lambda e, ft=ft: e.tensor_copy(xc[:], xf[ft % 2][:]))
        sqr_ev = ctx.on("scalar", [lev, stats_ev],
                        lambda e, ft=ft: e.activation(sq[:], xf[ft % 2][:], AF.Square))
        xf_free = [cast_ev, sqr_ev]

        def mk_stats(e, ft=ft):
            mm = None
            for tt in range(NTT):
                mm = e.matmul(pst[:, tt:tt + 1], xc[:, 128 * tt:128 * (tt + 1)], onesc_bf[:],
                              start=(ft == 0 and tt == 0), stop=(ft == NFT - 1))
            for tt in range(NTT):
                mm = e.matmul(pst[:, 16 + tt:17 + tt], sq[:, 128 * tt:128 * (tt + 1)], onesc_bf[:],
                              start=False, stop=(ft == NFT - 1))
            return mm
        stats_ev = ctx.on("tensor", [consts, cast_ev, sqr_ev], mk_stats)

    ev = ctx.on("vector", [stats_ev], lambda e: [
        e.tensor_scalar(stt[:], pst[:, 0:16], 1.0 / 4096.0, None, op0=ALU.mult),
        e.tensor_scalar(st2[:], pst[:, 16:32], 1.0 / 4096.0, None, op0=ALU.mult),
    ][-1])
    ev = ctx.on("scalar", [ev], lambda e: e.activation(st3[:], stt[:], AF.Square))
    ev = ctx.on("vector", [ev], lambda e: e.tensor_sub(st2[:], st2[:], st3[:]))
    ev = ctx.on("scalar", [ev], lambda e: [
        e.activation(st3[:], st2[:], AF.Ln, bias=epsb[:]),
        e.activation(st3[:], st3[:], AF.Exp, scale=-0.5),
    ][-1])
    rows_ev = ctx.on("vector", [ev], lambda e:
        e.tensor_mul(stt[:], stt[:], st3[:]))

    prow = pkv
    bc_ev = None
    for src_col, rowbuf in [(st3, arow), (stt, brow)]:
        for tt in range(NTT):
            tev = ctx.on("tensor", [rows_ev, cload, bc_ev],
                         lambda e, tt=tt, s=src_col: e.transpose(prow[0:1, 0:128], s[:, tt:tt + 1], identf[:]))
            cpe = ctx.on("vector", [tev],
                         lambda e, r=rowbuf: e.tensor_copy(r[:], prow[0:1, 0:128]))
            bc_ev = ctx.on("tensor", [cpe],
                           lambda e, tt=tt, r=rowbuf: e.matmul(
                               pacc[:, 128 * tt:128 * (tt + 1)], ones1_f[:], r[:],
                               start=(tt % 4 == 0), stop=True))
        if src_col is st3:
            abc_ev = ctx.on("vector", [bc_ev], lambda e: e.tensor_copy(abc[:], pacc[:]))
            bc_ev = abc_ev
        else:
            bbc_ev = ctx.on("vector", [bc_ev], lambda e: e.tensor_copy(bbc[:], pacc[:]))

    norm_ev = [None] * NFT
    for ft in range(NFT):
        lev = ctx.on("sync", [norm_ev[ft - 1] if ft >= 1 else [stats_ev] + xf_free],
                     lambda e, ft=ft: e.dma_start(out=xf[ft % 2][:], in_=xT[128 * ft:128 * (ft + 1), :]))
        norm_ev[ft] = ctx.on("vector", [lev, bbc_ev], lambda e, ft=ft: [
            e.tensor_mul(pa[:], xf[ft % 2][:], abc[:]),
            e.tensor_sub(xb[:, T * ft:T * (ft + 1)], pa[:], bbc[:]),
        ][-1])
    xn_ev = norm_ev[-1]

    dbg_ev = None
    if debug:
        dbg_ev = ctx.on("sync", [xn_ev], lambda e: [
            e.dma_start(out=xn_o[128 * ft:128 * (ft + 1), :], in_=xb[:, T * ft:T * (ft + 1)])
            for ft in range(NFT)])

    kv_evac_ev = None
    kv_st = None
    pacc_free = [xn_ev]
    tt_mm_ev = None
    vmm_ev = None
    for h in range(8):
        wload_ev = ctx.on("gpsimd", [vmm_ev], lambda e, h=h: [
            e.dma_start(out=wbuf[0][:], in_=wk_p[h]),
            e.dma_start(out=wbuf[1][:], in_=wv_p[h]),
        ])

        def mk_kmm(e, h=h):
            mm = None
            for ft in range(NFT):
                for c in range(4):
                    mm = e.matmul(pacc[:, 512 * c:512 * (c + 1)],
                                  wbuf[0][:, 128 * ft:128 * (ft + 1)],
                                  xb[:, T * ft + 512 * c:T * ft + 512 * (c + 1)],
                                  start=(ft == 0), stop=(ft == NFT - 1))
            return mm
        kmm_ev = ctx.on("tensor", [xn_ev, wload_ev, pacc_free, consts, cload], mk_kmm)
        kphi_ev, pacc_readers = phi_evac(pacc[:], kcol[:], kmm_ev, bias_col=biaskv[:, h:h + 1])

        def mk_vmm(e, h=h):
            mm = None
            for ft in range(NFT):
                for c in range(4):
                    mm = e.matmul(pacc[:, 512 * c:512 * (c + 1)],
                                  wbuf[1][:, 128 * ft:128 * (ft + 1)],
                                  xb[:, T * ft + 512 * c:T * ft + 512 * (c + 1)],
                                  start=(ft == 0), stop=(ft == NFT - 1))
            return mm
        vmm_ev = ctx.on("tensor", pacc_readers, mk_vmm)
        vcol_ev = ctx.on("vector", [vmm_ev], lambda e, h=h: e.tensor_scalar(
            vcol[:], pacc[:], biaskv[:, 8 + h:9 + h], None, op0=ALU.add))
        pacc_free = [vcol_ev]

        cp_ev = None
        for tt in range(NTT):
            trev = ctx.on("tensor", [kphi_ev, vcol_ev, cp_ev], lambda e, tt=tt: [
                e.transpose(ptrk[:], kcol[:, 128 * tt:128 * (tt + 1)], ident[:]),
                e.transpose(ptrv[:], vcol[:, 128 * tt:128 * (tt + 1)], ident[:]),
            ][-1])
            cp_ev = ctx.on("vector", [trev], lambda e, tt=tt: [
                e.tensor_copy(ktm[:], ptrk[:]),
                e.tensor_copy(vtm[:], ptrv[:]),
            ][-1])
            tt_mm_ev = ctx.on("tensor", [cp_ev, kv_evac_ev], lambda e, tt=tt: [
                e.matmul(pkv[:, 0:128], ktm[:], vtm[:], start=(tt == 0), stop=(tt == NTT - 1)),
                e.matmul(pkv[:, 128:129], ktm[:], onesc_bf[:], start=False, stop=(tt == NTT - 1)),
            ][-1])
        kv_evac_ev = ctx.on("vector", [tt_mm_ev, kv_st], lambda e, h=h:
            e.tensor_copy(kvh[:, 0:129], pkv[:, 0:129]))
        kv_st = ctx.on("sync", [kv_evac_ev], lambda e, h=h: e.dma_start(out=kvp_o[h], in_=kvh[:, 0:129]))

    store_ev = None
    outt_free = None
    wb_free = [vmm_ev, vmm_ev]
    for phase, (w_p, out_d, has_phi) in enumerate(
            [(wq_p, qphi_o, True), (had_p, rot_o, False)]):
        for ot in range(32):
            i = ot % 2
            wl = ctx.on("gpsimd", [wb_free[i]],
                        lambda e, w_p=w_p, ot=ot, i=i: e.dma_start(out=wbuf[i][:], in_=w_p[ot]))

            def mk_mm(e, ot=ot, i=i):
                mm = None
                for ft in range(NFT):
                    for c in range(4):
                        mm = e.matmul(pacc[:, 512 * c:512 * (c + 1)],
                                      wbuf[i][:, 128 * ft:128 * (ft + 1)],
                                      xb[:, T * ft + 512 * c:T * ft + 512 * (c + 1)],
                                      start=(ft == 0), stop=(ft == NFT - 1))
                return mm
            mm_ev = ctx.on("tensor", [wl] + pacc_free, mk_mm)
            wb_free[i] = mm_ev

            if has_phi:
                evac_ev, pacc_readers = phi_evac(pacc[:], outt[:], mm_ev,
                                                 bias_col=biasq[:, ot:ot + 1],
                                                 extra_out_dep=outt_free)
                pacc_free = pacc_readers
            else:
                evac_ev = ctx.on("vector", [mm_ev, outt_free],
                                 lambda e: e.tensor_copy(outt[:], pacc[:]))
                pacc_free = [evac_ev]
            store_ev = ctx.on("sync", [evac_ev],
                              lambda e, ot=ot, out_d=out_d:
                              e.dma_start(out=out_d[128 * ot:128 * (ot + 1), :], in_=outt[:]))
            outt_free = store_ev

    ctx.wait_only("gpsimd", [store_ev, kv_st, dbg_ev])

    es.close()
    return nc



_PC = {}


def _prep_l1(inp):
    if "wq_p" in _PC:
        return
    _PC["wq_p"] = pack_lhsT(np.ascontiguousarray(inp["L0_attn_Wq"].T))
    _PC["wk_p"] = pack_lhsT(np.ascontiguousarray(inp["L0_attn_Wk"].T))
    _PC["wv_p"] = pack_lhsT(np.ascontiguousarray(inp["L0_attn_Wv"].T))
    had = inp["L0_ffn_hadamard"]
    hperm = np.concatenate([had[:, 0::2], had[:, 1::2]], axis=1)
    _PC["had_p"] = pack_lhsT(np.ascontiguousarray(hperm))
    _PC["ident"] = np.eye(128, dtype=BF)
    _PC["bq_c"] = np.ascontiguousarray(inp["L0_attn_bq"].reshape(32, 128).T)
    _PC["bkv_c"] = np.ascontiguousarray(
        np.concatenate([inp["L0_attn_bk"], inp["L0_attn_bv"]]).reshape(16, 128).T)


def l1_in_map(inp, core):
    _prep_l1(inp)
    b, j = core // 4, core % 4
    x = inp["L0_attn_hidden_states"][b, T * j:T * (j + 1), :]
    return {
        "xT": np.ascontiguousarray(x.T),
        "wq_p": _PC["wq_p"], "wk_p": _PC["wk_p"], "wv_p": _PC["wv_p"],
        "had_p": _PC["had_p"], "bq_c": _PC["bq_c"], "bkv_c": _PC["bkv_c"],
        "ident": _PC["ident"],
    }



def build_launch2a():
    nc = bass.Bass("TRN2", target_bir_lowering=False)
    dT = nc.dram_tensor("dT", [256, 4096], I32, kind="ExternalInput").ap()
    sc = nc.dram_tensor("sc", [256, 4096], F32, kind="ExternalInput").ap()
    lo_o = nc.dram_tensor("w2lo", [256, 4096], BF16, kind="ExternalOutput").ap()
    hi_o = nc.dram_tensor("w2hi", [256, 4096], BF16, kind="ExternalOutput").ap()

    es = ExitStack()
    S = lambda *a: es.enter_context(nc.sbuf_tensor(*a))
    dint = S("dint", [128, 4096], I32)
    t1i = S("t1i", [128, 4096], I32)
    scb = S("scb", [128, 4096], F32)
    nibf = S("nibf", [128, 4096], F32)
    wlo = S("wlo", [128, 4096], BF16)
    whi = S("whi", [128, 4096], BF16)
    ctx = Ctx(nc, es)

    st_ev = None
    for j in range(2):
        ld = ctx.on("sync", [st_ev], lambda e, j=j: [
            e.dma_start(out=dint[:], in_=dT[128 * j:128 * (j + 1), :]),
            e.dma_start(out=scb[:], in_=sc[128 * j:128 * (j + 1), :]),
        ])
        lo_ev = ctx.on("vector", [ld], lambda e: [
            e.tensor_scalar(t1i[:], dint[:], 15, None, op0=ALU.bitwise_and),
            e.tensor_scalar(nibf[:], t1i[:], 8.0, None, op0=ALU.subtract),
            e.tensor_mul(wlo[:], nibf[:], scb[:]),
        ][-1])
        hi_ev = ctx.on("vector", [lo_ev], lambda e: [
            e.tensor_scalar(t1i[:], dint[:], 4, 15, op0=ALU.logical_shift_right, op1=ALU.bitwise_and),
            e.tensor_scalar(nibf[:], t1i[:], 8.0, None, op0=ALU.subtract),
            e.tensor_mul(whi[:], nibf[:], scb[:]),
        ][-1])
        st_ev = ctx.on("sync", [lo_ev, hi_ev], lambda e, j=j: [
            e.dma_start(out=lo_o[128 * j:128 * (j + 1), :], in_=wlo[:]),
            e.dma_start(out=hi_o[128 * j:128 * (j + 1), :], in_=whi[:]),
        ])
    ctx.wait_only("gpsimd", [st_ev])
    es.close()
    return nc



def build_launch2b(debug=False):
    nc = bass.Bass("TRN2", target_bir_lowering=False)
    rot_d = nc.dram_tensor("rotT", [4096, T], BF16, kind="ExternalInput").ap()
    qphi_d = nc.dram_tensor("qphiT", [4096, T], BF16, kind="ExternalInput").ap()
    kvp4 = nc.dram_tensor("kvp4", [4, 8, 128, 129], F32, kind="ExternalInput").ap()
    w2_p = nc.dram_tensor("w2_p", [32, 128, 4096], BF16, kind="ExternalInput").ap()
    wo_p = nc.dram_tensor("wo_p", [32, 128, 4096], F32, kind="ExternalInput").ap()
    fb_c = nc.dram_tensor("fb_c", [128, 32], F32, kind="ExternalInput").ap()
    bo_c = nc.dram_tensor("bo_c", [128, 32], F32, kind="ExternalInput").ap()
    xT = nc.dram_tensor("xT", [4096, T], F32, kind="ExternalInput").ap()
    ident_d = nc.dram_tensor("ident", [128, 128], BF16, kind="ExternalInput").ap()

    x1_o = nc.dram_tensor("x1T", [4096, T], F32, kind="ExternalOutput").ap()
    xn1_o = nc.dram_tensor("xn1T", [4096, T], BF16, kind="ExternalOutput").ap()
    lgv_o = nc.dram_tensor("lgv", [128, 16], F32, kind="ExternalOutput").ap()
    lgp_o = nc.dram_tensor("lgp", [1, 1], F32, kind="ExternalOutput").ap()
    if debug:
        g0_o = nc.dram_tensor("g0T", [4096, T], F32, kind="ExternalOutput").ap()
        pst_dbg = nc.dram_tensor("pst_dbg", [128, 64], F32, kind="ExternalOutput").ap()
        fin_dbg = nc.dram_tensor("fin_dbg", [128, 48], F32, kind="ExternalOutput").ap()
        abc_dbg = nc.dram_tensor("abc_dbg", [128, 64], F32, kind="ExternalOutput").ap()
        s0_o = nc.dram_tensor("s0T", [4096, T], F32, kind="ExternalOutput").ap()
        attn_o = nc.dram_tensor("attnT", [4096, T], BF16, kind="ExternalOutput").ap()
        kvs_o = nc.dram_tensor("kvsum_o", [128, 1032], F32, kind="ExternalOutput").ap()
        rn_o = nc.dram_tensor("rnorm_o", [128, 8], F32, kind="ExternalOutput").ap()

    g0_d = nc.dram_tensor("g0_d", [4096, T], F32, kind="Internal").ap()

    es = ExitStack()
    S = lambda *a: es.enter_context(nc.sbuf_tensor(*a))
    P = lambda *a: es.enter_context(nc.psum_tensor(*a))
    xb2 = S("xb2", [128, NFT * T], BF16)
    wbuf = [S(f"wbuf{i}", [128, 4096], BF16) for i in range(2)]
    xload = S("xload", [128, T], F32)
    gld = S("gld", [128, T], F32)
    sg = S("sg", [128, T], F32)
    x1c = S("x1c", [128, T], BF16)
    sqb = S("sqb", [128, T], RBF16)
    kvsum = S("kvsum", [128, 1032], F32)
    kvt2 = S("kvt2", [128, 1032], F32)
    kvbf = S("kvbf", [128, 1024], BF16)
    rnorm = S("rnorm", [128, 8], F32)
    fbias = S("fbias", [128, 32], F32)
    bobias = S("bobias", [128, 32], F32)
    stt = S("stt", [128, 16], F32)
    st2 = S("st2", [128, 16], F32)
    st3 = S("st3", [128, 16], F32)
    arow = S("arow", [1, 128], F32)
    onesc_bf = S("onesc_bf", [128, 1], BF16)
    onesc_bb = S("onesc_bb", [128, 1], RBF16)
    onesc_f = S("onesc_f", [128, 1], F32)
    ones1_f = S("ones1_f", [1, 128], F32)
    epsb = S("epsb", [128, 1], F32)
    oneb = S("oneb", [128, 1], F32)
    ident = S("identS", [128, 128], BF16)
    identf = S("identfS", [128, 128], F32)
    abc = S("abc", [128, T], F32)
    bbc = S("bbc", [128, T], F32)

    pacc = P("pacc", [128, T], F32)
    pst = P("pst", [128, 64], F32)
    ppr = P("ppr", [1, 128], F32)
    psc = P("psc", [1, 64], F32)

    ctx = Ctx(nc, es)

    consts = ctx.on("vector", [], lambda e: [
        e.memset(onesc_bf[:], 1.0),
        e.memset(onesc_bb[:], 1.0),
        e.memset(onesc_f[:], 1.0),
        e.memset(ones1_f[:], 1.0),
        e.memset(epsb[:], 1e-5),
        e.memset(oneb[:], 1.0),
    ][-1])
    cload = ctx.on("gpsimd", [], lambda e: [
        e.dma_start(out=ident[:], in_=ident_d[:]),
        e.dma_start(out=fbias[:], in_=fb_c[:]),
        e.dma_start(out=bobias[:], in_=bo_c[:]),
    ])
    cload = ctx.on("vector", [cload], lambda e: e.tensor_copy(identf[:], ident[:]))

    rl_ev = ctx.on("sync", [], lambda e: [
        e.dma_start(out=xb2[:, T * ft:T * (ft + 1)], in_=rot_d[128 * ft:128 * (ft + 1), :])
        for ft in range(NFT)])

    pacc_free = None
    g_st = None
    wl_free = [None, None]
    for ot in range(32):
        i = ot % 2
        wl = ctx.on("sync", [wl_free[i]], lambda e, ot=ot, i=i: e.dma_start(out=wbuf[i][:], in_=w2_p[ot]))

        def mk_mm(e, ot=ot, i=i):
            mm = None
            for ft in range(NFT):
                for c in range(4):
                    mm = e.matmul(pacc[:, 512 * c:512 * (c + 1)],
                                  wbuf[i][:, 128 * ft:128 * (ft + 1)],
                                  xb2[:, T * ft + 512 * c:T * ft + 512 * (c + 1)],
                                  start=(ft == 0), stop=(ft == NFT - 1))
            return mm
        mm_ev = ctx.on("tensor", [wl, rl_ev, pacc_free], mk_mm)
        wl_free[i] = mm_ev
        e1 = ctx.on("vector", [mm_ev, cload, g_st], lambda e, ot=ot: e.tensor_scalar(
            sg[:], pacc[:], fbias[:, ot:ot + 1], None, op0=ALU.add))
        pacc_free = e1
        e2 = ctx.on("scalar", [e1, g_st], lambda e: e.activation(gld[:], sg[:], AF.Silu))
        g_st = ctx.on("sync", [e2], lambda e, ot=ot:
                      e.dma_start(out=g0_d[128 * ot:128 * (ot + 1), :], in_=gld[:]))
        if debug:
            g_st = ctx.on("sync", [g_st], lambda e, ot=ot:
                          e.dma_start(out=g0_o[128 * ot:128 * (ot + 1), :], in_=gld[:]))

    kr = ctx.on("sync", [], lambda e: e.dma_start(out=kvsum[:].rearrange("p (h c) -> p h c", h=8), in_=kvp4[0].rearrange("h p c -> p h c")))
    kr2 = ctx.on("sync", [kr], lambda e: e.dma_start(out=kvt2[:].rearrange("p (h c) -> p h c", h=8), in_=kvp4[1].rearrange("h p c -> p h c")))
    s_ev = ctx.on("vector", [kr2], lambda e: e.tensor_add(kvsum[:], kvsum[:], kvt2[:]))
    kr3 = ctx.on("sync", [s_ev], lambda e: e.dma_start(out=kvt2[:].rearrange("p (h c) -> p h c", h=8), in_=kvp4[2].rearrange("h p c -> p h c")))
    s_ev = ctx.on("vector", [kr3], lambda e: e.tensor_add(kvsum[:], kvsum[:], kvt2[:]))
    kr4 = ctx.on("sync", [s_ev], lambda e: e.dma_start(out=kvt2[:].rearrange("p (h c) -> p h c", h=8), in_=kvp4[3].rearrange("h p c -> p h c")))
    s_ev = ctx.on("vector", [kr4], lambda e: e.tensor_add(kvsum[:], kvsum[:], kvt2[:]))
    kbf_ev = ctx.on("vector", [s_ev], lambda e: [
        e.tensor_copy(kvbf[:, 128 * h:128 * (h + 1)], kvsum[:, 129 * h:129 * h + 128])
        for h in range(8)][-1])
    rn_ev = ctx.on("vector", [s_ev], lambda e: [
        e.tensor_scalar(rnorm[:, h:h + 1], kvsum[:, 129 * h + 128:129 * h + 129],
                        1e-10, None, op0=ALU.add)
        for h in range(8)][-1])
    rn_ev = ctx.on("vector", [rn_ev], lambda e: e.reciprocal(rnorm[:], rnorm[:]))

    if debug:
        dkv = ctx.on("sync", [kbf_ev, rn_ev], lambda e: [
            e.dma_start(out=kvs_o[:], in_=kvsum[:]),
            e.dma_start(out=rn_o[:], in_=rnorm[:]),
        ])

    attn_done = None
    q_free = [wl_free[0], wl_free[1]]
    for qh in range(32):
        i = qh % 2
        ql = ctx.on("sync", [q_free[i]], lambda e, qh=qh, i=i: e.dma_start(
            out=wbuf[i][:, 0:T], in_=qphi_d[128 * qh:128 * (qh + 1), :]))

        def mk_attn(e, qh=qh, i=i):
            kvh = qh // 4
            mm = None
            for c in range(4):
                mm = e.matmul(pacc[:, 512 * c:512 * (c + 1)],
                              kvbf[:, 128 * kvh:128 * (kvh + 1)],
                              wbuf[i][:, 512 * c:512 * (c + 1)],
                              start=True, stop=True)
            return mm
        mm_ev = ctx.on("tensor", [ql, kbf_ev, pacc_free], mk_attn)
        q_free[i] = mm_ev
        ev = ctx.on("vector", [mm_ev, rn_ev], lambda e, qh=qh: e.tensor_scalar(
            xb2[:, T * qh:T * (qh + 1)], pacc[:], rnorm[:, qh // 4:qh // 4 + 1], None, op0=ALU.mult))
        pacc_free = ev
        attn_done = ev

    if debug:
        attn_done = ctx.on("sync", [attn_done], lambda e: [
            e.dma_start(out=attn_o[128 * f:128 * (f + 1), :], in_=xb2[:, T * f:T * (f + 1)])
            for f in range(NFT)])

    wl_free = [attn_done, attn_done]
    xl_ev = None
    stats_ev = None
    x1st = None
    for ot in range(32):
        i = ot % 2
        wl = ctx.on("gpsimd", [wl_free[i]], lambda e, ot=ot, i=i: e.dma_start(out=wbuf[i][:], in_=wo_p[ot]))
        xl = ctx.on("sync", [xl_ev], lambda e, ot=ot: [
            e.dma_start(out=xload[:], in_=xT[128 * ot:128 * (ot + 1), :]),
            e.dma_start(out=gld[:], in_=g0_d[128 * ot:128 * (ot + 1), :]),
        ])

        def mk_mm(e, ot=ot, i=i):
            mm = None
            for ft in range(NFT):
                for c in range(4):
                    mm = e.matmul(pacc[:, 512 * c:512 * (c + 1)],
                                  wbuf[i][:, 128 * ft:128 * (ft + 1)],
                                  xb2[:, T * ft + 512 * c:T * ft + 512 * (c + 1)],
                                  start=(ft == 0), stop=(ft == NFT - 1))
            return mm
        mm_ev = ctx.on("tensor", [wl, attn_done, pacc_free], mk_mm)
        wl_free[i] = mm_ev
        e1 = ctx.on("vector", [mm_ev, cload, x1st], lambda e, ot=ot: e.tensor_scalar(
            sg[:], pacc[:], bobias[:, ot:ot + 1], None, op0=ALU.add))
        pacc_free = e1
        if debug:
            sdbg = ctx.on("sync", [e1], lambda e, ot=ot:
                          e.dma_start(out=s0_o[128 * ot:128 * (ot + 1), :], in_=sg[:]))
            e1 = [e1, sdbg]
        e2 = ctx.on("vector", [e1, xl], lambda e: [
            e.tensor_mul(gld[:], sg[:], gld[:]),
            e.tensor_add(xload[:], gld[:], xload[:]),
        ][-1])
        xl_ev = e2
        x1st = ctx.on("sync", [e2], lambda e, ot=ot:
                      e.dma_start(out=x1_o[128 * ot:128 * (ot + 1), :], in_=xload[:]))
        cst = ctx.on("vector", [e2], lambda e: e.tensor_copy(x1c[:], xload[:]))
        sqe = ctx.on("scalar", [e2, stats_ev], lambda e: e.activation(sqb[:], xload[:], AF.Square))
        xl_ev = [x1st, cst, sqe]

        def mk_stats(e, ot=ot):
            mm = None
            for tt in range(NTT):
                mm = e.matmul(pst[:, tt:tt + 1], x1c[:, 128 * tt:128 * (tt + 1)], onesc_bf[:],
                              start=(ot == 0 and tt == 0), stop=(ot == NFT - 1))
            for tt in range(NTT):
                mm = e.matmul(pst[:, 16 + tt:17 + tt], sqb[:, 128 * tt:128 * (tt + 1)], onesc_bb[:],
                              start=False, stop=(ot == NFT - 1))
            return mm
        stats_ev = ctx.on("tensor", [consts, cst, sqe], mk_stats)

    if debug:
        dps = ctx.on("vector", [stats_ev], lambda e: e.tensor_copy(sg[:, 0:64], pst[:]))
        ctx.on("sync", [dps], lambda e: e.dma_start(out=pst_dbg[:], in_=sg[:, 0:64]))
    ev = ctx.on("vector", [stats_ev], lambda e: [
        e.tensor_scalar(stt[:], pst[:, 0:16], 1.0 / 4096.0, None, op0=ALU.mult),
        e.tensor_scalar(st2[:], pst[:, 16:32], 1.0 / 4096.0, None, op0=ALU.mult),
    ][-1])
    ev = ctx.on("scalar", [ev], lambda e: e.activation(st3[:], stt[:], AF.Square))
    ev = ctx.on("vector", [ev], lambda e: e.tensor_sub(st2[:], st2[:], st3[:]))
    ev = ctx.on("scalar", [ev, consts], lambda e: [
        e.activation(st3[:], st2[:], AF.Ln, bias=epsb[:]),
        e.activation(st3[:], st3[:], AF.Exp, scale=-0.5),
    ][-1])
    rows_ev = ctx.on("vector", [ev], lambda e: e.tensor_mul(stt[:], stt[:], st3[:]))
    if debug:
        dfn = ctx.on("vector", [rows_ev], lambda e: [
            e.tensor_copy(sg[:, 0:16], stt[:]),
            e.tensor_copy(sg[:, 16:32], st2[:]),
            e.tensor_copy(sg[:, 32:48], st3[:]),
        ][-1])
        ctx.on("sync", [dfn], lambda e: e.dma_start(out=fin_dbg[:], in_=sg[:, 0:48]))

    bc_ev = None
    abc_ev = bbc_ev = None
    for src_col, dst in [(st3, abc), (stt, bbc)]:
        for tt in range(NTT):
            tev = ctx.on("tensor", [rows_ev, cload, bc_ev],
                         lambda e, tt=tt, s=src_col: e.transpose(ppr[0:1, 0:128], s[:, tt:tt + 1], identf[:]))
            cpe = ctx.on("vector", [tev], lambda e: e.tensor_copy(arow[:], ppr[0:1, 0:128]))
            bc_ev = ctx.on("tensor", [cpe], lambda e, tt=tt: e.matmul(
                pacc[:, 128 * tt:128 * (tt + 1)], ones1_f[:], arow[:],
                start=(tt % 4 == 0), stop=True))
        cpev = ctx.on("vector", [bc_ev], lambda e, dst=dst: e.tensor_copy(dst[:], pacc[:]))
        if src_col is st3:
            abc_ev = cpev
        else:
            bbc_ev = cpev
        bc_ev = cpev

    if debug:
        dab = ctx.on("vector", [abc_ev, bbc_ev], lambda e: [
            e.tensor_copy(sg[:, 0:32], abc[:, 0:32]),
            e.tensor_copy(sg[:, 32:64], bbc[:, 0:32]),
        ][-1])
        ctx.on("sync", [dab], lambda e: e.dma_start(out=abc_dbg[:], in_=sg[:, 0:64]))

    vstats_ev = None
    nx_ev = None
    xn_st = None
    for ft in range(NFT):
        lx = ctx.on("sync", [nx_ev], lambda e, ft=ft: e.dma_start(out=xload[:], in_=x1_o[128 * ft:128 * (ft + 1), :]))
        ne = ctx.on("vector", [lx, abc_ev, bbc_ev, xn_st], lambda e, ft=ft: [
            e.tensor_mul(sg[:], xload[:], abc[:]),
            e.tensor_sub(x1c[:], sg[:], bbc[:]),
        ][-1])
        nx_ev = ne
        xn_st = ctx.on("sync", [ne], lambda e, ft=ft:
                       e.dma_start(out=xn1_o[128 * ft:128 * (ft + 1), :], in_=x1c[:]))
        sqe = ctx.on("scalar", [ne, vstats_ev], lambda e: e.activation(sqb[:], x1c[:], AF.Square))

        def mk_vstats(e, ft=ft):
            mm = None
            for tt in range(NTT):
                mm = e.matmul(pst[:, 32 + tt:33 + tt], x1c[:, 128 * tt:128 * (tt + 1)], onesc_bf[:],
                              start=(ft == 0 and tt == 0), stop=(ft == NFT - 1))
            for tt in range(NTT):
                mm = e.matmul(pst[:, 48 + tt:49 + tt], sqb[:, 128 * tt:128 * (tt + 1)], onesc_bb[:],
                              start=False, stop=(ft == NFT - 1))
            return mm
        vstats_ev = ctx.on("tensor", [sqe, ne], mk_vstats)
        nx_ev = [nx_ev, sqe, vstats_ev]

    ev = ctx.on("vector", [vstats_ev], lambda e: [
        e.tensor_scalar(stt[:], pst[:, 32:48], 1.0 / 4096.0, None, op0=ALU.mult),
        e.tensor_scalar(st2[:], pst[:, 48:64], 1.0 / 4096.0, None, op0=ALU.mult),
    ][-1])
    ev = ctx.on("scalar", [ev], lambda e: e.activation(st3[:], stt[:], AF.Square))
    ev = ctx.on("vector", [ev], lambda e: e.tensor_sub(st2[:], st2[:], st3[:]))
    lg_ev = ctx.on("scalar", [ev, consts], lambda e: e.activation(st3[:], st2[:], AF.Ln, bias=oneb[:]))
    lg_st = ctx.on("sync", [lg_ev], lambda e: e.dma_start(out=lgv_o[:], in_=st3[:]))
    ev = ctx.on("vector", [lg_ev], lambda e: e.reduce_sum(stt[:, 0:1], st3[:], axis=AX.X))
    ev = ctx.on("tensor", [ev], lambda e: e.matmul(psc[0:1, 0:1], stt[:, 0:1], onesc_f[:],
                                                   start=True, stop=True))
    ev = ctx.on("vector", [ev], lambda e: e.tensor_copy(st2[0:1, 0:1], psc[0:1, 0:1]))
    lp_st = ctx.on("sync", [ev], lambda e: e.dma_start(out=lgp_o[:], in_=st2[0:1, 0:1]))

    ctx.wait_only("gpsimd", [xn_st, lg_st, lp_st, x1st])
    es.close()
    return nc


def _prep_l2(inp):
    if "w2ref" in _PC:
        return
    _PC["dTfull"] = np.ascontiguousarray(inp["L0_ffn_weight_data"].T)
    _PC["scbig"] = np.ascontiguousarray(np.repeat(inp["L0_ffn_weight_scales"].T, 64, axis=0))
    _PC["wo_p"] = pack_lhsT(np.ascontiguousarray(inp["L0_attn_Wo"].T))
    _PC["fb_c"] = np.ascontiguousarray(inp["L0_ffn_bias"].reshape(32, 128).T)
    _PC["bo_c"] = np.ascontiguousarray(inp["L0_attn_bo"].reshape(32, 128).T)
    _PC["w2ref"] = True


def l2a_in_map(inp, core):
    _prep_l2(inp)
    return {"dT": _PC["dTfull"][256 * core:256 * (core + 1)],
            "sc": _PC["scbig"][256 * core:256 * (core + 1)]}


def assemble_w2(res2a):
    w2 = np.empty((4096, 4096), dtype=BF)
    for c in range(8):
        w2[256 * c:256 * (c + 1)] = res2a.results[c]["w2lo"]
        w2[2048 + 256 * c:2048 + 256 * (c + 1)] = res2a.results[c]["w2hi"]
    return pack_lhsT(w2)


def l2b_in_map(inp, core, res1, w2_p):
    _prep_l2(inp)
    b, j = core // 4, core % 4
    x = inp["L0_attn_hidden_states"][b, T * j:T * (j + 1), :]
    kvp4 = np.stack([res1.results[4 * b + g]["kvp"] for g in range(4)])
    return {
        "rotT": res1.results[core]["rotT"],
        "qphiT": res1.results[core]["qphiT"],
        "kvp4": np.ascontiguousarray(kvp4),
        "w2_p": w2_p,
        "wo_p": _PC["wo_p"], "fb_c": _PC["fb_c"], "bo_c": _PC["bo_c"],
        "xT": np.ascontiguousarray(x.T),
        "ident": _PC["ident"],
    }



def build_launch3(debug=False):
    nc = bass.Bass("TRN2", target_bir_lowering=False)
    xn1_d = nc.dram_tensor("xn1T", [4096, T], BF16, kind="ExternalInput").ap()
    wqkv_p = nc.dram_tensor("wqkv_p", [48, 128, 4096], F32, kind="ExternalInput").ap()
    bqkv_c = nc.dram_tensor("bqkv_c", [128, 48], F32, kind="ExternalInput").ap()
    wu1_p = nc.dram_tensor("wu1_p", [8, 128, 4096], F32, kind="ExternalInput").ap()
    bu1_c = nc.dram_tensor("bu1_c", [128, 8], F32, kind="ExternalInput").ap()
    wqr_pk = nc.dram_tensor("wqr_pk", [128, 2048], F32, kind="ExternalInput").ap()
    wr_s = nc.dram_tensor("wr_s", [64, 48], F32, kind="ExternalInput").ap()
    akT_d = nc.dram_tensor("akT", [64, 1024], F32, kind="ExternalInput").ap()
    lr_d = nc.dram_tensor("logrel", [1, 1024], F32, kind="ExternalInput").ap()
    av_p = nc.dram_tensor("av_p", [32, 128, 1024], F32, kind="ExternalInput").ap()
    wvd_p = nc.dram_tensor("wvd_p", [2, 128, 4096], F32, kind="ExternalInput").ap()
    wvu_p = nc.dram_tensor("wvu_p", [32, 128, 256], F32, kind="ExternalInput").ap()
    wu2_s = nc.dram_tensor("wu2_s", [128, 8], F32, kind="ExternalInput").ap()
    bu2_c = nc.dram_tensor("bu2_c", [128, 1], F32, kind="ExternalInput").ap()
    ident_d = nc.dram_tensor("ident", [128, 128], BF16, kind="ExternalInput").ap()

    q_o = nc.dram_tensor("qT", [2048, T], BF16, kind="ExternalOutput").ap()
    k_o = nc.dram_tensor("kT", [2048, T], BF16, kind="ExternalOutput").ap()
    v_o = nc.dram_tensor("vt", [16, 16, 128, 128], BF16, kind="ExternalOutput").ap()
    aux3_o = nc.dram_tensor("aux3T", [4096, T], BF16, kind="ExternalOutput").ap()
    lrn_o = nc.dram_tensor("learned", [128, 16], F32, kind="ExternalOutput").ap()
    if debug:
        wrt_o = nc.dram_tensor("wrt", [16, 128, 1024], BF16, kind="ExternalOutput").ap()
        auxo_o = nc.dram_tensor("auxoT", [4096, T], BF16, kind="ExternalOutput").ap()
        rs_dbg = nc.dram_tensor("rs_dbg", [128, 1024], F32, kind="ExternalOutput").ap()
        t8_dbg = nc.dram_tensor("t8_dbg", [128, 8], F32, kind="ExternalOutput").ap()
        mk_dbg = nc.dram_tensor("mk_dbg", [128, 1024], F32, kind="ExternalOutput").ap()
        ms_dbg = nc.dram_tensor("ms_dbg", [128, 1024], F32, kind="ExternalOutput").ap()
        ex_dbg = nc.dram_tensor("ex_dbg", [128, 1024], F32, kind="ExternalOutput").ap()
        ls_dbg = nc.dram_tensor("ls_dbg", [128, 2], F32, kind="ExternalOutput").ap()
        u2_dbg = nc.dram_tensor("u2_dbg", [128, 16], F32, kind="ExternalOutput").ap()
        as_dbg = nc.dram_tensor("as_dbg", [128, 1024], F32, kind="ExternalOutput").ap()
        m1_dbg = nc.dram_tensor("m1_dbg", [128, T], BF16, kind="ExternalOutput").ap()
        q_dbg = nc.dram_tensor("q_dbg", [64, T], BF16, kind="ExternalOutput").ap()

    aux_d = nc.dram_tensor("aux_d", [4096, T], BF16, kind="Internal").ap()
    m1_d = nc.dram_tensor("m1_d", [1024, T], BF16, kind="Internal").ap()

    es = ExitStack()
    S = lambda *a: es.enter_context(nc.sbuf_tensor(*a))
    P = lambda *a: es.enter_context(nc.psum_tensor(*a))
    xb = S("xb", [128, NFT * T], BF16)
    wbuf = [S(f"wbuf{i}", [128, 4096], BF16) for i in range(2)]
    wT = S("wT", [128, 8192], BF16)
    S1 = S("S1", [128, T], F32)
    S2 = S("S2", [128, T], F32)
    S3 = S("S3", [128, T], F32)
    qkvrow = S("qkvrow", [128, T], BF16)
    queryT = S("queryT", [64, T], BF16)
    q8T = S("q8T", [64, T], BF16)
    rqT = S("rqT", [48, T], BF16)
    rkT = S("rkT", [48, 1024], BF16)
    akT = S("akTs", [64, 1024], BF16)
    lrow = S("lrow", [1, 1024], BF16)
    wrow = None
    wrs = S("wrs", [64, 48], BF16)
    wu2c = S("wu2c", [128, 8], BF16)
    bqkv = S("bqkv", [128, 48], F32)
    bu1 = S("bu1", [128, 8], F32)
    bu2 = S("bu2", [128, 1], F32)
    st8 = S("st8", [128, 8], F32)
    rmx = S("rmx", [128, 1], F32)
    lsm = S("lsm", [128, 1], F32)
    rl = S("rl", [128, 1], F32)
    lrn = S("lrn", [128, 16], F32)
    vtm = S("vtm", [128, 128], BF16)
    ident = S("identS", [128, 128], BF16)
    ones1h = S("ones1h", [1, 128], BF16)
    wvul = S("wvul", [128, 256], BF16)

    wrow = qkvrow[:, 1024:2048]
    wqr = wT[:, 0:2048]

    pacc = P("pacc", [128, T], F32)
    prt = P("prt", [128, 1024], F32)
    pst = P("pst", [128, 16], F32)
    ptrk = P("ptrk", [128, 128], BF16)

    ctx = Ctx(nc, es)

    consts = ctx.on("vector", [], lambda e: e.memset(ones1h[:], 1.0))
    cload = ctx.on("gpsimd", [], lambda e: [
        e.dma_start(out=ident[:], in_=ident_d[:]),
        e.dma_start(out=bqkv[:], in_=bqkv_c[:]),
        e.dma_start(out=bu1[:], in_=bu1_c[:]),
        e.dma_start(out=bu2[:], in_=bu2_c[:]),
        e.dma_start(out=wrs[:], in_=wr_s[:]),
        e.dma_start(out=wqr, in_=wqr_pk[:]),
        e.dma_start(out=akT[:], in_=akT_d[:]),
        e.dma_start(out=lrow[:], in_=lr_d[:]),
        e.dma_start(out=wu2c[:], in_=wu2_s[:]),
    ])
    xl_ev = ctx.on("sync", [], lambda e: [
        e.dma_start(out=xb[:, T * ft:T * (ft + 1)], in_=xn1_d[128 * ft:128 * (ft + 1), :])
        for ft in range(NFT)])

    pacc_free = None
    wl_free = [None, None]
    row_free = None
    tr_free = None
    for ot in range(48):
        i = ot % 2
        wl = ctx.on("gpsimd", [wl_free[i]], lambda e, ot=ot, i=i: e.dma_start(out=wbuf[i][:], in_=wqkv_p[ot]))

        def mk_mm(e, ot=ot, i=i):
            mm = None
            for ft in range(NFT):
                for c in range(4):
                    mm = e.matmul(pacc[:, 512 * c:512 * (c + 1)],
                                  wbuf[i][:, 128 * ft:128 * (ft + 1)],
                                  xb[:, T * ft + 512 * c:T * ft + 512 * (c + 1)],
                                  start=(ft == 0), stop=(ft == NFT - 1))
            return mm
        mm_ev = ctx.on("tensor", [wl, xl_ev, pacc_free], mk_mm)
        wl_free[i] = mm_ev
        ev = ctx.on("vector", [mm_ev, cload, row_free], lambda e, ot=ot: e.tensor_scalar(
            qkvrow[:], pacc[:], bqkv[:, ot:ot + 1], None, op0=ALU.add))
        pacc_free = ev
        if ot < 32:
            out_d = q_o if ot < 16 else k_o
            row = ot % 16
            row_free = ctx.on("sync", [ev], lambda e, out_d=out_d, row=row:
                              e.dma_start(out=out_d[128 * row:128 * (row + 1), :], in_=qkvrow[:]))
        else:
            ct = ot - 32
            vst = None
            for tt in range(NTT):
                trev = ctx.on("tensor", [ev, tr_free], lambda e, tt=tt:
                              e.transpose(ptrk[:], qkvrow[:, 128 * tt:128 * (tt + 1)], ident[:]))
                cpe = ctx.on("vector", [trev, vst], lambda e: e.tensor_copy(vtm[:], ptrk[:]))
                tr_free = cpe
                vst = ctx.on("sync", [cpe], lambda e, ct=ct, tt=tt:
                             e.dma_start(out=v_o[ct, tt], in_=vtm[:]))
            row_free = [vst, tr_free]

    def mk_q(e):
        mm = None
        for ft in range(NFT):
            for c in range(4):
                mm = e.matmul(pacc[0:64, 512 * c:512 * (c + 1)],
                              wqr[:, 64 * ft:64 * (ft + 1)],
                              xb[:, T * ft + 512 * c:T * ft + 512 * (c + 1)],
                              start=(ft == 0), stop=(ft == NFT - 1))
        return mm
    qm_ev = ctx.on("tensor", [cload, xl_ev, pacc_free], mk_q)
    qv_ev = ctx.on("vector", [qm_ev, row_free], lambda e: e.tensor_copy(queryT[:], pacc[0:64, :]))
    q8_ev = ctx.on("vector", [qv_ev], lambda e: e.tensor_scalar(q8T[:], pacc[0:64, :], 0.125, None, op0=ALU.mult))
    pacc_free = q8_ev
    if debug:
        ctx.on("sync", [qv_ev], lambda e: e.dma_start(out=q_dbg[:], in_=queryT[:]))

    def mk_rq(e):
        mm = None
        for c in range(4):
            mm = e.matmul(pacc[0:48, 512 * c:512 * (c + 1)], wrs[:], queryT[:, 512 * c:512 * (c + 1)],
                          start=True, stop=True)
        return mm
    ev = ctx.on("tensor", [qv_ev, pacc_free], mk_rq)
    rq_ev = ctx.on("vector", [ev], lambda e: e.tensor_copy(rqT[:], pacc[0:48, :]))
    pacc_free = rq_ev

    def mk_rk(e):
        mm = None
        for c in range(2):
            mm = e.matmul(pacc[0:48, 512 * c:512 * (c + 1)], wrs[:], akT[:, 512 * c:512 * (c + 1)],
                          start=True, stop=True)
        return mm
    ev = ctx.on("tensor", [rq_ev, cload], mk_rk)
    rk_ev = ctx.on("vector", [ev], lambda e: e.tensor_scalar(
        rkT[:], pacc[0:48, 0:1024], 0.14433756729740643, None, op0=ALU.mult))
    pacc_free = rk_ev

    aux_st = None
    avl_free = [wl_free[0], wl_free[1]]
    for half in range(2):
        wt_ev = None
        for ti in range(8):
            tt = 8 * half + ti

            def mk_rs(e, tt=tt):
                mm = None
                for c in range(2):
                    mm = e.matmul(prt[:, 512 * c:512 * (c + 1)],
                                  rqT[:, 128 * tt:128 * (tt + 1)],
                                  rkT[:, 512 * c:512 * (c + 1)], start=True, stop=True)
                for c in range(2):
                    mm = e.matmul(prt[:, 512 * c:512 * (c + 1)],
                                  ones1h[:], lrow[0:1, 512 * c:512 * (c + 1)],
                                  start=False, stop=True)
                return mm
            rs_ev = ctx.on("tensor", [rk_ev, q8_ev, wt_ev, consts], mk_rs)
            if debug and tt == 0:
                dttt = ctx.on("vector", [rs_ev], lambda e: e.tensor_copy(S2[:, 0:1024], prt[:]))
                ctx.on("sync", [dttt], lambda e: e.dma_start(out=rs_dbg[:], in_=S2[:, 0:1024]))
            mx_ev = ctx.on("vector", [rs_ev], lambda e: e.max(out=st8[:], in_=prt[:]))
            mk_ev = ctx.on("vector", [mx_ev], lambda e: e.tensor_scalar(
                S1[:, 0:1024], prt[:], st8[:, 7:8], None, op0=ALU.is_lt))
            if debug and tt == 0:
                ctx.on("sync", [mk_ev], lambda e: [
                    e.dma_start(out=t8_dbg[:], in_=st8[:]),
                    e.dma_start(out=mk_dbg[:], in_=S1[:, 0:1024]),
                ])

            def mk_as(e, tt=tt):
                mm = None
                for c in range(2):
                    mm = e.matmul(prt[:, 512 * c:512 * (c + 1)],
                                  q8T[:, 128 * tt:128 * (tt + 1)],
                                  akT[:, 512 * c:512 * (c + 1)], start=True, stop=True)
                for c in range(2):
                    mm = e.matmul(prt[:, 512 * c:512 * (c + 1)],
                                  ones1h[:], lrow[0:1, 512 * c:512 * (c + 1)],
                                  start=False, stop=True)
                return mm
            as_ev = ctx.on("tensor", [mk_ev, q8_ev, consts], mk_as)
            if debug and tt == 0:
                dtt2 = ctx.on("vector", [as_ev], lambda e: e.tensor_copy(S3[:, 0:1024], prt[:]))
                ctx.on("sync", [dtt2], lambda e: e.dma_start(out=as_dbg[:], in_=S3[:, 0:1024]))
            msk_ev = ctx.on("vector", [as_ev], lambda e: e.scalar_tensor_tensor(
                S2[:, 0:1024], S1[:, 0:1024], -60000.0, prt[:], op0=ALU.mult, op1=ALU.add))
            mxr_ev = ctx.on("vector", [msk_ev], lambda e: e.reduce_max(
                rmx[:], S2[:, 0:1024], axis=AX.X, negate=True))
            if debug and tt == 0:
                ctx.on("sync", [mxr_ev], lambda e: e.dma_start(out=ms_dbg[:], in_=S2[:, 0:1024]))
            exp_ev = ctx.on("scalar", [mxr_ev], lambda e: e.activation(
                S3[:, 0:1024], S2[:, 0:1024], AF.Exp, bias=rmx[:], accum_out=lsm[:]))
            if debug and tt == 0:
                dbb = ctx.on("vector", [exp_ev], lambda e: [
                    e.tensor_copy(st8[:, 0:1], rmx[:]),
                    e.tensor_copy(st8[:, 1:2], lsm[:]),
                ][-1])
                ctx.on("sync", [dbb, exp_ev], lambda e: [
                    e.dma_start(out=ex_dbg[:], in_=S3[:, 0:1024]),
                    e.dma_start(out=ls_dbg[:], in_=st8[:, 0:2]),
                ])
            rl_ev = ctx.on("scalar", [exp_ev], lambda e: [
                e.activation(rl[:], lsm[:], AF.Ln),
                e.activation(rl[:], rl[:], AF.Exp, scale=-1.0),
            ][-1])
            wr_ev = ctx.on("vector", [rl_ev], lambda e: e.tensor_scalar(
                wrow, S3[:, 0:1024], rl[:, 0:1], None, op0=ALU.mult))
            if debug:
                ctx.on("sync", [wr_ev], lambda e, tt=tt: e.dma_start(out=wrt_o[tt], in_=wrow))
            cp_ev = None
            for nt in range(8):
                trev = ctx.on("tensor", [wr_ev, cp_ev], lambda e, nt=nt:
                              e.transpose(ptrk[:], wrow[:, 128 * nt:128 * (nt + 1)], ident[:]))
                cp_ev = ctx.on("vector", [trev], lambda e, nt=nt, ti=ti: e.tensor_copy(
                    wT[:, 1024 * nt + 128 * ti:1024 * nt + 128 * (ti + 1)], ptrk[:]))
            wt_ev = cp_ev
        for ot in range(32):
            i = ot % 2
            avl = ctx.on("gpsimd", [avl_free[i]], lambda e, ot=ot, i=i: e.dma_start(
                out=wbuf[i][:, 0:1024], in_=av_p[ot]))

            def mk_aux(e, ot=ot, i=i, half=half):
                mm = None
                for nt in range(8):
                    for c in range(2):
                        mm = e.matmul(pacc[:, 512 * c:512 * (c + 1)],
                                      wbuf[i][:, 128 * nt:128 * (nt + 1)],
                                      wT[:, 1024 * nt + 512 * c:1024 * nt + 512 * (c + 1)],
                                      start=(nt == 0), stop=(nt == 7))
                return mm
            am_ev = ctx.on("tensor", [avl, wt_ev, pacc_free], mk_aux)
            avl_free[i] = am_ev
            ax_ev = ctx.on("vector", [am_ev, aux_st], lambda e: e.tensor_copy(qkvrow[:, 0:1024], pacc[:, 0:1024]))
            pacc_free = ax_ev
            aux_st = ctx.on("sync", [ax_ev], lambda e, ot=ot, half=half: e.dma_start(
                out=aux_d[128 * ot:128 * (ot + 1), 1024 * half:1024 * (half + 1)], in_=qkvrow[:, 0:1024]))
            if debug:
                aux_st = ctx.on("sync", [aux_st], lambda e, ot=ot, half=half: e.dma_start(
                    out=auxo_o[128 * ot:128 * (ot + 1), 1024 * half:1024 * (half + 1)], in_=qkvrow[:, 0:1024]))

    vb_ev = None
    ab_free = aux_st
    for vt_i in range(2):
        wl = ctx.on("gpsimd", [wl_free[0] if vt_i == 0 else vb_ev], lambda e, vt_i=vt_i:
                    e.dma_start(out=wbuf[0][:], in_=wvd_p[vt_i]))
        mm_last = None
        for ot in range(32):
            al = ctx.on("sync", [ab_free], lambda e, ot=ot: e.dma_start(
                out=wbuf[1][:, 0:2048], in_=aux_d[128 * ot:128 * (ot + 1), :]))

            def mk_vb(e, ot=ot, vt_i=vt_i):
                mm = None
                for c in range(4):
                    mm = e.matmul(pacc[:, 512 * c:512 * (c + 1)],
                                  wbuf[0][:, 128 * ot:128 * (ot + 1)],
                                  wbuf[1][:, 512 * c:512 * (c + 1)],
                                  start=(ot == 0), stop=(ot == 31))
                return mm
            mm_last = ctx.on("tensor", [al, wl, pacc_free if ot == 0 else None, mm_last], mk_vb)
            ab_free = mm_last
        vb_ev = ctx.on("vector", [mm_last], lambda e, vt_i=vt_i: e.tensor_copy(
            wT[:, 2048 * vt_i:2048 * (vt_i + 1)], pacc[:]))
        pacc_free = vb_ev

    a3_st = None
    wl2 = None
    for ot in range(32):
        wl2 = ctx.on("gpsimd", [vb_ev if ot == 0 else None, wl2], lambda e, ot=ot:
                     e.dma_start(out=wvul[:], in_=wvu_p[ot]))
        def mk_a3(e, ot=ot):
            mm = None
            for vt_i in range(2):
                for c in range(4):
                    mm = e.matmul(pacc[:, 512 * c:512 * (c + 1)],
                                  wvul[:, 128 * vt_i:128 * (vt_i + 1)],
                                  wT[:, 2048 * vt_i + 512 * c:2048 * vt_i + 512 * (c + 1)],
                                  start=(vt_i == 0), stop=(vt_i == 1))
            return mm
        mm_ev = ctx.on("tensor", [wl2, vb_ev, pacc_free], mk_a3)
        wl2 = mm_ev
        a3_ev = ctx.on("vector", [mm_ev, a3_st], lambda e: e.tensor_copy(qkvrow[:], pacc[:]))
        pacc_free = a3_ev
        a3_st = ctx.on("sync", [a3_ev], lambda e, ot=ot: e.dma_start(
            out=aux3_o[128 * ot:128 * (ot + 1), :], in_=qkvrow[:]))

    C1 = 0.7978845608028654
    C2 = C1 * 0.044715
    m1_st = None
    for ut in range(8):
        i = ut % 2
        wl = ctx.on("gpsimd", [wl_free[i]], lambda e, ut=ut, i=i:
                    e.dma_start(out=wbuf[i][:], in_=wu1_p[ut]))

        def mk_m1(e, ut=ut, i=i):
            mm = None
            for ft in range(NFT):
                for c in range(4):
                    mm = e.matmul(pacc[:, 512 * c:512 * (c + 1)],
                                  wbuf[i][:, 128 * ft:128 * (ft + 1)],
                                  xb[:, T * ft + 512 * c:T * ft + 512 * (c + 1)],
                                  start=(ft == 0), stop=(ft == NFT - 1))
            return mm
        mm_ev = ctx.on("tensor", [wl, pacc_free], mk_m1)
        wl_free[i] = mm_ev
        e1 = ctx.on("vector", [mm_ev, cload], lambda e, ut=ut: e.tensor_scalar(
            S1[:], pacc[:], bu1[:, ut:ut + 1], None, op0=ALU.add))
        pacc_free = e1
        e2 = ctx.on("scalar", [e1], lambda e: e.activation(S2[:], S1[:], AF.Square))
        e3 = ctx.on("vector", [e2], lambda e: e.tensor_scalar(S2[:], S2[:], C2, C1, op0=ALU.mult, op1=ALU.add))
        e4 = ctx.on("vector", [e3], lambda e: e.tensor_mul(S2[:], S1[:], S2[:]))
        e5 = ctx.on("scalar", [e4], lambda e: e.activation(S3[:], S2[:], AF.Tanh))
        e6 = ctx.on("vector", [e5], lambda e: e.tensor_scalar(S3[:], S3[:], 0.5, 0.5, op0=ALU.mult, op1=ALU.add))
        e7 = ctx.on("vector", [e6, m1_st], lambda e: e.tensor_mul(qkvrow[:], S1[:], S3[:]))
        m1_st = ctx.on("sync", [e7], lambda e, ut=ut: e.dma_start(
            out=m1_d[128 * ut:128 * (ut + 1), :], in_=qkvrow[:]))
        if debug and ut == 0:
            m1_st = ctx.on("sync", [m1_st], lambda e: e.dma_start(out=m1_dbg[:], in_=qkvrow[:]))

    u2_mm = None
    ml_ev = None
    for ut in range(8):
        ml_ev = ctx.on("sync", [(m1_st if ut == 0 else None),
                                (wl_free[0] if ut == 0 else None), u2_mm],
                       lambda e, ut=ut: e.dma_start(
            out=wbuf[0][:, 0:2048], in_=m1_d[128 * ut:128 * (ut + 1), :]))

        def mk_u2(e, ut=ut):
            mm = None
            for tt in range(NTT):
                mm = e.matmul(pst[:, tt:tt + 1], wbuf[0][:, 128 * tt:128 * (tt + 1)],
                              wu2c[:, ut:ut + 1], start=(ut == 0 and tt == 0), stop=(ut == 7))
            return mm
        u2_mm = ctx.on("tensor", [ml_ev, cload], mk_u2)
    if debug:
        du2 = ctx.on("vector", [u2_mm], lambda e: e.tensor_copy(S1[:, 0:16], pst[:, 0:16]))
        ctx.on("sync", [du2], lambda e: e.dma_start(out=u2_dbg[:], in_=S1[:, 0:16]))
    lr_ev = ctx.on("vector", [u2_mm], lambda e: e.tensor_scalar(
        lrn[:], pst[:, 0:16], bu2[:, 0:1], None, op0=ALU.add))
    lr_st = ctx.on("sync", [lr_ev], lambda e: e.dma_start(out=lrn_o[:], in_=lrn[:]))

    ctx.wait_only("gpsimd", [lr_st, a3_st, aux_st, m1_st])
    es.close()
    return nc


def _prep_l3(inp):
    if "wqkv_p" in _PC:
        return
    _PC["wqkv_p"] = pack_lhsT(np.ascontiguousarray(inp["L1_attn_w_qkv"].T))
    _PC["bqkv_c"] = np.ascontiguousarray(inp["L1_attn_b_qkv"].reshape(48, 128).T)
    _PC["wu1_p"] = pack_lhsT(np.ascontiguousarray(inp["L1_ffn_Wu1"].T))
    _PC["bu1_c"] = np.ascontiguousarray(inp["L1_ffn_bu1"].reshape(8, 128).T)
    _PC["wqr_pk"] = np.ascontiguousarray(
        inp["L1_ffn_Wq"].T.reshape(32, 128, 64).transpose(1, 0, 2).reshape(128, 2048))
    _PC["wr_s"] = np.ascontiguousarray(inp["L1_ffn_Wr"].T)
    _PC["akT"] = np.ascontiguousarray(inp["L1_ffn_aux_keys"].T)
    _PC["logrel"] = np.ascontiguousarray(inp["L1_ffn_log_rel"].reshape(1, 1024))
    _PC["av_p"] = pack_lhsT(np.ascontiguousarray(inp["L1_ffn_aux_values"]))
    _PC["wvd_p"] = pack_lhsT(np.ascontiguousarray(inp["L1_ffn_Wvd"].T))
    _PC["wvu_p"] = pack_lhsT(np.ascontiguousarray(inp["L1_ffn_Wvu"].T))
    _PC["wu2_s"] = np.ascontiguousarray(inp["L1_ffn_Wu2"][0].reshape(8, 128).T)
    _PC["bu2_c"] = np.ascontiguousarray(np.broadcast_to(
        inp["L1_ffn_bu2"].reshape(1, 1), (128, 1)))


def l3_in_map(inp, core, xn1T):
    _prep_l3(inp)
    return {
        "xn1T": xn1T,
        "wqkv_p": _PC["wqkv_p"], "bqkv_c": _PC["bqkv_c"],
        "wu1_p": _PC["wu1_p"], "bu1_c": _PC["bu1_c"],
        "wqr_pk": _PC["wqr_pk"], "wr_s": _PC["wr_s"], "akT": _PC["akT"],
        "logrel": _PC["logrel"], "av_p": _PC["av_p"], "wvd_p": _PC["wvd_p"],
        "wvu_p": _PC["wvu_p"], "wu2_s": _PC["wu2_s"], "bu2_c": _PC["bu2_c"],
        "ident": _PC["ident"],
    }



def build_launch4(debug=False, n_heads=8, do_pass1=True, do_pass2=True, do_pv=True, do_l=True, do_diag=True, do_exp=True, do_accum=True, do_fin=True, fin_mode='all', nfb=None):
    nc = bass.Bass("TRN2", target_bir_lowering=False)
    qT_d = nc.dram_tensor("qT", [2048, T], BF16, kind="ExternalInput").ap()
    ko_d = nc.dram_tensor("kT", [2048, T], BF16, kind="ExternalInput").ap()
    vo_d = nc.dram_tensor("vt", [16, 16, 128, 128], BF16, kind="ExternalInput").ap()
    kf_d = nc.dram_tensor("k_full", [2048, 8192], BF16, kind="ExternalInput").ap()
    vf_d = nc.dram_tensor("v_full2", [16, 8, 128, 1024], BF16, kind="ExternalInput").ap()
    offs1_d = nc.dram_tensor("offs1", [128, 256], F32, kind="ExternalInput").ap()
    sup2_d = nc.dram_tensor("sup2", [128, 256], F32, kind="ExternalInput").ap()
    tri1_d = nc.dram_tensor("tri1", [4, 128, 512], F32, kind="ExternalInput").ap()
    tri2_d = nc.dram_tensor("tri2", [4, 128, 512], F32, kind="ExternalInput").ap()
    aux3_d = nc.dram_tensor("aux3T", [4096, T], BF16, kind="ExternalInput").ap()
    xn1_d = nc.dram_tensor("xn1T", [4096, T], BF16, kind="ExternalInput").ap()
    x1_d = nc.dram_tensor("x1T", [4096, T], F32, kind="ExternalInput").ap()
    lgv_d = nc.dram_tensor("lgv", [128, 16], F32, kind="ExternalInput").ap()
    lgp8_d = nc.dram_tensor("lgp8", [1, 8], F32, kind="ExternalInput").ap()
    lrn_d = nc.dram_tensor("learned", [128, 16], F32, kind="ExternalInput").ap()
    gw1_d = nc.dram_tensor("gw1_c", [128, 1], F32, kind="ExternalInput").ap()
    gb_d = nc.dram_tensor("gb_c", [128, 1], F32, kind="ExternalInput").ap()
    wo1_p = nc.dram_tensor("wo1_p", [32, 128, 2048], F32, kind="ExternalInput").ap()
    bo1_c = nc.dram_tensor("bo1_c", [128, 32], F32, kind="ExternalInput").ap()
    ident_d = nc.dram_tensor("ident", [128, 128], BF16, kind="ExternalInput").ap()

    out_o = nc.dram_tensor("outT", [4096, T], F32, kind="ExternalOutput").ap()
    if debug:
        vals_o = nc.dram_tensor("vals_dbg", [2048, T], BF16, kind="ExternalOutput").ap()
        gate_o = nc.dram_tensor("gate_dbg", [128, 16], F32, kind="ExternalOutput").ap()
        s1_o = nc.dram_tensor("s1T", [4096, T], F32, kind="ExternalOutput").ap()
        g1_o = nc.dram_tensor("g1T", [4096, T], F32, kind="ExternalOutput").ap()
        mrun_o = nc.dram_tensor("mrun_dbg", [128, 16], F32, kind="ExternalOutput").ap()
        l_o = nc.dram_tensor("l_dbg", [1, 512], F32, kind="ExternalOutput").ap()
        sc1_o = nc.dram_tensor("sc1_dbg", [1, 16], F32, kind="ExternalOutput").ap()
        gg_o = nc.dram_tensor("gg_dbg", [128, 16], F32, kind="ExternalOutput").ap()

    es = ExitStack()
    S = lambda *a: es.enter_context(nc.sbuf_tensor(*a))
    P = lambda *a: es.enter_context(nc.psum_tensor(*a))
    valsT = S("valsT_sb", [128, 16 * T], BF16)
    qh = S("qh", [128, 2 * T], BF16)
    ko = S("ko", [128, 2 * T], BF16)
    kf = [S(f"kf{i}", [128, 2048], BF16) for i in range(2)]
    vt = [S(f"vt{i}", [128, 2048], BF16) for i in range(2)]
    m_run = S("m_run", [128, 16], F32)
    m_bc = S("m_bc", [128, 512], F32)
    S1 = S("S1f", [128, T], F32)
    Pt = S("Pt", [128, 512], BF16)
    mrow = S("mrow", [1, 512], F32)
    lrow = S("lrow", [1, 512], F32)
    tri1 = S("tri1s", [128, 4 * 512], F32)
    tri2 = S("tri2s", [128, 4 * 512], F32)
    offs1 = S("offs1s", [128, 256], F32)
    sup2 = S("sup2s", [128, 256], F32)
    lgv = S("lgvs", [128, 16], F32)
    lrn = S("lrns", [128, 16], F32)
    gg = S("ggs", [128, 16], F32)
    gh = S("ghs", [128, 16], F32)
    grgm = S("grgm", [128, 1], F32)
    gw1 = S("gw1s", [128, 1], F32)
    gb = S("gbs", [128, 1], F32)
    grow = S("grow", [1, T], F32)
    gate_bch = S("gate_bch", [128, T], BF16)
    wo_sb = [S(f"wo_sb{i}", [128, 2048], BF16) for i in range(2)]
    a3b = S("a3b", [128, T], BF16)
    xnb = S("xnb", [128, T], BF16)
    t1h = S("t1h", [128, T], BF16)
    x1b = S("x1b", [128, T], F32)
    S2 = S("S2f", [128, T], F32)
    g1f = S("g1f", [128, T], F32)
    onesc_h = S("onesc_h", [128, 1], BF16)
    ones1_f = S("ones1_f", [1, 128], F32)
    epsb6 = S("epsb6", [1, 1], F32)
    bo1 = S("bo1s", [128, 32], F32)
    sc1 = S("sc1", [1, 16], F32)
    ident = S("identS", [128, 128], BF16)
    identf = S("identfS", [128, 128], F32)

    pA = P("pA", [128, 1024], F32)
    pO = P("pO", [128, 1024], F32)
    pL = P("pL", [128, 512], F32)
    pB = P("pB", [128, 512], F32)

    ctx = Ctx(nc, es)

    def mk_consts(e):
        e.memset(onesc_h[:], 1.0)
        e.memset(ones1_f[:], 1.0)
        last = e.memset(epsb6[:], 1e-6)
        if n_heads < 8:
            last = e.memset(valsT[:], 0.0)
        return last
    consts = ctx.on("vector", [], mk_consts)
    cload = ctx.on("gpsimd", [], lambda e: [
        e.dma_start(out=ident[:], in_=ident_d[:]),
        e.dma_start(out=tri1[:].rearrange("p (r c) -> p r c", r=4), in_=tri1_d[:].rearrange("r p c -> p r c")),
        e.dma_start(out=tri2[:].rearrange("p (r c) -> p r c", r=4), in_=tri2_d[:].rearrange("r p c -> p r c")),
        e.dma_start(out=offs1[:], in_=offs1_d[:]),
        e.dma_start(out=sup2[:], in_=sup2_d[:]),
        e.dma_start(out=lgv[:], in_=lgv_d[:]),
        e.dma_start(out=lrn[:], in_=lrn_d[:]),
        e.dma_start(out=gw1[:], in_=gw1_d[:]),
        e.dma_start(out=gb[:], in_=gb_d[:]),
        e.dma_start(out=bo1[:], in_=bo1_c[:]),
        e.dma_start(out=sc1[0:1, 0:8], in_=lgp8_d[:]),
    ])
    cload = ctx.on("vector", [cload], lambda e: e.tensor_copy(identf[:], ident[:]))

    ev = ctx.on("vector", [cload], lambda e: e.reduce_sum(sc1[0:1, 8:9], sc1[0:1, 0:8], axis=AX.X))
    ev = ctx.on("vector", [ev], lambda e: e.tensor_scalar(
        sc1[0:1, 9:10], sc1[0:1, 8:9], 1.0 / 16384.0, None, op0=ALU.mult))
    ev = ctx.on("vector", [ev], lambda e: e.tensor_scalar(
        sc1[0:1, 10:11], sc1[0:1, 9:10], epsb6[0:1, 0:1], None, op0=ALU.add))
    ev = ctx.on("scalar", [ev], lambda e: e.activation(sc1[0:1, 11:12], sc1[0:1, 10:11], AF.Ln))
    ev = ctx.on("scalar", [ev], lambda e: e.activation(sc1[0:1, 12:13], sc1[0:1, 11:12], AF.Exp, scale=-1.0))
    ev = ctx.on("tensor", [ev, consts], lambda e: e.matmul(
        pB[:, 0:1], ones1_f[:], sc1[0:1, 12:13], start=True, stop=True))
    ev = ctx.on("vector", [ev], lambda e: e.tensor_copy(grgm[:], pB[:, 0:1]))
    ev = ctx.on("vector", [ev], lambda e: e.tensor_scalar(
        gg[:], lgv[:], grgm[:, 0:1], 0.5, op0=ALU.mult, op1=ALU.mult))
    sg_ev = ctx.on("scalar", [cload], lambda e: e.activation(gh[:], lrn[:], AF.Sigmoid))
    ev = ctx.on("vector", [ev, sg_ev], lambda e: e.scalar_tensor_tensor(
        gg[:], gh[:], 2.5, gg[:], op0=ALU.mult, op1=ALU.add))
    ev = ctx.on("vector", [ev], lambda e: e.tensor_scalar(
        gg[:], gg[:], 0.0, 5.0, op0=ALU.max, op1=ALU.min))
    ev = ctx.on("vector", [ev], lambda e: e.tensor_scalar(
        gg[:], gg[:], 0.5, 1.0 / 1.5, op0=ALU.subtract, op1=ALU.mult))
    ev = ctx.on("vector", [ev], lambda e: e.tensor_scalar(
        gg[:], gg[:], 0.0, 1.0, op0=ALU.max, op1=ALU.min))
    if debug:
        dgg = ctx.on("sync", [ev], lambda e: e.dma_start(out=gg_o[:], in_=gg[:]))
        ev = [ev, dgg]
    ev = ctx.on("scalar", [ev], lambda e: e.activation(gh[:], gg[:], AF.Sigmoid,
                                                       bias=gb[:, 0:1], scale=gw1[:, 0:1]))
    if debug:
        dsc = ctx.on("sync", [ev], lambda e: e.dma_start(out=sc1_o[:], in_=sc1[:]))
    gate_ev = ctx.on("vector", [ev], lambda e: e.tensor_scalar(
        gg[:], gh[:], 0.05, None, op0=ALU.is_ge))
    gate_ev = ctx.on("vector", [gate_ev], lambda e: e.tensor_mul(gg[:], gg[:], gh[:]))
    if debug:
        ctx.on("sync", [gate_ev], lambda e: e.dma_start(out=gate_o[:], in_=gg[:]))
    gr_ev = gate_ev
    for tt in range(NTT):
        tev = ctx.on("tensor", [gr_ev, cload], lambda e, tt=tt:
                     e.transpose(pB[0:1, 0:128], gg[:, tt:tt + 1], identf[:]))
        gr_ev = ctx.on("vector", [tev], lambda e, tt=tt:
                       e.tensor_copy(grow[:, 128 * tt:128 * (tt + 1)], pB[0:1, 0:128]))
    gb_mm = None
    for c in range(4):
        gb_mm = ctx.on("tensor", [gr_ev, gb_mm], lambda e, c=c: e.matmul(
            pB[:, 0:512], ones1_f[:],
            grow[:, 512 * c:512 * (c + 1)], start=True, stop=True))
        gb_mm = ctx.on("vector", [gb_mm], lambda e, c=c: e.tensor_copy(
            gate_bch[:, 512 * c:512 * (c + 1)], pB[:, 0:512]))
    gatebc_ev = gb_mm

    vals_ev = None
    psD_free = [gb_mm, gb_mm]
    pO_free = None
    kq_ld = None
    for h in range(n_heads):
        kq_ld = ctx.on("sync", [kq_ld, vals_ev], lambda e, h=h: [
            e.dma_start(out=qh[:, 0:T], in_=qT_d[256 * h:256 * h + 128, :]),
            e.dma_start(out=qh[:, T:2 * T], in_=qT_d[256 * h + 128:256 * h + 256, :]),
            e.dma_start(out=ko[:, 0:T], in_=ko_d[256 * h:256 * h + 128, :]),
            e.dma_start(out=ko[:, T:2 * T], in_=ko_d[256 * h + 128:256 * h + 256, :]),
        ])
        p1ev = None
        if not do_pass1:
            p1ev = ctx.on("vector", [kq_ld], lambda e: e.memset(m_run[:], 0.0))
        for qt in (range(NTT) if do_pass1 else []):
            oc_rel = qt // 4
            def mk_diag(e, qt=qt, oc=oc_rel):
                mm = None
                for ct in range(2):
                    mm = e.matmul(pA[:, 0:512],
                                  qh[:, T * ct + 128 * qt:T * ct + 128 * (qt + 1)],
                                  ko[:, T * ct + 512 * oc:T * ct + 512 * (oc + 1)],
                                  start=(ct == 0), stop=(ct == 1))
                return mm
            dmm = ctx.on("tensor", [kq_ld, p1ev, vals_ev], mk_diag)
            aev = ctx.on("vector", [dmm, cload], lambda e, qt=qt: e.tensor_add(
                S1[:, 0:512], pA[:, 0:512], tri1[:, 512 * (qt % 4):512 * (qt % 4 + 1)]))
            p1ev = ctx.on("vector", [aev], lambda e, qt=qt: e.reduce_max(
                m_run[:, qt:qt + 1], S1[:, 0:512], axis=AX.X))
        kl1 = None
        for fc in (range(16) if do_pass1 else []):
            kl1 = ctx.on("sync", [kl1, p1ev], lambda e, h=h, fc=fc: [
                e.dma_start(out=kf[fc % 2][:, 0:512], in_=kf_d[256 * h:256 * h + 128, 512 * fc:512 * (fc + 1)]),
                e.dma_start(out=kf[fc % 2][:, 1024:1536], in_=kf_d[256 * h + 128:256 * h + 256, 512 * fc:512 * (fc + 1)]),
            ])
            for qt in range(NTT):
                bank = qt % 2
                def mk_p1(e, qt=qt, fc=fc, bank=bank):
                    mm = None
                    for ct in range(2):
                        mm = e.matmul(pA[:, 512 * bank:512 * (bank + 1)],
                                      qh[:, T * ct + 128 * qt:T * ct + 128 * (qt + 1)],
                                      kf[fc % 2][:, 1024 * ct:1024 * ct + 512],
                                      start=(ct == 0), stop=(ct == 1))
                    return mm
                mm1 = ctx.on("tensor", [kl1, p1ev], mk_p1)
                p1ev = ctx.on("vector", [mm1], lambda e, qt=qt: e.reduce_max(
                    m_bc[:, 0:1], pA[:, 512 * (qt % 2):512 * (qt % 2 + 1)], axis=AX.X))
                p1ev = ctx.on("vector", [p1ev], lambda e, qt=qt, fc=fc: e.scalar_tensor_tensor(
                    m_run[:, qt:qt + 1], m_bc[:, 0:1], offs1[:, 16 * qt + fc:16 * qt + fc + 1],
                    m_run[:, qt:qt + 1], op0=ALU.add, op1=ALU.max))
        if debug and h == 0:
            ctx.on("sync", [p1ev], lambda e: e.dma_start(out=mrun_o[:], in_=m_run[:]))
        if not do_pass2:
            vals_ev = ctx.on("vector", [p1ev], lambda e: e.memset(valsT[:], 0.01))
            continue
        for qc in range(4):
            mr_ev = [p1ev, vals_ev]
            for i in range(4):
                tev = ctx.on("tensor", [mr_ev, cload], lambda e, i=i, qc=qc:
                             e.transpose(pB[0:1, 0:128], m_run[:, 4 * qc + i:4 * qc + i + 1], identf[:]))
                mr_ev = ctx.on("vector", [tev], lambda e, i=i: e.tensor_copy(
                    mrow[:, 128 * i:128 * (i + 1)], pB[0:1, 0:128]))
            bmm = ctx.on("tensor", [mr_ev], lambda e: e.matmul(
                pB[:, 0:512], ones1_f[:], mrow[:], start=True, stop=True))
            mbc_ev = ctx.on("vector", [bmm], lambda e: e.tensor_copy(m_bc[:], pB[:, 0:512]))

            ldk = None
            ldv = None
            pv_mm = None
            exp_prev = None
            first = True
            for fb in range(nfb if nfb is not None else (68 if do_diag else 64)):
                diag = fb >= 64
                if not diag:
                    if fb % 8 == 0:
                        ldk = ctx.on("sync", [ldk, mbc_ev if fb == 0 else None,
                                      psD_free[0], psD_free[1]],
                                     lambda e, h=h, fb=fb: [
                            e.dma_start(out=kf[(fb // 8) % 2][:, 0:1024],
                                        in_=kf_d[256 * h:256 * h + 128, 128 * fb:128 * fb + 1024]),
                            e.dma_start(out=kf[(fb // 8) % 2][:, 1024:2048],
                                        in_=kf_d[256 * h + 128:256 * h + 256, 128 * fb:128 * fb + 1024]),
                        ])
                        ldv = ctx.on("gpsimd", [ldv, mbc_ev if fb == 0 else None, pv_mm],
                                     lambda e, h=h, fb=fb: [
                            e.dma_start(out=vt[(fb // 8) % 2][:, 0:1024], in_=vf_d[2 * h, fb // 8]),
                            e.dma_start(out=vt[(fb // 8) % 2][:, 1024:2048], in_=vf_d[2 * h + 1, fb // 8]),
                        ])
                    kbuf = kf[(fb // 8) % 2]
                    vbuf = vt[(fb // 8) % 2]
                    koff = 128 * (fb % 8)
                    def mk_sc(e, qc=qc, kbuf=kbuf, koff=koff, bank=fb % 2):
                        mm = None
                        for ct in range(2):
                            mm = e.matmul(pA[:, 512 * bank:512 * (bank + 1)],
                                          kbuf[:, 1024 * ct + koff:1024 * ct + koff + 128],
                                          qh[:, T * ct + 512 * qc:T * ct + 512 * (qc + 1)],
                                          start=(ct == 0), stop=(ct == 1))
                        return mm
                else:
                    fd = fb - 64
                    def mk_sc(e, qc=qc, fd=fd, bank=fb % 2):
                        mm = None
                        for ct in range(2):
                            mm = e.matmul(pA[:, 512 * bank:512 * (bank + 1)],
                                          ko[:, T * ct + 512 * qc + 128 * fd:T * ct + 512 * qc + 128 * (fd + 1)],
                                          qh[:, T * ct + 512 * qc:T * ct + 512 * (qc + 1)],
                                          start=(ct == 0), stop=(ct == 1))
                        return mm
                sc_mm = ctx.on("tensor", [ldk if not diag else kq_ld, mbc_ev, psD_free[fb % 2]], mk_sc)
                sub_ev = ctx.on("vector", [sc_mm, exp_prev], lambda e, bank=fb % 2: e.tensor_sub(
                    S1[:, 0:512], pA[:, 512 * bank:512 * (bank + 1)], m_bc[:]))
                psD_free[fb % 2] = sub_ev
                if diag:
                    fd = fb - 64
                    sub_ev = ctx.on("vector", [sub_ev, cload], lambda e, fd=fd: e.tensor_add(
                        S1[:, 0:512], S1[:, 0:512], tri2[:, 512 * fd:512 * (fd + 1)]))
                    exp_ev = (ctx.on("scalar", [sub_ev, pv_mm], lambda e: e.activation(
                        Pt[:], S1[:, 0:512], AF.Exp)) if do_exp else
                        ctx.on("vector", [sub_ev, pv_mm], lambda e: e.tensor_copy(Pt[:], S1[:, 0:512])))
                else:
                    exp_ev = (ctx.on("scalar", [sub_ev, cload, pv_mm], lambda e, qc=qc, fb=fb: e.activation(
                        Pt[:], S1[:, 0:512], AF.Exp, bias=sup2[:, 64 * qc + fb:64 * qc + fb + 1])) if do_exp else
                        ctx.on("vector", [sub_ev, cload, pv_mm], lambda e: e.tensor_copy(Pt[:], S1[:, 0:512])))
                exp_prev = exp_ev

                def mk_pv(e, fb=fb, h=h, first=first, diag=diag):
                    if not do_pv:
                        return e.matmul(pL[0:1, 0:512], onesc_h[:], Pt[:], start=first, stop=(fb == (67 if do_diag else 63)))
                    if diag:
                        fd = fb - 64
                        v0 = vt[0][:, 0:128]
                        v1 = vt[0][:, 1024:1152]
                    else:
                        vbuf = vt[(fb // 8) % 2]
                        v0 = vbuf[:, 128 * (fb % 8):128 * (fb % 8 + 1)]
                        v1 = vbuf[:, 1024 + 128 * (fb % 8):1024 + 128 * (fb % 8 + 1)]
                    st_ = first if do_accum else True
                    last_fb = (nfb - 1) if nfb is not None else (67 if do_diag else 63)
                    sp_ = (fb == last_fb) if do_accum else True
                    mm = e.matmul(pO[:, 0:512], v0, Pt[:], start=st_, stop=sp_)
                    mm = e.matmul(pO[:, 512:1024], v1, Pt[:], start=st_, stop=sp_)
                    if do_l:
                        mm = e.matmul(pL[0:1, 0:512], onesc_h[:], Pt[:], start=st_, stop=sp_)
                    return mm
                if diag:
                    fd = fb - 64
                    ldvo = ctx.on("gpsimd", [pv_mm, exp_ev], lambda e, h=h, qc=qc, fd=fd: [
                        e.dma_start(out=vt[0][:, 0:128], in_=vo_d[2 * h, 4 * qc + fd]),
                        e.dma_start(out=vt[0][:, 1024:1152], in_=vo_d[2 * h + 1, 4 * qc + fd]),
                    ])
                    pv_mm = ctx.on("tensor", [exp_ev, ldvo, consts], mk_pv)
                else:
                    pv_mm = ctx.on("tensor", [exp_ev, ldv, consts, pO_free if first else None], mk_pv)
                first = False
            if not do_fin:
                vals_ev = ctx.on("vector", [pv_mm], lambda e, h=h, qc=qc: e.memset(
                    valsT[:, T * (2 * h) + 512 * qc:T * (2 * h) + 512 * (qc + 1)], 0.01))
                vals_ev = ctx.on("vector", [vals_ev], lambda e, h=h, qc=qc: e.memset(
                    valsT[:, T * (2 * h + 1) + 512 * qc:T * (2 * h + 1) + 512 * (qc + 1)], 0.01))
                pO_free = vals_ev
                continue
            lv_ev = ctx.on("vector", [pv_mm], lambda e: e.tensor_copy(lrow[:], pL[0:1, 0:512]))
            if debug and h == 0 and qc == 0:
                ctx.on("sync", [lv_ev], lambda e: e.dma_start(out=l_o[:], in_=lrow[:]))
            if fin_mode == "nolnexp":
                rl_ev = ctx.on("vector", [lv_ev], lambda e: e.tensor_scalar(
                    lrow[:], lrow[:], 1.0, None, op0=ALU.mult))
            elif fin_mode == "recipdve":
                rl_ev = ctx.on("vector", [lv_ev], lambda e: e.reciprocal(lrow[:], lrow[:]))
            else:
                rl_ev = ctx.on("scalar", [lv_ev], lambda e: e.activation(lrow[:], lrow[:], AF.Ln))
                rl_ev = ctx.on("scalar", [rl_ev], lambda e: e.activation(lrow[:], lrow[:], AF.Exp, scale=-1.0))
            rb_mm = ctx.on("tensor", [rl_ev], lambda e: e.matmul(
                pB[:, 0:512], ones1_f[:], lrow[:], start=True, stop=True))
            rb_ev = ctx.on("vector", [rb_mm], lambda e: e.tensor_copy(S1[:, 0:512], pB[:, 0:512]))
            if fin_mode == "nomul":
                v0_ev = ctx.on("vector", [rb_ev], lambda e, h=h, qc=qc: e.memset(
                    valsT[:, T * (2 * h) + 512 * qc:T * (2 * h) + 512 * (qc + 1)], 0.01))
                v1_ev = ctx.on("vector", [v0_ev], lambda e, h=h, qc=qc: e.memset(
                    valsT[:, T * (2 * h + 1) + 512 * qc:T * (2 * h + 1) + 512 * (qc + 1)], 0.01))
            else:
                v0_ev = ctx.on("vector", [rb_ev], lambda e, h=h, qc=qc: e.tensor_mul(
                    valsT[:, T * (2 * h) + 512 * qc:T * (2 * h) + 512 * (qc + 1)],
                    pO[:, 0:512], S1[:, 0:512]))
                v1_ev = ctx.on("vector", [v0_ev], lambda e, h=h, qc=qc: e.tensor_mul(
                    valsT[:, T * (2 * h + 1) + 512 * qc:T * (2 * h + 1) + 512 * (qc + 1)],
                    pO[:, 512:1024], S1[:, 0:512]))
            pO_free = v1_ev
            vals_ev = v1_ev

    if debug:
        ctx.on("sync", [vals_ev], lambda e: [
            e.dma_start(out=vals_o[128 * ct:128 * (ct + 1), :], in_=valsT[:, T * ct:T * (ct + 1)])
            for ct in range(16)])

    st_ev = None
    wl_free = [None, None]
    ld_free = None
    ev_free = None
    for ot in range(32):
        i = ot % 2
        wl = ctx.on("gpsimd", [wl_free[i]], lambda e, ot=ot, i=i: e.dma_start(out=wo_sb[i][:], in_=wo1_p[ot]))
        ldx = ctx.on("sync", [ld_free], lambda e, ot=ot: [
            e.dma_start(out=a3b[:], in_=aux3_d[128 * ot:128 * (ot + 1), :]),
            e.dma_start(out=xnb[:], in_=xn1_d[128 * ot:128 * (ot + 1), :]),
            e.dma_start(out=x1b[:], in_=x1_d[128 * ot:128 * (ot + 1), :]),
        ])
        halves_ev = []
        for half in range(2):
            def mk_s1(e, ot=ot, i=i, half=half):
                mm = None
                for ct in range(16):
                    for c in range(2):
                        mm = e.matmul(pA[:, 512 * c:512 * (c + 1)],
                                      wo_sb[i][:, 128 * ct:128 * (ct + 1)],
                                      valsT[:, T * ct + 1024 * half + 512 * c:T * ct + 1024 * half + 512 * (c + 1)],
                                      start=(ct == 0), stop=(ct == 15))
                return mm
            mm_ev = ctx.on("tensor", [wl, vals_ev, ev_free if half == 0 and ot > 0 else None,
                                      halves_ev[0] if half == 1 else None], mk_s1)
            e1 = ctx.on("vector", [mm_ev, cload, st_ev], lambda e, ot=ot, half=half: e.tensor_scalar(
                S2[:, 1024 * half:1024 * (half + 1)], pA[:, 0:1024], bo1[:, ot:ot + 1], None, op0=ALU.add))
            halves_ev.append(e1)
        s1_ev = halves_ev[1]
        ev1 = ctx.on("vector", [ldx, gatebc_ev, st_ev], lambda e: e.tensor_mul(t1h[:], a3b[:], gate_bch[:]))
        ev2 = ctx.on("vector", [ev1], lambda e: e.tensor_add(t1h[:], xnb[:], t1h[:]))
        ev3 = ctx.on("scalar", [ev2], lambda e: e.activation(g1f[:], t1h[:], AF.Silu))
        dsg = None
        if debug:
            dsg = ctx.on("sync", [ev3, s1_ev], lambda e, ot=ot: [
                e.dma_start(out=s1_o[128 * ot:128 * (ot + 1), :], in_=S2[:]),
                e.dma_start(out=g1_o[128 * ot:128 * (ot + 1), :], in_=g1f[:]),
            ])
        ev4 = ctx.on("vector", [ev3, s1_ev, dsg if debug else None], lambda e: e.tensor_mul(S2[:], S2[:], g1f[:]))
        ev5 = ctx.on("vector", [ev4], lambda e: e.tensor_add(S2[:], S2[:], x1b[:]))
        ev_free = ev5
        ld_free = ev5
        if debug:
            ev5 = [ev5, dsg]
        st_ev = ctx.on("sync", [ev5], lambda e, ot=ot: e.dma_start(
            out=out_o[128 * ot:128 * (ot + 1), :], in_=S2[:]))

    ctx.wait_only("gpsimd", [st_ev])
    es.close()
    return nc


def _prep_l4(inp):
    if "wo1_p" in _PC:
        return
    _PC["wo1_p"] = pack_lhsT(np.ascontiguousarray(inp["L1_attn_w_o"].T))
    _PC["bo1_c"] = np.ascontiguousarray(inp["L1_attn_b_o"].reshape(32, 128).T)
    _PC["gw1_c"] = np.ascontiguousarray(np.broadcast_to(
        np.asarray(inp["L1_ffn_gate_w1"], np.float32).reshape(1, 1), (128, 1)))
    _PC["gb_c"] = np.ascontiguousarray(np.broadcast_to(
        np.asarray(inp["L1_ffn_gate_bias"], np.float32).reshape(1, 1), (128, 1)))
    p = np.arange(128)[:, None]
    cc = np.arange(512)[None, :]
    tri1 = np.zeros((4, 128, 512), np.float32)
    tri2 = np.zeros((4, 128, 512), np.float32)
    for r in range(4):
        tri1[r] = np.where(cc <= 128 * r + p, 0.0, -1e9)
        tri2[r] = np.where(128 * r + p <= cc, 0.0, -1e9)
    _PC["tri1"] = tri1
    _PC["tri2"] = tri2


def l4_suppr(core):
    j = core % 4
    offs1 = np.zeros((128, 256), np.float32)
    for qt in range(16):
        oc = 4 * j + qt // 4
        for fc in range(16):
            offs1[:, 16 * qt + fc] = 0.0 if fc < oc else -1e9
    sup2 = np.zeros((128, 256), np.float32)
    for qc in range(4):
        own = 4 * (4 * j + qc)
        for fb in range(64):
            sup2[:, 64 * qc + fb] = 0.0 if fb < own else -1e9
    return offs1, sup2


def l4_in_map(inp, core, r2, r3, kfull_b, vfull_b, lgp8):
    _prep_l4(inp)
    offs1, sup2 = l4_suppr(core)
    return {
        "qT": r3["qT"], "kT": r3["kT"], "vt": r3["vt"],
        "k_full": kfull_b, "v_full2": vfull_b,
        "offs1": offs1, "sup2": sup2,
        "tri1": _PC["tri1"], "tri2": _PC["tri2"],
        "aux3T": r3["aux3T"], "xn1T": r2["xn1T"], "x1T": r2["x1T"],
        "lgv": r2["lgv"], "lgp8": lgp8, "learned": r3["learned"],
        "gw1_c": _PC["gw1_c"], "gb_c": _PC["gb_c"],
        "wo1_p": _PC["wo1_p"], "bo1_c": _PC["bo1_c"],
        "ident": _PC["ident"],
    }


def kernel(**inputs):
    _PC.clear()
    inp = {k: np.asarray(v) for k, v in inputs.items()}

    res1 = run_bass_kernel_spmd(build_launch1(), [l1_in_map(inp, c) for c in range(8)],
                                core_ids=list(range(8)))
    res2a = run_bass_kernel_spmd(build_launch2a(), [l2a_in_map(inp, c) for c in range(8)],
                                 core_ids=list(range(8)))
    w2_p = assemble_w2(res2a)
    res2b = run_bass_kernel_spmd(build_launch2b(), [l2b_in_map(inp, c, res1, w2_p) for c in range(8)],
                                 core_ids=list(range(8)))
    res3 = run_bass_kernel_spmd(build_launch3(),
                                [l3_in_map(inp, c, res2b.results[c]["xn1T"]) for c in range(8)],
                                core_ids=list(range(8)))
    r3 = [res3.results[c] for c in range(8)]
    r2 = [res2b.results[c] for c in range(8)]
    kfull = [np.ascontiguousarray(np.concatenate([r3[4 * b + j]["kT"] for j in range(4)], axis=1))
             for b in range(2)]
    vfull = []
    for b in range(2):
        vf = np.concatenate([r3[4 * b + j]["vt"] for j in range(4)], axis=1)
        vfull.append(np.ascontiguousarray(
            vf.reshape(16, 8, 8, 128, 128).transpose(0, 1, 3, 2, 4).reshape(16, 8, 128, 1024)))
    lgp8 = np.ascontiguousarray(np.concatenate([r2[c]["lgp"] for c in range(8)], axis=1))
    res4 = run_bass_kernel_spmd(build_launch4(),
                                [l4_in_map(inp, c, r2[c], r3[c], kfull[c // 4], vfull[c // 4], lgp8)
                                 for c in range(8)],
                                core_ids=list(range(8)))
    out = np.empty((2, 8192, 4096), np.float32)
    for c in range(8):
        b, j = c // 4, c % 4
        out[b, T * j:T * (j + 1), :] = res4.results[c]["outT"].T
    return out


# version 03
# speedup vs baseline: 1.0072x; 1.0072x over previous
import numpy as np
import ml_dtypes
from contextlib import ExitStack
import concourse.bass as bass
import concourse.mybir as mybir
from concourse.bass_utils import run_bass_kernel_spmd

F32 = mybir.dt.float32
BF16 = mybir.dt.float16
RBF16 = mybir.dt.bfloat16
I32 = mybir.dt.int32
AF = mybir.ActivationFunctionType
ALU = mybir.AluOpType
AX = mybir.AxisListType

BF = np.float16
T = 2048
NFT = 32
NTT = 16

ENGINES = ("sync", "gpsimd", "vector", "scalar", "tensor")
DMA_ENG = {"sync", "gpsimd"}


class Ctx:

    def __init__(self, nc, es):
        self.nc = nc
        self.es = es
        self.block = es.enter_context(nc.Block())
        self.sem = {n: es.enter_context(nc.semaphore("sem_" + n)) for n in ENGINES}
        self.cnt = {n: 0 for n in ENGINES}

    @staticmethod
    def _flat(deps):
        out = []
        for d in deps:
            if d is None:
                continue
            if isinstance(d, list):
                out.extend(x for x in d if x is not None)
            else:
                out.append(d)
        best = {}
        for (s, c) in out:
            best[s] = max(best.get(s, 0), c)
        return sorted(best.items())

    def barrier(self):
        snap = dict(self.cnt)
        for e in ENGINES:
            deps = [(o, snap[o]) for o in ENGINES if o != e]
            self.on(e, deps, lambda eng: eng.nop(), step_override=1)

    def on(self, eng_name, deps, emit, signal=True, step_override=None):
        step = step_override if step_override is not None else (16 if eng_name in DMA_ENG else 1)
        sem = self.sem[eng_name]
        deps = self._flat(deps)

        def body(eng):
            for (s, c) in deps:
                if c > 0:
                    eng.wait_ge(self.sem[s], c)
            ret = emit(eng)
            if signal:
                if isinstance(ret, list):
                    for ins in ret:
                        ins.then_inc(sem, step)
                        self.cnt[eng_name] += step
                else:
                    ret.then_inc(sem, step)
                    self.cnt[eng_name] += step

        getattr(self.block, eng_name)(body)
        return (eng_name, self.cnt[eng_name])

    def wait_only(self, eng_name, deps):
        deps = self._flat(deps)

        def body(eng):
            for (s, c) in deps:
                if c > 0:
                    eng.wait_ge(self.sem[s], c)

        getattr(self.block, eng_name)(body)


class FusedEnv:

    def __init__(self, nc, ctx, prefix, overrides):
        self.nc = nc
        self.ctx = ctx
        self.prefix = prefix
        self.overrides = overrides
        self.es = ExitStack()

    def dram(self, name, shape, dtype, kind):
        if name in self.overrides:
            return self.overrides[name]
        return self.nc.dram_tensor(self.prefix + name, shape, dtype, kind=kind).ap()

    def close(self):
        self.es.close()



def pack_lhsT(WT):
    Fd, Od = WT.shape
    return np.ascontiguousarray(
        WT.reshape(Fd // 128, 128, Od // 128, 128).transpose(2, 1, 0, 3).reshape(Od // 128, 128, Fd)
    )



def build_launch1(debug=False, F=None):
    standalone = F is None
    if standalone:
        nc = bass.Bass("TRN2", target_bir_lowering=False)
        D = lambda name, shape, dtype, kind: nc.dram_tensor(name, shape, dtype, kind=kind).ap()
    else:
        nc = F.nc
        D = F.dram
    xT = D("xT", [4096, T], F32, "ExternalInput")
    wq_p = D("wq_p", [32, 128, 4096], F32, "ExternalInput")
    wk_p = D("wk_p", [8, 128, 4096], F32, "ExternalInput")
    wv_p = D("wv_p", [8, 128, 4096], F32, "ExternalInput")
    had_p = D("had_p", [32, 128, 4096], F32, "ExternalInput")
    bq_c = D("bq_c", [128, 32], F32, "ExternalInput")
    bkv_c = D("bkv_c", [128, 16], F32, "ExternalInput")
    ident_d = D("ident", [128, 128], BF16, "ExternalInput")

    kvp_o = D("kvp", [8, 128, 129], F32, "ExternalOutput")
    qphi_o = D("qphiT", [4096, T], BF16, "ExternalOutput")
    rot_o = D("rotT", [4096, T], BF16, "ExternalOutput")
    if debug:
        xn_o = D("xn0T", [4096, T], BF16, "ExternalOutput")

    es = ExitStack() if standalone else F.es
    _pfx = "" if standalone else F.prefix
    S = lambda n, *a: es.enter_context(nc.sbuf_tensor(_pfx + n, *a))
    P = lambda n, *a: es.enter_context(nc.psum_tensor(_pfx + n, *a))
    xb = S("xb", [128, NFT * T], BF16)
    _xf0 = S("xf0", [128, T], F32)
    xf = [_xf0, _xf0]
    wbuf = [S(f"wbuf{i}", [128, 4096], BF16) for i in range(2)]
    abc = S("abc", [128, T], F32)
    bbc = S("bbc", [128, T], F32)
    stt = S("stt", [128, 16], F32)
    st2 = S("st2", [128, 16], F32)
    st3 = S("st3", [128, 16], F32)
    arow = S("arow", [1, 128], F32)
    brow = S("brow", [1, 128], F32)
    onesc_bf = S("onesc_bf", [128, 1], BF16)
    epsb = S("epsb", [128, 1], F32)
    ones1_f = S("ones1_f", [1, 128], F32)
    ident = S("identS", [128, 128], BF16)
    identf = S("identfS", [128, 128], F32)
    biasq = S("biasq", [128, 32], F32)
    biaskv = S("biaskv", [128, 16], F32)
    kcol = S("kcol", [128, T], BF16)
    vcol = S("vcol", [128, T], BF16)
    outt = kcol
    pa = S("pa", [128, T], F32)
    pex = S("pex", [128, T], BF16)
    prl = S("prl", [128, T], BF16)
    sq = pex
    xc = prl
    ktm = S("ktm", [128, 128], BF16)
    vtm = S("vtm", [128, 128], BF16)
    kvh = abc

    pacc = P("pacc", [128, T], F32)
    pkv = P("pkv", [128, 512], F32)
    ptrk = P("ptrk", [128, 128], BF16)
    ptrv = P("ptrv", [128, 128], BF16)
    pst = P("pst", [128, 64], F32)

    ctx = Ctx(nc, es) if standalone else F.ctx

    def phi_evac(psum_ap, out_ap, dep_pe, bias_col=None, extra_out_dep=None, sl=slice(0, T)):
        pa_, pex_, prl_ = pa[:, sl], pex[:, sl], prl[:, sl]
        if bias_col is not None:
            e1 = ctx.on("vector", [dep_pe, cload], lambda e: e.tensor_scalar(
                pa_, psum_ap, bias_col, 0.0, op0=ALU.add, op1=ALU.min))
            e3 = ctx.on("vector", [dep_pe, cload], lambda e: e.tensor_scalar(
                prl_, psum_ap, bias_col, 0.0, op0=ALU.add, op1=ALU.max))
        else:
            e1 = ctx.on("vector", [dep_pe], lambda e: e.tensor_scalar(
                pa_, psum_ap, 0.0, None, op0=ALU.min))
            e3 = ctx.on("vector", [dep_pe], lambda e: e.tensor_scalar(
                prl_, psum_ap, 0.0, None, op0=ALU.max))
        e2 = ctx.on("scalar", [e1],
                    lambda e: e.activation(pex_, pa_, AF.Exp))
        e4 = ctx.on("vector", [e2, e3, extra_out_dep],
                    lambda e: e.tensor_add(out_ap, pex_, prl_))
        return e4, [e1, e3, e4]

    consts = ctx.on("vector", [], lambda e: [
        e.memset(onesc_bf[:], 1.0),
        e.memset(ones1_f[:], 1.0),
        e.memset(epsb[:], 1e-5),
    ][-1])
    cload = ctx.on("gpsimd", [], lambda e: [
        e.dma_start(out=ident[:], in_=ident_d[:]),
        e.dma_start(out=biasq[:], in_=bq_c[:]),
        e.dma_start(out=biaskv[:], in_=bkv_c[:]),
    ])
    cload = ctx.on("vector", [cload], lambda e: e.tensor_copy(identf[:], ident[:]))

    xf_free = None
    stats_ev = None
    for ft in range(NFT):
        lev = ctx.on("sync", [xf_free],
                     lambda e, ft=ft: e.dma_start(out=xf[ft % 2][:], in_=xT[128 * ft:128 * (ft + 1), :]))
        cast_ev = ctx.on("vector", [lev, stats_ev],
                         lambda e, ft=ft: e.tensor_copy(xc[:], xf[ft % 2][:]))
        sqr_ev = ctx.on("scalar", [lev, stats_ev],
                        lambda e, ft=ft: e.activation(sq[:], xf[ft % 2][:], AF.Square))
        xf_free = [cast_ev, sqr_ev]

        def mk_stats(e, ft=ft):
            mm = None
            for tt in range(NTT):
                mm = e.matmul(pst[:, tt:tt + 1], xc[:, 128 * tt:128 * (tt + 1)], onesc_bf[:],
                              start=(ft == 0 and tt == 0), stop=(ft == NFT - 1))
            for tt in range(NTT):
                mm = e.matmul(pst[:, 16 + tt:17 + tt], sq[:, 128 * tt:128 * (tt + 1)], onesc_bf[:],
                              start=False, stop=(ft == NFT - 1))
            return mm
        stats_ev = ctx.on("tensor", [consts, cast_ev, sqr_ev], mk_stats)

    ev = ctx.on("vector", [stats_ev], lambda e: [
        e.tensor_scalar(stt[:], pst[:, 0:16], 1.0 / 4096.0, None, op0=ALU.mult),
        e.tensor_scalar(st2[:], pst[:, 16:32], 1.0 / 4096.0, None, op0=ALU.mult),
    ][-1])
    ev = ctx.on("scalar", [ev], lambda e: e.activation(st3[:], stt[:], AF.Square))
    ev = ctx.on("vector", [ev], lambda e: e.tensor_sub(st2[:], st2[:], st3[:]))
    ev = ctx.on("scalar", [ev], lambda e: [
        e.activation(st3[:], st2[:], AF.Ln, bias=epsb[:]),
        e.activation(st3[:], st3[:], AF.Exp, scale=-0.5),
    ][-1])
    rows_ev = ctx.on("vector", [ev], lambda e:
        e.tensor_mul(stt[:], stt[:], st3[:]))

    prow = pkv
    bc_ev = None
    for src_col, rowbuf in [(st3, arow), (stt, brow)]:
        for tt in range(NTT):
            tev = ctx.on("tensor", [rows_ev, cload, bc_ev],
                         lambda e, tt=tt, s=src_col: e.transpose(prow[0:1, 0:128], s[:, tt:tt + 1], identf[:]))
            cpe = ctx.on("vector", [tev],
                         lambda e, r=rowbuf: e.tensor_copy(r[:], prow[0:1, 0:128]))
            bc_ev = ctx.on("tensor", [cpe],
                           lambda e, tt=tt, r=rowbuf: e.matmul(
                               pacc[:, 128 * tt:128 * (tt + 1)], ones1_f[:], r[:],
                               start=(tt % 4 == 0), stop=True))
        if src_col is st3:
            abc_ev = ctx.on("vector", [bc_ev], lambda e: e.tensor_copy(abc[:], pacc[:]))
            bc_ev = abc_ev
        else:
            bbc_ev = ctx.on("vector", [bc_ev], lambda e: e.tensor_copy(bbc[:], pacc[:]))

    norm_ev = [None] * NFT
    for ft in range(NFT):
        lev = ctx.on("sync", [norm_ev[ft - 1] if ft >= 1 else [stats_ev] + xf_free],
                     lambda e, ft=ft: e.dma_start(out=xf[ft % 2][:], in_=xT[128 * ft:128 * (ft + 1), :]))
        norm_ev[ft] = ctx.on("vector", [lev, bbc_ev], lambda e, ft=ft: [
            e.tensor_mul(pa[:], xf[ft % 2][:], abc[:]),
            e.tensor_sub(xb[:, T * ft:T * (ft + 1)], pa[:], bbc[:]),
        ][-1])
    xn_ev = norm_ev[-1]

    dbg_ev = None
    if debug:
        dbg_ev = ctx.on("sync", [xn_ev], lambda e: [
            e.dma_start(out=xn_o[128 * ft:128 * (ft + 1), :], in_=xb[:, T * ft:T * (ft + 1)])
            for ft in range(NFT)])

    kv_evac_ev = None
    kv_st = None
    pacc_free = [xn_ev]
    tt_mm_ev = None
    vmm_ev = None
    for h in range(8):
        wload_ev = ctx.on("gpsimd", [vmm_ev], lambda e, h=h: [
            e.dma_start(out=wbuf[0][:], in_=wk_p[h]),
            e.dma_start(out=wbuf[1][:], in_=wv_p[h]),
        ])

        def mk_kmm(e, h=h):
            mm = None
            for ft in range(NFT):
                for c in range(4):
                    mm = e.matmul(pacc[:, 512 * c:512 * (c + 1)],
                                  wbuf[0][:, 128 * ft:128 * (ft + 1)],
                                  xb[:, T * ft + 512 * c:T * ft + 512 * (c + 1)],
                                  start=(ft == 0), stop=(ft == NFT - 1))
            return mm
        kmm_ev = ctx.on("tensor", [xn_ev, wload_ev, pacc_free, consts, cload], mk_kmm)
        kphi_ev, pacc_readers = phi_evac(pacc[:], kcol[:], kmm_ev, bias_col=biaskv[:, h:h + 1])

        def mk_vmm(e, h=h):
            mm = None
            for ft in range(NFT):
                for c in range(4):
                    mm = e.matmul(pacc[:, 512 * c:512 * (c + 1)],
                                  wbuf[1][:, 128 * ft:128 * (ft + 1)],
                                  xb[:, T * ft + 512 * c:T * ft + 512 * (c + 1)],
                                  start=(ft == 0), stop=(ft == NFT - 1))
            return mm
        vmm_ev = ctx.on("tensor", pacc_readers, mk_vmm)
        vcol_ev = ctx.on("vector", [vmm_ev], lambda e, h=h: e.tensor_scalar(
            vcol[:], pacc[:], biaskv[:, 8 + h:9 + h], None, op0=ALU.add))
        pacc_free = [vcol_ev]

        cp_ev = None
        for tt in range(NTT):
            trev = ctx.on("tensor", [kphi_ev, vcol_ev, cp_ev], lambda e, tt=tt: [
                e.transpose(ptrk[:], kcol[:, 128 * tt:128 * (tt + 1)], ident[:]),
                e.transpose(ptrv[:], vcol[:, 128 * tt:128 * (tt + 1)], ident[:]),
            ][-1])
            cp_ev = ctx.on("vector", [trev], lambda e, tt=tt: [
                e.tensor_copy(ktm[:], ptrk[:]),
                e.tensor_copy(vtm[:], ptrv[:]),
            ][-1])
            tt_mm_ev = ctx.on("tensor", [cp_ev, kv_evac_ev], lambda e, tt=tt: [
                e.matmul(pkv[:, 0:128], ktm[:], vtm[:], start=(tt == 0), stop=(tt == NTT - 1)),
                e.matmul(pkv[:, 128:129], ktm[:], onesc_bf[:], start=False, stop=(tt == NTT - 1)),
            ][-1])
        kv_evac_ev = ctx.on("vector", [tt_mm_ev, kv_st], lambda e, h=h:
            e.tensor_copy(kvh[:, 0:129], pkv[:, 0:129]))
        kv_st = ctx.on("sync", [kv_evac_ev], lambda e, h=h: e.dma_start(out=kvp_o[h], in_=kvh[:, 0:129]))

    store_ev = None
    outt_free = None
    wb_free = [vmm_ev, vmm_ev]
    hfree = [pacc_free, pacc_free]
    for phase, (w_p, out_d, has_phi) in enumerate(
            [(wq_p, qphi_o, True), (had_p, rot_o, False)]):
        for ot in range(32):
            i = ot % 2
            wl = ctx.on("gpsimd", [wb_free[i]],
                        lambda e, w_p=w_p, ot=ot, i=i: e.dma_start(out=wbuf[i][:], in_=w_p[ot]))
            half_mm = [None, None]
            for hf in range(2):
                def mk_mm(e, ot=ot, i=i, hf=hf):
                    mm = None
                    for ft in range(NFT):
                        for c in (2 * hf, 2 * hf + 1):
                            mm = e.matmul(pacc[:, 512 * c:512 * (c + 1)],
                                          wbuf[i][:, 128 * ft:128 * (ft + 1)],
                                          xb[:, T * ft + 512 * c:T * ft + 512 * (c + 1)],
                                          start=(ft == 0), stop=(ft == NFT - 1))
                    return mm
                half_mm[hf] = ctx.on("tensor", [wl] + hfree[hf], mk_mm)
            wb_free[i] = half_mm[1]
            store_ev = None
            for hf in range(2):
                sl = slice(1024 * hf, 1024 * (hf + 1))
                psl = pacc[:, sl]
                if has_phi:
                    evac_ev, rd = phi_evac(psl, outt[:, sl], half_mm[hf],
                                           bias_col=biasq[:, ot:ot + 1],
                                           extra_out_dep=outt_free, sl=sl)
                    hfree[hf] = rd
                else:
                    evac_ev = ctx.on("vector", [half_mm[hf], outt_free],
                                     lambda e, psl=psl, sl=sl: e.tensor_copy(outt[:, sl], psl))
                    hfree[hf] = [evac_ev]
                store_ev = ctx.on("sync", [evac_ev],
                                  lambda e, ot=ot, out_d=out_d, sl=sl:
                                  e.dma_start(out=out_d[128 * ot:128 * (ot + 1), sl], in_=outt[:, sl]))
            outt_free = store_ev
    pacc_free = hfree[0] + hfree[1]

    ctx.wait_only("gpsimd", [store_ev, kv_st, dbg_ev])

    if standalone:
        es.close()
    return nc



_PC = {}


def _prep_l1(inp):
    if "wq_p" in _PC:
        return
    _PC["wq_p"] = pack_lhsT(np.ascontiguousarray(inp["L0_attn_Wq"].T))
    _PC["wk_p"] = pack_lhsT(np.ascontiguousarray(inp["L0_attn_Wk"].T))
    _PC["wv_p"] = pack_lhsT(np.ascontiguousarray(inp["L0_attn_Wv"].T))
    had = inp["L0_ffn_hadamard"]
    hperm = np.concatenate([had[:, 0::2], had[:, 1::2]], axis=1)
    _PC["had_p"] = pack_lhsT(np.ascontiguousarray(hperm))
    _PC["ident"] = np.eye(128, dtype=BF)
    _PC["bq_c"] = np.ascontiguousarray(inp["L0_attn_bq"].reshape(32, 128).T)
    _PC["bkv_c"] = np.ascontiguousarray(
        np.concatenate([inp["L0_attn_bk"], inp["L0_attn_bv"]]).reshape(16, 128).T)


def l1_in_map(inp, core):
    _prep_l1(inp)
    b, j = core // 4, core % 4
    x = inp["L0_attn_hidden_states"][b, T * j:T * (j + 1), :]
    return {
        "xT": np.ascontiguousarray(x.T),
        "wq_p": _PC["wq_p"], "wk_p": _PC["wk_p"], "wv_p": _PC["wv_p"],
        "had_p": _PC["had_p"], "bq_c": _PC["bq_c"], "bkv_c": _PC["bkv_c"],
        "ident": _PC["ident"],
    }



def build_launch2a(F=None):
    standalone = F is None
    if standalone:
        nc = bass.Bass("TRN2", target_bir_lowering=False)
        D = lambda name, shape, dtype, kind: nc.dram_tensor(name, shape, dtype, kind=kind).ap()
    else:
        nc = F.nc
        D = F.dram
    dT = D("dT", [256, 4096], I32, "ExternalInput")
    sc = D("sc", [256, 4096], F32, "ExternalInput")
    lo_o = D("w2lo", [256, 4096], BF16, "ExternalOutput")
    hi_o = D("w2hi", [256, 4096], BF16, "ExternalOutput")

    es = ExitStack() if standalone else F.es
    _pfx = "" if standalone else F.prefix
    S = lambda n, *a: es.enter_context(nc.sbuf_tensor(_pfx + n, *a))
    dint = S("dint", [128, 4096], I32)
    t1i = S("t1i", [128, 4096], I32)
    scb = S("scb", [128, 4096], F32)
    nibf = S("nibf", [128, 4096], F32)
    wlo = S("wlo", [128, 4096], BF16)
    whi = S("whi", [128, 4096], BF16)
    ctx = Ctx(nc, es) if standalone else F.ctx

    st_ev = None
    for j in range(2 if standalone else 16):
        ld = ctx.on("sync", [st_ev], lambda e, j=j: [
            e.dma_start(out=dint[:], in_=dT[128 * j:128 * (j + 1), :]),
            e.dma_start(out=scb[:], in_=sc[128 * j:128 * (j + 1), :]),
        ])
        lo_ev = ctx.on("vector", [ld], lambda e: [
            e.tensor_scalar(t1i[:], dint[:], 15, None, op0=ALU.bitwise_and),
            e.tensor_scalar(nibf[:], t1i[:], 8.0, None, op0=ALU.subtract),
            e.tensor_mul(wlo[:], nibf[:], scb[:]),
        ][-1])
        hi_ev = ctx.on("vector", [lo_ev], lambda e: [
            e.tensor_scalar(t1i[:], dint[:], 4, 15, op0=ALU.logical_shift_right, op1=ALU.bitwise_and),
            e.tensor_scalar(nibf[:], t1i[:], 8.0, None, op0=ALU.subtract),
            e.tensor_mul(whi[:], nibf[:], scb[:]),
        ][-1])
        st_ev = ctx.on("sync", [lo_ev, hi_ev], lambda e, j=j: [
            e.dma_start(out=lo_o[128 * j:128 * (j + 1), :], in_=wlo[:]),
            e.dma_start(out=hi_o[128 * j:128 * (j + 1), :], in_=whi[:]),
        ])
    ctx.wait_only("gpsimd", [st_ev])
    if standalone:
        es.close()
    return nc



def build_launch2b(debug=False, F=None):
    standalone = F is None
    if standalone:
        nc = bass.Bass("TRN2", target_bir_lowering=False)
        D = lambda name, shape, dtype, kind: nc.dram_tensor(name, shape, dtype, kind=kind).ap()
    else:
        nc = F.nc
        D = F.dram
    rot_d = D("rotT", [4096, T], BF16, "ExternalInput")
    qphi_d = D("qphiT", [4096, T], BF16, "ExternalInput")
    if standalone:
        kvp4 = D("kvp4", [4, 8, 128, 129], F32, "ExternalInput")
    else:
        kvp4 = F.overrides["kv_red"]
    if standalone:
        w2_p = D("w2_p", [32, 128, 4096], BF16, "ExternalInput")
    else:
        w2_p = F.overrides["w2rows"]
    wo_p = D("wo_p", [32, 128, 4096], F32, "ExternalInput")
    fb_c = D("fb_c", [128, 32], F32, "ExternalInput")
    bo_c = D("bo_c", [128, 32], F32, "ExternalInput")
    xT = D("xT", [4096, T], F32, "ExternalInput")
    ident_d = D("ident", [128, 128], BF16, "ExternalInput")

    x1_o = D("x1T", [4096, T], F32, "ExternalOutput")
    xn1_o = D("xn1T", [4096, T], BF16, "ExternalOutput")
    lgv_o = D("lgv", [128, 16], F32, "ExternalOutput")
    lgp_o = D("lgp", [1, 16], F32, "ExternalOutput")
    if debug:
        g0_o = D("g0T", [4096, T], F32, "ExternalOutput")
        pst_dbg = D("pst_dbg", [128, 64], F32, "ExternalOutput")
        fin_dbg = D("fin_dbg", [128, 48], F32, "ExternalOutput")
        abc_dbg = D("abc_dbg", [128, 64], F32, "ExternalOutput")
        s0_o = D("s0T", [4096, T], F32, "ExternalOutput")
        attn_o = D("attnT", [4096, T], BF16, "ExternalOutput")
        kvs_o = D("kvsum_o", [128, 1032], F32, "ExternalOutput")
        rn_o = D("rnorm_o", [128, 8], F32, "ExternalOutput")

    g0_d = D("g0_d", [4096, T], F32, "Internal")

    es = ExitStack() if standalone else F.es
    _pfx = "" if standalone else F.prefix
    S = lambda n, *a: es.enter_context(nc.sbuf_tensor(_pfx + n, *a))
    P = lambda n, *a: es.enter_context(nc.psum_tensor(_pfx + n, *a))
    xb2 = S("xb2", [128, NFT * T], BF16)
    wbuf = [S(f"wbuf{i}", [128, 4096], BF16) for i in range(2)]
    xload = S("xload", [128, T], F32)
    gld = S("gld", [128, T], F32)
    sg = S("sg", [128, T], F32)
    x1c = S("x1c", [128, T], BF16)
    sqb = S("sqb", [128, T], RBF16)
    kvsum = S("kvsum", [128, 1032], F32)
    kvt2 = S("kvt2", [128, 1032], F32)
    kvbf = S("kvbf", [128, 1024], BF16)
    rnorm = S("rnorm", [128, 8], F32)
    fbias = S("fbias", [128, 32], F32)
    bobias = S("bobias", [128, 32], F32)
    stt = S("stt", [128, 16], F32)
    st2 = S("st2", [128, 16], F32)
    st3 = S("st3", [128, 16], F32)
    arow = S("arow", [1, 128], F32)
    onesc_bf = S("onesc_bf", [128, 1], BF16)
    onesc_bb = S("onesc_bb", [128, 1], RBF16)
    onesc_f = S("onesc_f", [128, 1], F32)
    ones1_f = S("ones1_f", [1, 128], F32)
    epsb = S("epsb", [128, 1], F32)
    oneb = S("oneb", [128, 1], F32)
    ident = S("identS", [128, 128], BF16)
    identf = S("identfS", [128, 128], F32)
    abc = S("abc", [128, T], F32)
    bbc = S("bbc", [128, T], F32)

    pacc = P("pacc", [128, T], F32)
    pst = P("pst", [128, 64], F32)
    ppr = P("ppr", [1, 128], F32)
    psc = P("psc", [1, 64], F32)

    ctx = Ctx(nc, es) if standalone else F.ctx

    consts = ctx.on("vector", [], lambda e: [
        e.memset(onesc_bf[:], 1.0),
        e.memset(onesc_bb[:], 1.0),
        e.memset(onesc_f[:], 1.0),
        e.memset(ones1_f[:], 1.0),
        e.memset(epsb[:], 1e-5),
        e.memset(oneb[:], 1.0),
    ][-1])
    cload = ctx.on("gpsimd", [], lambda e: [
        e.dma_start(out=ident[:], in_=ident_d[:]),
        e.dma_start(out=fbias[:], in_=fb_c[:]),
        e.dma_start(out=bobias[:], in_=bo_c[:]),
    ])
    cload = ctx.on("vector", [cload], lambda e: e.tensor_copy(identf[:], ident[:]))

    rl_ev = ctx.on("sync", [], lambda e: [
        e.dma_start(out=xb2[:, T * ft:T * (ft + 1)], in_=rot_d[128 * ft:128 * (ft + 1), :])
        for ft in range(NFT)])

    pacc_free = None
    bfree = [None, None]
    g_st = None
    wl_free = [None, None]
    for ot in range(32):
        i = ot % 2
        if standalone:
            wl = ctx.on("sync", [wl_free[i]], lambda e, ot=ot, i=i: e.dma_start(out=wbuf[i][:], in_=w2_p[ot]))
        else:
            wl = ctx.on("sync", [wl_free[i]], lambda e, ot=ot, i=i: e.dma_start(
                out=wbuf[i][:].rearrange("p (rt c) -> p rt c", rt=32),
                in_=w2_p.rearrange("(rt p) o -> p rt o", p=128)[:, :, 128 * ot:128 * (ot + 1)]))

        half_mm = [None, None]
        for hf in range(2):
            def mk_mm(e, ot=ot, i=i, hf=hf):
                mm = None
                for ft in range(NFT):
                    for c in (2 * hf, 2 * hf + 1):
                        mm = e.matmul(pacc[:, 512 * c:512 * (c + 1)],
                                      wbuf[i][:, 128 * ft:128 * (ft + 1)],
                                      xb2[:, T * ft + 512 * c:T * ft + 512 * (c + 1)],
                                      start=(ft == 0), stop=(ft == NFT - 1))
                return mm
            half_mm[hf] = ctx.on("tensor", [wl, rl_ev, bfree[hf]], mk_mm)
        wl_free[i] = half_mm[1]
        for hf in range(2):
            sl = slice(1024 * hf, 1024 * (hf + 1))
            e1 = ctx.on("vector", [half_mm[hf], cload, g_st], lambda e, ot=ot, sl=sl: e.tensor_scalar(
                sg[:, sl], pacc[:, sl], fbias[:, ot:ot + 1], None, op0=ALU.add))
            bfree[hf] = [e1]
            e2 = ctx.on("scalar", [e1, g_st], lambda e, sl=sl: e.activation(gld[:, sl], sg[:, sl], AF.Silu))
            g_st = ctx.on("sync", [e2], lambda e, ot=ot, sl=sl:
                          e.dma_start(out=g0_d[128 * ot:128 * (ot + 1), sl], in_=gld[:, sl]))
        if debug:
            g_st = ctx.on("sync", [g_st], lambda e, ot=ot:
                          e.dma_start(out=g0_o[128 * ot:128 * (ot + 1), :], in_=gld[:]))

    pacc_free = (bfree[0] or []) + (bfree[1] or [])
    if standalone:
        kr = ctx.on("sync", [], lambda e: e.dma_start(out=kvsum[:].rearrange("p (h c) -> p h c", h=8), in_=kvp4[0].rearrange("h p c -> p h c")))
        kr2 = ctx.on("sync", [kr], lambda e: e.dma_start(out=kvt2[:].rearrange("p (h c) -> p h c", h=8), in_=kvp4[1].rearrange("h p c -> p h c")))
        s_ev = ctx.on("vector", [kr2], lambda e: e.tensor_add(kvsum[:], kvsum[:], kvt2[:]))
        kr3 = ctx.on("sync", [s_ev], lambda e: e.dma_start(out=kvt2[:].rearrange("p (h c) -> p h c", h=8), in_=kvp4[2].rearrange("h p c -> p h c")))
        s_ev = ctx.on("vector", [kr3], lambda e: e.tensor_add(kvsum[:], kvsum[:], kvt2[:]))
        kr4 = ctx.on("sync", [s_ev], lambda e: e.dma_start(out=kvt2[:].rearrange("p (h c) -> p h c", h=8), in_=kvp4[3].rearrange("h p c -> p h c")))
        s_ev = ctx.on("vector", [kr4], lambda e: e.tensor_add(kvsum[:], kvsum[:], kvt2[:]))
    else:
        s_ev = ctx.on("sync", [F.collective_evs["kv_ar"]], lambda e: e.dma_start(
            out=kvsum[:].rearrange("p (h c) -> p h c", h=8),
            in_=kvp4.rearrange("h p c -> p h c")))
    kbf_ev = ctx.on("vector", [s_ev], lambda e: [
        e.tensor_copy(kvbf[:, 128 * h:128 * (h + 1)], kvsum[:, 129 * h:129 * h + 128])
        for h in range(8)][-1])
    rn_ev = ctx.on("vector", [s_ev], lambda e: [
        e.tensor_scalar(rnorm[:, h:h + 1], kvsum[:, 129 * h + 128:129 * h + 129],
                        1e-10, None, op0=ALU.add)
        for h in range(8)][-1])
    rn_ev = ctx.on("vector", [rn_ev], lambda e: e.reciprocal(rnorm[:], rnorm[:]))

    if debug:
        dkv = ctx.on("sync", [kbf_ev, rn_ev], lambda e: [
            e.dma_start(out=kvs_o[:], in_=kvsum[:]),
            e.dma_start(out=rn_o[:], in_=rnorm[:]),
        ])

    attn_done = None
    q_free = [wl_free[0], wl_free[1]]
    for qh in range(32):
        i = qh % 2
        ql = ctx.on("sync", [q_free[i]], lambda e, qh=qh, i=i: e.dma_start(
            out=wbuf[i][:, 0:T], in_=qphi_d[128 * qh:128 * (qh + 1), :]))

        def mk_attn(e, qh=qh, i=i):
            kvh = qh // 4
            mm = None
            for c in range(4):
                mm = e.matmul(pacc[:, 512 * c:512 * (c + 1)],
                              kvbf[:, 128 * kvh:128 * (kvh + 1)],
                              wbuf[i][:, 512 * c:512 * (c + 1)],
                              start=True, stop=True)
            return mm
        mm_ev = ctx.on("tensor", [ql, kbf_ev, pacc_free], mk_attn)
        q_free[i] = mm_ev
        ev = ctx.on("vector", [mm_ev, rn_ev], lambda e, qh=qh: e.tensor_scalar(
            xb2[:, T * qh:T * (qh + 1)], pacc[:], rnorm[:, qh // 4:qh // 4 + 1], None, op0=ALU.mult))
        pacc_free = ev
        attn_done = ev

    if debug:
        attn_done = ctx.on("sync", [attn_done], lambda e: [
            e.dma_start(out=attn_o[128 * f:128 * (f + 1), :], in_=xb2[:, T * f:T * (f + 1)])
            for f in range(NFT)])

    dfree = [[attn_done], [attn_done]]
    wl_free = [attn_done, attn_done]
    xl_ev = None
    stats_ev = None
    x1st = None
    for ot in range(32):
        i = ot % 2
        wl = ctx.on("gpsimd", [wl_free[i]], lambda e, ot=ot, i=i: e.dma_start(out=wbuf[i][:], in_=wo_p[ot]))
        xl = ctx.on("sync", [xl_ev], lambda e, ot=ot: [
            e.dma_start(out=xload[:], in_=xT[128 * ot:128 * (ot + 1), :]),
            e.dma_start(out=gld[:], in_=g0_d[128 * ot:128 * (ot + 1), :]),
        ])

        half_mm = [None, None]
        for hf in range(2):
            def mk_mm(e, ot=ot, i=i, hf=hf):
                mm = None
                for ft in range(NFT):
                    for c in (2 * hf, 2 * hf + 1):
                        mm = e.matmul(pacc[:, 512 * c:512 * (c + 1)],
                                      wbuf[i][:, 128 * ft:128 * (ft + 1)],
                                      xb2[:, T * ft + 512 * c:T * ft + 512 * (c + 1)],
                                      start=(ft == 0), stop=(ft == NFT - 1))
                return mm
            half_mm[hf] = ctx.on("tensor", [wl, attn_done, dfree[hf]], mk_mm)
        mm_ev = half_mm[1]
        wl_free[i] = mm_ev
        e1h = []
        for hf in range(2):
            sl = slice(1024 * hf, 1024 * (hf + 1))
            eh = ctx.on("vector", [half_mm[hf], cload, x1st], lambda e, ot=ot, sl=sl: e.tensor_scalar(
                sg[:, sl], pacc[:, sl], bobias[:, ot:ot + 1], None, op0=ALU.add))
            dfree[hf] = [eh]
            e1h.append(eh)
        e1 = e1h[1]
        if debug:
            sdbg = ctx.on("sync", [e1], lambda e, ot=ot:
                          e.dma_start(out=s0_o[128 * ot:128 * (ot + 1), :], in_=sg[:]))
            e1 = [e1, sdbg]
        e2 = ctx.on("vector", [e1h[0], e1h[1], xl], lambda e: [
            e.tensor_mul(gld[:], sg[:], gld[:]),
            e.tensor_add(xload[:], gld[:], xload[:]),
        ][-1])
        xl_ev = e2
        x1st = ctx.on("sync", [e2], lambda e, ot=ot:
                      e.dma_start(out=x1_o[128 * ot:128 * (ot + 1), :], in_=xload[:]))
        cst = ctx.on("vector", [e2], lambda e: e.tensor_copy(x1c[:], xload[:]))
        sqe = ctx.on("scalar", [e2, stats_ev], lambda e: e.activation(sqb[:], xload[:], AF.Square))
        xl_ev = [x1st, cst, sqe]

        def mk_stats(e, ot=ot):
            mm = None
            for tt in range(NTT):
                mm = e.matmul(pst[:, tt:tt + 1], x1c[:, 128 * tt:128 * (tt + 1)], onesc_bf[:],
                              start=(ot == 0 and tt == 0), stop=(ot == NFT - 1))
            for tt in range(NTT):
                mm = e.matmul(pst[:, 16 + tt:17 + tt], sqb[:, 128 * tt:128 * (tt + 1)], onesc_bb[:],
                              start=False, stop=(ot == NFT - 1))
            return mm
        stats_ev = ctx.on("tensor", [consts, cst, sqe], mk_stats)

    if debug:
        dps = ctx.on("vector", [stats_ev], lambda e: e.tensor_copy(sg[:, 0:64], pst[:]))
        ctx.on("sync", [dps], lambda e: e.dma_start(out=pst_dbg[:], in_=sg[:, 0:64]))
    ev = ctx.on("vector", [stats_ev], lambda e: [
        e.tensor_scalar(stt[:], pst[:, 0:16], 1.0 / 4096.0, None, op0=ALU.mult),
        e.tensor_scalar(st2[:], pst[:, 16:32], 1.0 / 4096.0, None, op0=ALU.mult),
    ][-1])
    ev = ctx.on("scalar", [ev], lambda e: e.activation(st3[:], stt[:], AF.Square))
    ev = ctx.on("vector", [ev], lambda e: e.tensor_sub(st2[:], st2[:], st3[:]))
    ev = ctx.on("scalar", [ev, consts], lambda e: [
        e.activation(st3[:], st2[:], AF.Ln, bias=epsb[:]),
        e.activation(st3[:], st3[:], AF.Exp, scale=-0.5),
    ][-1])
    rows_ev = ctx.on("vector", [ev], lambda e: e.tensor_mul(stt[:], stt[:], st3[:]))
    if debug:
        dfn = ctx.on("vector", [rows_ev], lambda e: [
            e.tensor_copy(sg[:, 0:16], stt[:]),
            e.tensor_copy(sg[:, 16:32], st2[:]),
            e.tensor_copy(sg[:, 32:48], st3[:]),
        ][-1])
        ctx.on("sync", [dfn], lambda e: e.dma_start(out=fin_dbg[:], in_=sg[:, 0:48]))

    bc_ev = None
    abc_ev = bbc_ev = None
    for src_col, dst in [(st3, abc), (stt, bbc)]:
        for tt in range(NTT):
            tev = ctx.on("tensor", [rows_ev, cload, bc_ev],
                         lambda e, tt=tt, s=src_col: e.transpose(ppr[0:1, 0:128], s[:, tt:tt + 1], identf[:]))
            cpe = ctx.on("vector", [tev], lambda e: e.tensor_copy(arow[:], ppr[0:1, 0:128]))
            bc_ev = ctx.on("tensor", [cpe], lambda e, tt=tt: e.matmul(
                pacc[:, 128 * tt:128 * (tt + 1)], ones1_f[:], arow[:],
                start=(tt % 4 == 0), stop=True))
        cpev = ctx.on("vector", [bc_ev], lambda e, dst=dst: e.tensor_copy(dst[:], pacc[:]))
        if src_col is st3:
            abc_ev = cpev
        else:
            bbc_ev = cpev
        bc_ev = cpev

    if debug:
        dab = ctx.on("vector", [abc_ev, bbc_ev], lambda e: [
            e.tensor_copy(sg[:, 0:32], abc[:, 0:32]),
            e.tensor_copy(sg[:, 32:64], bbc[:, 0:32]),
        ][-1])
        ctx.on("sync", [dab], lambda e: e.dma_start(out=abc_dbg[:], in_=sg[:, 0:64]))

    vstats_ev = None
    nx_ev = None
    xn_st = None
    for ft in range(NFT):
        lx = ctx.on("sync", [nx_ev], lambda e, ft=ft: e.dma_start(out=xload[:], in_=x1_o[128 * ft:128 * (ft + 1), :]))
        ne = ctx.on("vector", [lx, abc_ev, bbc_ev, xn_st], lambda e, ft=ft: [
            e.tensor_mul(sg[:], xload[:], abc[:]),
            e.tensor_sub(x1c[:], sg[:], bbc[:]),
        ][-1])
        nx_ev = ne
        xn_st = ctx.on("sync", [ne], lambda e, ft=ft:
                       e.dma_start(out=xn1_o[128 * ft:128 * (ft + 1), :], in_=x1c[:]))
        sqe = ctx.on("scalar", [ne, vstats_ev], lambda e: e.activation(sqb[:], x1c[:], AF.Square))

        def mk_vstats(e, ft=ft):
            mm = None
            for tt in range(NTT):
                mm = e.matmul(pst[:, 32 + tt:33 + tt], x1c[:, 128 * tt:128 * (tt + 1)], onesc_bf[:],
                              start=(ft == 0 and tt == 0), stop=(ft == NFT - 1))
            for tt in range(NTT):
                mm = e.matmul(pst[:, 48 + tt:49 + tt], sqb[:, 128 * tt:128 * (tt + 1)], onesc_bb[:],
                              start=False, stop=(ft == NFT - 1))
            return mm
        vstats_ev = ctx.on("tensor", [sqe, ne], mk_vstats)
        nx_ev = [nx_ev, sqe, vstats_ev]

    ev = ctx.on("vector", [vstats_ev], lambda e: [
        e.tensor_scalar(stt[:], pst[:, 32:48], 1.0 / 4096.0, None, op0=ALU.mult),
        e.tensor_scalar(st2[:], pst[:, 48:64], 1.0 / 4096.0, None, op0=ALU.mult),
    ][-1])
    ev = ctx.on("scalar", [ev], lambda e: e.activation(st3[:], stt[:], AF.Square))
    ev = ctx.on("vector", [ev], lambda e: e.tensor_sub(st2[:], st2[:], st3[:]))
    lg_ev = ctx.on("scalar", [ev, consts], lambda e: e.activation(st3[:], st2[:], AF.Ln, bias=oneb[:]))
    lg_st = ctx.on("sync", [lg_ev], lambda e: e.dma_start(out=lgv_o[:], in_=st3[:]))
    ev = ctx.on("vector", [lg_ev], lambda e: e.reduce_sum(stt[:, 0:1], st3[:], axis=AX.X))
    ev = ctx.on("tensor", [ev], lambda e: e.matmul(psc[0:1, 0:1], stt[:, 0:1], onesc_f[:],
                                                   start=True, stop=True))
    ev = ctx.on("vector", [ev], lambda e: e.memset(st2[0:1, 1:16], 0.0))
    ev = ctx.on("vector", [ev], lambda e: e.tensor_copy(st2[0:1, 0:1], psc[0:1, 0:1]))
    lp_st = ctx.on("sync", [ev], lambda e: e.dma_start(out=lgp_o[:], in_=st2[0:1, 0:16]))

    ctx.wait_only("gpsimd", [xn_st, lg_st, lp_st, x1st])
    if standalone:
        es.close()
    return nc


def _prep_l2(inp):
    if "w2ref" in _PC:
        return
    _PC["dTfull"] = np.ascontiguousarray(inp["L0_ffn_weight_data"].T)
    _PC["scbig"] = np.ascontiguousarray(np.repeat(inp["L0_ffn_weight_scales"].T, 64, axis=0))
    _PC["wo_p"] = pack_lhsT(np.ascontiguousarray(inp["L0_attn_Wo"].T))
    _PC["fb_c"] = np.ascontiguousarray(inp["L0_ffn_bias"].reshape(32, 128).T)
    _PC["bo_c"] = np.ascontiguousarray(inp["L0_attn_bo"].reshape(32, 128).T)
    _PC["w2ref"] = True


def l2a_in_map(inp, core):
    _prep_l2(inp)
    return {"dT": _PC["dTfull"][256 * core:256 * (core + 1)],
            "sc": _PC["scbig"][256 * core:256 * (core + 1)]}


def assemble_w2(res2a):
    w2 = np.empty((4096, 4096), dtype=BF)
    for c in range(8):
        w2[256 * c:256 * (c + 1)] = res2a.results[c]["w2lo"]
        w2[2048 + 256 * c:2048 + 256 * (c + 1)] = res2a.results[c]["w2hi"]
    return pack_lhsT(w2)


def l2b_in_map(inp, core, res1, w2_p):
    _prep_l2(inp)
    b, j = core // 4, core % 4
    x = inp["L0_attn_hidden_states"][b, T * j:T * (j + 1), :]
    kvp4 = np.stack([res1.results[4 * b + g]["kvp"] for g in range(4)])
    return {
        "rotT": res1.results[core]["rotT"],
        "qphiT": res1.results[core]["qphiT"],
        "kvp4": np.ascontiguousarray(kvp4),
        "w2_p": w2_p,
        "wo_p": _PC["wo_p"], "fb_c": _PC["fb_c"], "bo_c": _PC["bo_c"],
        "xT": np.ascontiguousarray(x.T),
        "ident": _PC["ident"],
    }



def build_launch3(debug=False, F=None):
    standalone = F is None
    if standalone:
        nc = bass.Bass("TRN2", target_bir_lowering=False)
        D = lambda name, shape, dtype, kind: nc.dram_tensor(name, shape, dtype, kind=kind).ap()
    else:
        nc = F.nc
        D = F.dram
    xn1_d = D("xn1T", [4096, T], BF16, "ExternalInput")
    wqkv_p = D("wqkv_p", [48, 128, 4096], F32, "ExternalInput")
    bqkv_c = D("bqkv_c", [128, 48], F32, "ExternalInput")
    wu1_p = D("wu1_p", [8, 128, 4096], F32, "ExternalInput")
    bu1_c = D("bu1_c", [128, 8], F32, "ExternalInput")
    wqr_pk = D("wqr_pk", [128, 2048], F32, "ExternalInput")
    wr_s = D("wr_s", [64, 48], F32, "ExternalInput")
    akT_d = D("akT", [64, 1024], F32, "ExternalInput")
    lr_d = D("logrel", [1, 1024], F32, "ExternalInput")
    av_p = D("av_p", [32, 128, 1024], F32, "ExternalInput")
    wvd_p = D("wvd_p", [2, 128, 4096], F32, "ExternalInput")
    wvu_p = D("wvu_p", [32, 128, 256], F32, "ExternalInput")
    wu2_s = D("wu2_s", [128, 8], F32, "ExternalInput")
    bu2_c = D("bu2_c", [128, 1], F32, "ExternalInput")
    ident_d = D("ident", [128, 128], BF16, "ExternalInput")

    q_o = D("qT", [2048, T], BF16, "ExternalOutput")
    k_o = D("kT", [2048, T], BF16, "ExternalOutput")
    if standalone:
        v_o = D("vt", [16, 16, 128, 128], BF16, "ExternalOutput")
    else:
        v_o = F.overrides["v2c"]
    aux3_o = D("aux3T", [4096, T], BF16, "ExternalOutput")
    lrn_o = D("learned", [128, 16], F32, "ExternalOutput")
    if debug:
        wrt_o = D("wrt", [16, 128, 1024], BF16, "ExternalOutput")
        auxo_o = D("auxoT", [4096, T], BF16, "ExternalOutput")
        rs_dbg = D("rs_dbg", [128, 1024], F32, "ExternalOutput")
        t8_dbg = D("t8_dbg", [128, 8], F32, "ExternalOutput")
        mk_dbg = D("mk_dbg", [128, 1024], F32, "ExternalOutput")
        ms_dbg = D("ms_dbg", [128, 1024], F32, "ExternalOutput")
        ex_dbg = D("ex_dbg", [128, 1024], F32, "ExternalOutput")
        ls_dbg = D("ls_dbg", [128, 2], F32, "ExternalOutput")
        u2_dbg = D("u2_dbg", [128, 16], F32, "ExternalOutput")
        as_dbg = D("as_dbg", [128, 1024], F32, "ExternalOutput")
        m1_dbg = D("m1_dbg", [128, T], BF16, "ExternalOutput")
        q_dbg = D("q_dbg", [64, T], BF16, "ExternalOutput")

    aux_d = D("aux_d", [4096, T], BF16, "Internal")
    m1_d = D("m1_d", [1024, T], BF16, "Internal")

    es = ExitStack() if standalone else F.es
    _pfx = "" if standalone else F.prefix
    S = lambda n, *a: es.enter_context(nc.sbuf_tensor(_pfx + n, *a))
    P = lambda n, *a: es.enter_context(nc.psum_tensor(_pfx + n, *a))
    xb = S("xb", [128, NFT * T], BF16)
    wbuf = [S(f"wbuf{i}", [128, 4096], BF16) for i in range(2)]
    wT = S("wT", [128, 8192], BF16)
    S1 = S("S1", [128, T], F32)
    S2 = S("S2", [128, T], F32)
    S3 = S("S3", [128, T], F32)
    qkvrow = S("qkvrow", [128, T], BF16)
    queryT = S("queryT", [64, T], BF16)
    q8T = S("q8T", [64, T], BF16)
    rqT = S("rqT", [48, T], BF16)
    rkT = S("rkT", [48, 1024], BF16)
    akT = S("akTs", [64, 1024], BF16)
    lrow = S("lrow", [1, 1024], BF16)
    wrow = None
    wrs = S("wrs", [64, 48], BF16)
    wu2c = S("wu2c", [128, 8], BF16)
    bqkv = S("bqkv", [128, 48], F32)
    bu1 = S("bu1", [128, 8], F32)
    bu2 = S("bu2", [128, 1], F32)
    st8 = S("st8", [128, 8], F32)
    rmx = S("rmx", [128, 1], F32)
    lsm = S("lsm", [128, 1], F32)
    rl = S("rl", [128, 1], F32)
    lrn = S("lrn", [128, 16], F32)
    vtm = S("vtm", [128, 128], BF16)
    ident = S("identS", [128, 128], BF16)
    ones1h = S("ones1h", [1, 128], BF16)
    wvul = S("wvul", [128, 256], BF16)

    wrow = qkvrow[:, 1024:2048]
    wqr = wT[:, 0:2048]

    pacc = P("pacc", [128, T], F32)
    prt = P("prt", [128, 1024], F32)
    pst = P("pst", [128, 16], F32)
    ptrk = P("ptrk", [128, 128], BF16)

    ctx = Ctx(nc, es) if standalone else F.ctx

    consts = ctx.on("vector", [], lambda e: e.memset(ones1h[:], 1.0))
    cload = ctx.on("gpsimd", [], lambda e: [
        e.dma_start(out=ident[:], in_=ident_d[:]),
        e.dma_start(out=bqkv[:], in_=bqkv_c[:]),
        e.dma_start(out=bu1[:], in_=bu1_c[:]),
        e.dma_start(out=bu2[:], in_=bu2_c[:]),
        e.dma_start(out=wrs[:], in_=wr_s[:]),
        e.dma_start(out=wqr, in_=wqr_pk[:]),
        e.dma_start(out=akT[:], in_=akT_d[:]),
        e.dma_start(out=lrow[:], in_=lr_d[:]),
        e.dma_start(out=wu2c[:], in_=wu2_s[:]),
    ])
    xl_ev = ctx.on("sync", [], lambda e: [
        e.dma_start(out=xb[:, T * ft:T * (ft + 1)], in_=xn1_d[128 * ft:128 * (ft + 1), :])
        for ft in range(NFT)])

    pacc_free = None
    efree = [None, None]
    wl_free = [None, None]
    row_free = None
    tr_free = None
    for ot in range(48):
        i = ot % 2
        wl = ctx.on("gpsimd", [wl_free[i]], lambda e, ot=ot, i=i: e.dma_start(out=wbuf[i][:], in_=wqkv_p[ot]))

        half_mm = [None, None]
        for hf in range(2):
            def mk_mm(e, ot=ot, i=i, hf=hf):
                mm = None
                for ft in range(NFT):
                    for c in (2 * hf, 2 * hf + 1):
                        mm = e.matmul(pacc[:, 512 * c:512 * (c + 1)],
                                      wbuf[i][:, 128 * ft:128 * (ft + 1)],
                                      xb[:, T * ft + 512 * c:T * ft + 512 * (c + 1)],
                                      start=(ft == 0), stop=(ft == NFT - 1))
                return mm
            half_mm[hf] = ctx.on("tensor", [wl, xl_ev, efree[hf]], mk_mm)
        mm_ev = half_mm[1]
        wl_free[i] = mm_ev
        evh = []
        for hf in range(2):
            sl = slice(1024 * hf, 1024 * (hf + 1))
            eh = ctx.on("vector", [half_mm[hf], cload, row_free], lambda e, ot=ot, sl=sl: e.tensor_scalar(
                qkvrow[:, sl], pacc[:, sl], bqkv[:, ot:ot + 1], None, op0=ALU.add))
            efree[hf] = [eh]
            evh.append(eh)
        ev = evh[1]
        pacc_free = ev
        if ot < 32:
            out_d = q_o if ot < 16 else k_o
            row = ot % 16
            for hf in range(2):
                sl = slice(1024 * hf, 1024 * (hf + 1))
                row_free = ctx.on("sync", [evh[hf]], lambda e, out_d=out_d, row=row, sl=sl:
                                  e.dma_start(out=out_d[128 * row:128 * (row + 1), sl], in_=qkvrow[:, sl]))
        else:
            ct = ot - 32
            vst = None
            for tt in range(NTT):
                trev = ctx.on("tensor", [ev, tr_free], lambda e, tt=tt:
                              e.transpose(ptrk[:], qkvrow[:, 128 * tt:128 * (tt + 1)], ident[:]))
                cpe = ctx.on("vector", [trev, vst], lambda e: e.tensor_copy(vtm[:], ptrk[:]))
                tr_free = cpe
                if standalone:
                    vst = ctx.on("sync", [cpe], lambda e, ct=ct, tt=tt:
                                 e.dma_start(out=v_o[ct, tt], in_=vtm[:]))
                else:
                    vst = ctx.on("sync", [cpe], lambda e, ct=ct, tt=tt:
                                 e.dma_start(out=v_o[ct, tt // 8, :, 128 * (tt % 8):128 * (tt % 8 + 1)],
                                             in_=vtm[:]))
            row_free = [vst, tr_free]

    def mk_q(e):
        mm = None
        for ft in range(NFT):
            for c in range(4):
                mm = e.matmul(pacc[0:64, 512 * c:512 * (c + 1)],
                              wqr[:, 64 * ft:64 * (ft + 1)],
                              xb[:, T * ft + 512 * c:T * ft + 512 * (c + 1)],
                              start=(ft == 0), stop=(ft == NFT - 1))
        return mm
    qm_ev = ctx.on("tensor", [cload, xl_ev, pacc_free], mk_q)
    qv_ev = ctx.on("vector", [qm_ev, row_free], lambda e: e.tensor_copy(queryT[:], pacc[0:64, :]))
    q8_ev = ctx.on("vector", [qv_ev], lambda e: e.tensor_scalar(q8T[:], pacc[0:64, :], 0.125, None, op0=ALU.mult))
    pacc_free = q8_ev
    if debug:
        ctx.on("sync", [qv_ev], lambda e: e.dma_start(out=q_dbg[:], in_=queryT[:]))

    def mk_rq(e):
        mm = None
        for c in range(4):
            mm = e.matmul(pacc[0:48, 512 * c:512 * (c + 1)], wrs[:], queryT[:, 512 * c:512 * (c + 1)],
                          start=True, stop=True)
        return mm
    ev = ctx.on("tensor", [qv_ev, pacc_free], mk_rq)
    rq_ev = ctx.on("vector", [ev], lambda e: e.tensor_copy(rqT[:], pacc[0:48, :]))
    pacc_free = rq_ev

    def mk_rk(e):
        mm = None
        for c in range(2):
            mm = e.matmul(pacc[0:48, 512 * c:512 * (c + 1)], wrs[:], akT[:, 512 * c:512 * (c + 1)],
                          start=True, stop=True)
        return mm
    ev = ctx.on("tensor", [rq_ev, cload], mk_rk)
    rk_ev = ctx.on("vector", [ev], lambda e: e.tensor_scalar(
        rkT[:], pacc[0:48, 0:1024], 0.14433756729740643, None, op0=ALU.mult))
    pacc_free = rk_ev

    aux_st = None
    avl_free = [wl_free[0], wl_free[1]]
    for half in range(2):
        wt_ev = None
        for ti in range(8):
            tt = 8 * half + ti

            def mk_rs(e, tt=tt):
                mm = None
                for c in range(2):
                    mm = e.matmul(prt[:, 512 * c:512 * (c + 1)],
                                  rqT[:, 128 * tt:128 * (tt + 1)],
                                  rkT[:, 512 * c:512 * (c + 1)], start=True, stop=True)
                for c in range(2):
                    mm = e.matmul(prt[:, 512 * c:512 * (c + 1)],
                                  ones1h[:], lrow[0:1, 512 * c:512 * (c + 1)],
                                  start=False, stop=True)
                return mm
            rs_ev = ctx.on("tensor", [rk_ev, q8_ev, wt_ev, consts], mk_rs)
            if debug and tt == 0:
                dttt = ctx.on("vector", [rs_ev], lambda e: e.tensor_copy(S2[:, 0:1024], prt[:]))
                ctx.on("sync", [dttt], lambda e: e.dma_start(out=rs_dbg[:], in_=S2[:, 0:1024]))
            mx_ev = ctx.on("vector", [rs_ev], lambda e: e.max(out=st8[:], in_=prt[:]))
            mk_ev = ctx.on("vector", [mx_ev], lambda e: e.tensor_scalar(
                S1[:, 0:1024], prt[:], st8[:, 7:8], None, op0=ALU.is_lt))
            if debug and tt == 0:
                ctx.on("sync", [mk_ev], lambda e: [
                    e.dma_start(out=t8_dbg[:], in_=st8[:]),
                    e.dma_start(out=mk_dbg[:], in_=S1[:, 0:1024]),
                ])

            def mk_as(e, tt=tt):
                mm = None
                for c in range(2):
                    mm = e.matmul(prt[:, 512 * c:512 * (c + 1)],
                                  q8T[:, 128 * tt:128 * (tt + 1)],
                                  akT[:, 512 * c:512 * (c + 1)], start=True, stop=True)
                for c in range(2):
                    mm = e.matmul(prt[:, 512 * c:512 * (c + 1)],
                                  ones1h[:], lrow[0:1, 512 * c:512 * (c + 1)],
                                  start=False, stop=True)
                return mm
            as_ev = ctx.on("tensor", [mk_ev, q8_ev, consts], mk_as)
            if debug and tt == 0:
                dtt2 = ctx.on("vector", [as_ev], lambda e: e.tensor_copy(S3[:, 0:1024], prt[:]))
                ctx.on("sync", [dtt2], lambda e: e.dma_start(out=as_dbg[:], in_=S3[:, 0:1024]))
            msk_ev = ctx.on("vector", [as_ev], lambda e: e.scalar_tensor_tensor(
                S2[:, 0:1024], S1[:, 0:1024], -60000.0, prt[:], op0=ALU.mult, op1=ALU.add))
            mxr_ev = ctx.on("vector", [msk_ev], lambda e: e.reduce_max(
                rmx[:], S2[:, 0:1024], axis=AX.X, negate=True))
            if debug and tt == 0:
                ctx.on("sync", [mxr_ev], lambda e: e.dma_start(out=ms_dbg[:], in_=S2[:, 0:1024]))
            exp_ev = ctx.on("scalar", [mxr_ev], lambda e: e.activation(
                S3[:, 0:1024], S2[:, 0:1024], AF.Exp, bias=rmx[:], accum_out=lsm[:]))
            if debug and tt == 0:
                dbb = ctx.on("vector", [exp_ev], lambda e: [
                    e.tensor_copy(st8[:, 0:1], rmx[:]),
                    e.tensor_copy(st8[:, 1:2], lsm[:]),
                ][-1])
                ctx.on("sync", [dbb, exp_ev], lambda e: [
                    e.dma_start(out=ex_dbg[:], in_=S3[:, 0:1024]),
                    e.dma_start(out=ls_dbg[:], in_=st8[:, 0:2]),
                ])
            rl_ev = ctx.on("scalar", [exp_ev], lambda e: [
                e.activation(rl[:], lsm[:], AF.Ln),
                e.activation(rl[:], rl[:], AF.Exp, scale=-1.0),
            ][-1])
            wr_ev = ctx.on("vector", [rl_ev], lambda e: e.tensor_scalar(
                wrow, S3[:, 0:1024], rl[:, 0:1], None, op0=ALU.mult))
            if debug:
                ctx.on("sync", [wr_ev], lambda e, tt=tt: e.dma_start(out=wrt_o[tt], in_=wrow))
            cp_ev = None
            for nt in range(8):
                trev = ctx.on("tensor", [wr_ev, cp_ev], lambda e, nt=nt:
                              e.transpose(ptrk[:], wrow[:, 128 * nt:128 * (nt + 1)], ident[:]))
                cp_ev = ctx.on("vector", [trev], lambda e, nt=nt, ti=ti: e.tensor_copy(
                    wT[:, 1024 * nt + 128 * ti:1024 * nt + 128 * (ti + 1)], ptrk[:]))
            wt_ev = cp_ev
        for ot in range(32):
            i = ot % 2
            avl = ctx.on("gpsimd", [avl_free[i]], lambda e, ot=ot, i=i: e.dma_start(
                out=wbuf[i][:, 0:1024], in_=av_p[ot]))

            def mk_aux(e, ot=ot, i=i, half=half):
                mm = None
                for nt in range(8):
                    for c in range(2):
                        mm = e.matmul(pacc[:, 512 * c:512 * (c + 1)],
                                      wbuf[i][:, 128 * nt:128 * (nt + 1)],
                                      wT[:, 1024 * nt + 512 * c:1024 * nt + 512 * (c + 1)],
                                      start=(nt == 0), stop=(nt == 7))
                return mm
            am_ev = ctx.on("tensor", [avl, wt_ev, pacc_free], mk_aux)
            avl_free[i] = am_ev
            ax_ev = ctx.on("vector", [am_ev, aux_st], lambda e: e.tensor_copy(qkvrow[:, 0:1024], pacc[:, 0:1024]))
            pacc_free = ax_ev
            aux_st = ctx.on("sync", [ax_ev], lambda e, ot=ot, half=half: e.dma_start(
                out=aux_d[128 * ot:128 * (ot + 1), 1024 * half:1024 * (half + 1)], in_=qkvrow[:, 0:1024]))
            if debug:
                aux_st = ctx.on("sync", [aux_st], lambda e, ot=ot, half=half: e.dma_start(
                    out=auxo_o[128 * ot:128 * (ot + 1), 1024 * half:1024 * (half + 1)], in_=qkvrow[:, 0:1024]))

    vb_ev = None
    ab_free = aux_st
    for vt_i in range(2):
        wl = ctx.on("gpsimd", [wl_free[0] if vt_i == 0 else vb_ev], lambda e, vt_i=vt_i:
                    e.dma_start(out=wbuf[0][:], in_=wvd_p[vt_i]))
        mm_last = None
        for ot in range(32):
            al = ctx.on("sync", [ab_free], lambda e, ot=ot: e.dma_start(
                out=wbuf[1][:, 0:2048], in_=aux_d[128 * ot:128 * (ot + 1), :]))

            def mk_vb(e, ot=ot, vt_i=vt_i):
                mm = None
                for c in range(4):
                    mm = e.matmul(pacc[:, 512 * c:512 * (c + 1)],
                                  wbuf[0][:, 128 * ot:128 * (ot + 1)],
                                  wbuf[1][:, 512 * c:512 * (c + 1)],
                                  start=(ot == 0), stop=(ot == 31))
                return mm
            mm_last = ctx.on("tensor", [al, wl, pacc_free if ot == 0 else None, mm_last], mk_vb)
            ab_free = mm_last
        vb_ev = ctx.on("vector", [mm_last], lambda e, vt_i=vt_i: e.tensor_copy(
            wT[:, 2048 * vt_i:2048 * (vt_i + 1)], pacc[:]))
        pacc_free = vb_ev

    a3_st = None
    wl2 = None
    for ot in range(32):
        wl2 = ctx.on("gpsimd", [vb_ev if ot == 0 else None, wl2], lambda e, ot=ot:
                     e.dma_start(out=wvul[:], in_=wvu_p[ot]))
        def mk_a3(e, ot=ot):
            mm = None
            for vt_i in range(2):
                for c in range(4):
                    mm = e.matmul(pacc[:, 512 * c:512 * (c + 1)],
                                  wvul[:, 128 * vt_i:128 * (vt_i + 1)],
                                  wT[:, 2048 * vt_i + 512 * c:2048 * vt_i + 512 * (c + 1)],
                                  start=(vt_i == 0), stop=(vt_i == 1))
            return mm
        mm_ev = ctx.on("tensor", [wl2, vb_ev, pacc_free], mk_a3)
        wl2 = mm_ev
        a3_ev = ctx.on("vector", [mm_ev, a3_st], lambda e: e.tensor_copy(qkvrow[:], pacc[:]))
        pacc_free = a3_ev
        a3_st = ctx.on("sync", [a3_ev], lambda e, ot=ot: e.dma_start(
            out=aux3_o[128 * ot:128 * (ot + 1), :], in_=qkvrow[:]))

    C1 = 0.7978845608028654
    C2 = C1 * 0.044715
    m1_st = None
    for ut in range(8):
        i = ut % 2
        wl = ctx.on("gpsimd", [wl_free[i]], lambda e, ut=ut, i=i:
                    e.dma_start(out=wbuf[i][:], in_=wu1_p[ut]))

        def mk_m1(e, ut=ut, i=i):
            mm = None
            for ft in range(NFT):
                for c in range(4):
                    mm = e.matmul(pacc[:, 512 * c:512 * (c + 1)],
                                  wbuf[i][:, 128 * ft:128 * (ft + 1)],
                                  xb[:, T * ft + 512 * c:T * ft + 512 * (c + 1)],
                                  start=(ft == 0), stop=(ft == NFT - 1))
            return mm
        mm_ev = ctx.on("tensor", [wl, pacc_free], mk_m1)
        wl_free[i] = mm_ev
        e1 = ctx.on("vector", [mm_ev, cload], lambda e, ut=ut: e.tensor_scalar(
            S1[:], pacc[:], bu1[:, ut:ut + 1], None, op0=ALU.add))
        pacc_free = e1
        e2 = ctx.on("scalar", [e1], lambda e: e.activation(S2[:], S1[:], AF.Square))
        e3 = ctx.on("vector", [e2], lambda e: e.tensor_scalar(S2[:], S2[:], C2, C1, op0=ALU.mult, op1=ALU.add))
        e4 = ctx.on("vector", [e3], lambda e: e.tensor_mul(S2[:], S1[:], S2[:]))
        e5 = ctx.on("scalar", [e4], lambda e: e.activation(S3[:], S2[:], AF.Tanh))
        e6 = ctx.on("vector", [e5], lambda e: e.tensor_scalar(S3[:], S3[:], 0.5, 0.5, op0=ALU.mult, op1=ALU.add))
        e7 = ctx.on("vector", [e6, m1_st], lambda e: e.tensor_mul(qkvrow[:], S1[:], S3[:]))
        m1_st = ctx.on("sync", [e7], lambda e, ut=ut: e.dma_start(
            out=m1_d[128 * ut:128 * (ut + 1), :], in_=qkvrow[:]))
        if debug and ut == 0:
            m1_st = ctx.on("sync", [m1_st], lambda e: e.dma_start(out=m1_dbg[:], in_=qkvrow[:]))

    u2_mm = None
    ml_ev = None
    for ut in range(8):
        ml_ev = ctx.on("sync", [(m1_st if ut == 0 else None),
                                (wl_free[0] if ut == 0 else None), u2_mm],
                       lambda e, ut=ut: e.dma_start(
            out=wbuf[0][:, 0:2048], in_=m1_d[128 * ut:128 * (ut + 1), :]))

        def mk_u2(e, ut=ut):
            mm = None
            for tt in range(NTT):
                mm = e.matmul(pst[:, tt:tt + 1], wbuf[0][:, 128 * tt:128 * (tt + 1)],
                              wu2c[:, ut:ut + 1], start=(ut == 0 and tt == 0), stop=(ut == 7))
            return mm
        u2_mm = ctx.on("tensor", [ml_ev, cload], mk_u2)
    if debug:
        du2 = ctx.on("vector", [u2_mm], lambda e: e.tensor_copy(S1[:, 0:16], pst[:, 0:16]))
        ctx.on("sync", [du2], lambda e: e.dma_start(out=u2_dbg[:], in_=S1[:, 0:16]))
    lr_ev = ctx.on("vector", [u2_mm], lambda e: e.tensor_scalar(
        lrn[:], pst[:, 0:16], bu2[:, 0:1], None, op0=ALU.add))
    lr_st = ctx.on("sync", [lr_ev], lambda e: e.dma_start(out=lrn_o[:], in_=lrn[:]))

    ctx.wait_only("gpsimd", [lr_st, a3_st, aux_st, m1_st])
    if standalone:
        es.close()
    return nc


def _prep_l3(inp):
    if "wqkv_p" in _PC:
        return
    _PC["wqkv_p"] = pack_lhsT(np.ascontiguousarray(inp["L1_attn_w_qkv"].T))
    _PC["bqkv_c"] = np.ascontiguousarray(inp["L1_attn_b_qkv"].reshape(48, 128).T)
    _PC["wu1_p"] = pack_lhsT(np.ascontiguousarray(inp["L1_ffn_Wu1"].T))
    _PC["bu1_c"] = np.ascontiguousarray(inp["L1_ffn_bu1"].reshape(8, 128).T)
    _PC["wqr_pk"] = np.ascontiguousarray(
        inp["L1_ffn_Wq"].T.reshape(32, 128, 64).transpose(1, 0, 2).reshape(128, 2048))
    _PC["wr_s"] = np.ascontiguousarray(inp["L1_ffn_Wr"].T)
    _PC["akT"] = np.ascontiguousarray(inp["L1_ffn_aux_keys"].T)
    _PC["logrel"] = np.ascontiguousarray(inp["L1_ffn_log_rel"].reshape(1, 1024))
    _PC["av_p"] = pack_lhsT(np.ascontiguousarray(inp["L1_ffn_aux_values"]))
    _PC["wvd_p"] = pack_lhsT(np.ascontiguousarray(inp["L1_ffn_Wvd"].T))
    _PC["wvu_p"] = pack_lhsT(np.ascontiguousarray(inp["L1_ffn_Wvu"].T))
    _PC["wu2_s"] = np.ascontiguousarray(inp["L1_ffn_Wu2"][0].reshape(8, 128).T)
    _PC["bu2_c"] = np.ascontiguousarray(np.broadcast_to(
        inp["L1_ffn_bu2"].reshape(1, 1), (128, 1)))


def l3_in_map(inp, core, xn1T):
    _prep_l3(inp)
    return {
        "xn1T": xn1T,
        "wqkv_p": _PC["wqkv_p"], "bqkv_c": _PC["bqkv_c"],
        "wu1_p": _PC["wu1_p"], "bu1_c": _PC["bu1_c"],
        "wqr_pk": _PC["wqr_pk"], "wr_s": _PC["wr_s"], "akT": _PC["akT"],
        "logrel": _PC["logrel"], "av_p": _PC["av_p"], "wvd_p": _PC["wvd_p"],
        "wvu_p": _PC["wvu_p"], "wu2_s": _PC["wu2_s"], "bu2_c": _PC["bu2_c"],
        "ident": _PC["ident"],
    }



def build_launch4(debug=False, n_heads=8, do_pass1=True, do_pass2=True, do_pv=True, do_l=True, do_diag=True, do_exp=True, do_accum=True, do_fin=True, fin_mode='all', nfb=None, F=None):
    standalone = F is None
    if standalone:
        nc = bass.Bass("TRN2", target_bir_lowering=False)
        D = lambda name, shape, dtype, kind: nc.dram_tensor(name, shape, dtype, kind=kind).ap()
    else:
        nc = F.nc
        D = F.dram
    qT_d = D("qT", [2048, T], BF16, "ExternalInput")
    ko_d = D("kT", [2048, T], BF16, "ExternalInput")
    if standalone:
        vo_d = D("vt", [16, 16, 128, 128], BF16, "ExternalInput")
    else:
        vo_d = F.overrides["v2c"]
    if standalone:
        kf_d = D("k_full", [2048, 8192], BF16, "ExternalInput")
    else:
        kf_d = F.overrides["k_gath"]
    if standalone:
        vf_d = D("v_full2", [16, 8, 128, 1024], BF16, "ExternalInput")
    else:
        vf_d = F.overrides["v_gath"]
    offs1_d = D("offs1", [128, 256], F32, "ExternalInput")
    sup2_d = D("sup2", [128, 256], F32, "ExternalInput")
    tri1_d = D("tri1", [4, 128, 512], F32, "ExternalInput")
    tri2_d = D("tri2", [4, 128, 512], F32, "ExternalInput")
    aux3_d = D("aux3T", [4096, T], BF16, "ExternalInput")
    xn1_d = D("xn1T", [4096, T], BF16, "ExternalInput")
    x1_d = D("x1T", [4096, T], F32, "ExternalInput")
    lgv_d = D("lgv", [128, 16], F32, "ExternalInput")
    if standalone:
        lgp8_d = D("lgp8", [1, 8], F32, "ExternalInput")
    else:
        lgp8_d = F.overrides["lgp_out"]
    lrn_d = D("learned", [128, 16], F32, "ExternalInput")
    gw1_d = D("gw1_c", [128, 1], F32, "ExternalInput")
    gb_d = D("gb_c", [128, 1], F32, "ExternalInput")
    wo1_p = D("wo1_p", [32, 128, 2048], F32, "ExternalInput")
    bo1_c = D("bo1_c", [128, 32], F32, "ExternalInput")
    ident_d = D("ident", [128, 128], BF16, "ExternalInput")

    out_o = D("outT", [4096, T], F32, "ExternalOutput")
    if debug:
        vals_o = D("vals_dbg", [2048, T], BF16, "ExternalOutput")
        gate_o = D("gate_dbg", [128, 16], F32, "ExternalOutput")
        s1_o = D("s1T", [4096, T], F32, "ExternalOutput")
        g1_o = D("g1T", [4096, T], F32, "ExternalOutput")
        mrun_o = D("mrun_dbg", [128, 16], F32, "ExternalOutput")
        l_o = D("l_dbg", [1, 512], F32, "ExternalOutput")
        sc1_o = D("sc1_dbg", [1, 16], F32, "ExternalOutput")
        gg_o = D("gg_dbg", [128, 16], F32, "ExternalOutput")

    es = ExitStack() if standalone else F.es
    _pfx = "" if standalone else F.prefix
    S = lambda n, *a: es.enter_context(nc.sbuf_tensor(_pfx + n, *a))
    P = lambda n, *a: es.enter_context(nc.psum_tensor(_pfx + n, *a))
    valsT = S("valsT_sb", [128, 16 * T], BF16)
    qh = S("qh", [128, 2 * T], BF16)
    ko = S("ko", [128, 2 * T], BF16)
    kf = [S(f"kf{i}", [128, 2048], BF16) for i in range(2)]
    vt = [S(f"vt{i}", [128, 2048], BF16) for i in range(2)]
    m_run = S("m_run", [128, 16], F32)
    m_bc = S("m_bc", [128, 512], F32)
    S1 = S("S1f", [128, T], F32)
    Pt = S("Pt", [128, 1024], BF16)
    mrow = S("mrow", [1, 512], F32)
    lrow = S("lrow", [1, 512], F32)
    tri1 = S("tri1s", [128, 4 * 512], F32)
    tri2 = S("tri2s", [128, 4 * 512], F32)
    offs1 = S("offs1s", [128, 256], F32)
    sup2 = S("sup2s", [128, 256], F32)
    lgv = S("lgvs", [128, 16], F32)
    lrn = S("lrns", [128, 16], F32)
    gg = S("ggs", [128, 16], F32)
    gh = S("ghs", [128, 16], F32)
    grgm = S("grgm", [128, 1], F32)
    gw1 = S("gw1s", [128, 1], F32)
    gb = S("gbs", [128, 1], F32)
    grow = S("grow", [1, T], F32)
    gate_bch = S("gate_bch", [128, T], BF16)
    wo_sb = [S(f"wo_sb{i}", [128, 2048], BF16) for i in range(2)]
    a3b = S("a3b", [128, T], BF16)
    xnb = S("xnb", [128, T], BF16)
    t1h = S("t1h", [128, T], BF16)
    x1b = S("x1b", [128, T], F32)
    S2 = S("S2f", [128, T], F32)
    g1f = S("g1f", [128, T], F32)
    onesc_h = S("onesc_h", [128, 1], BF16)
    ones1_f = S("ones1_f", [1, 128], F32)
    epsb6 = S("epsb6", [1, 1], F32)
    bo1 = S("bo1s", [128, 32], F32)
    sc1 = S("sc1", [1, 16], F32)
    ident = S("identS", [128, 128], BF16)
    identf = S("identfS", [128, 128], F32)

    pA = P("pA", [128, 1024], F32)
    pO = P("pO", [128, 1024], F32)
    pL = P("pL", [128, 512], F32)
    pB = P("pB", [128, 512], F32)

    ctx = Ctx(nc, es) if standalone else F.ctx

    def mk_consts(e):
        e.memset(onesc_h[:], 1.0)
        e.memset(ones1_f[:], 1.0)
        last = e.memset(epsb6[:], 1e-6)
        if n_heads < 8:
            last = e.memset(valsT[:], 0.0)
        return last
    consts = ctx.on("vector", [], mk_consts)
    cload = ctx.on("gpsimd", [], lambda e: [
        e.dma_start(out=ident[:], in_=ident_d[:]),
        e.dma_start(out=tri1[:].rearrange("p (r c) -> p r c", r=4), in_=tri1_d[:].rearrange("r p c -> p r c")),
        e.dma_start(out=tri2[:].rearrange("p (r c) -> p r c", r=4), in_=tri2_d[:].rearrange("r p c -> p r c")),
        e.dma_start(out=offs1[:], in_=offs1_d[:]),
        e.dma_start(out=sup2[:], in_=sup2_d[:]),
        e.dma_start(out=lgv[:], in_=lgv_d[:]),
        e.dma_start(out=lrn[:], in_=lrn_d[:]),
        e.dma_start(out=gw1[:], in_=gw1_d[:]),
        e.dma_start(out=gb[:], in_=gb_d[:]),
        e.dma_start(out=bo1[:], in_=bo1_c[:]),
        e.dma_start(out=sc1[0:1, 0:8], in_=(lgp8_d[:] if standalone else lgp8_d[0:1, 0:8])),
    ])
    cload = ctx.on("vector", [cload], lambda e: e.tensor_copy(identf[:], ident[:]))

    if standalone:
        ev = ctx.on("vector", [cload], lambda e: e.reduce_sum(sc1[0:1, 8:9], sc1[0:1, 0:8], axis=AX.X))
    else:
        ev = ctx.on("vector", [cload], lambda e: e.tensor_copy(sc1[0:1, 8:9], sc1[0:1, 0:1]))
    ev = ctx.on("vector", [ev], lambda e: e.tensor_scalar(
        sc1[0:1, 9:10], sc1[0:1, 8:9], 1.0 / 16384.0, None, op0=ALU.mult))
    ev = ctx.on("vector", [ev], lambda e: e.tensor_scalar(
        sc1[0:1, 10:11], sc1[0:1, 9:10], epsb6[0:1, 0:1], None, op0=ALU.add))
    ev = ctx.on("scalar", [ev], lambda e: e.activation(sc1[0:1, 11:12], sc1[0:1, 10:11], AF.Ln))
    ev = ctx.on("scalar", [ev], lambda e: e.activation(sc1[0:1, 12:13], sc1[0:1, 11:12], AF.Exp, scale=-1.0))
    ev = ctx.on("tensor", [ev, consts], lambda e: e.matmul(
        pB[:, 0:1], ones1_f[:], sc1[0:1, 12:13], start=True, stop=True))
    ev = ctx.on("vector", [ev], lambda e: e.tensor_copy(grgm[:], pB[:, 0:1]))
    ev = ctx.on("vector", [ev], lambda e: e.tensor_scalar(
        gg[:], lgv[:], grgm[:, 0:1], 0.5, op0=ALU.mult, op1=ALU.mult))
    sg_ev = ctx.on("scalar", [cload], lambda e: e.activation(gh[:], lrn[:], AF.Sigmoid))
    ev = ctx.on("vector", [ev, sg_ev], lambda e: e.scalar_tensor_tensor(
        gg[:], gh[:], 2.5, gg[:], op0=ALU.mult, op1=ALU.add))
    ev = ctx.on("vector", [ev], lambda e: e.tensor_scalar(
        gg[:], gg[:], 0.0, 5.0, op0=ALU.max, op1=ALU.min))
    ev = ctx.on("vector", [ev], lambda e: e.tensor_scalar(
        gg[:], gg[:], 0.5, 1.0 / 1.5, op0=ALU.subtract, op1=ALU.mult))
    ev = ctx.on("vector", [ev], lambda e: e.tensor_scalar(
        gg[:], gg[:], 0.0, 1.0, op0=ALU.max, op1=ALU.min))
    if debug:
        dgg = ctx.on("sync", [ev], lambda e: e.dma_start(out=gg_o[:], in_=gg[:]))
        ev = [ev, dgg]
    ev = ctx.on("scalar", [ev], lambda e: e.activation(gh[:], gg[:], AF.Sigmoid,
                                                       bias=gb[:, 0:1], scale=gw1[:, 0:1]))
    if debug:
        dsc = ctx.on("sync", [ev], lambda e: e.dma_start(out=sc1_o[:], in_=sc1[:]))
    gate_ev = ctx.on("vector", [ev], lambda e: e.tensor_scalar(
        gg[:], gh[:], 0.05, None, op0=ALU.is_ge))
    gate_ev = ctx.on("vector", [gate_ev], lambda e: e.tensor_mul(gg[:], gg[:], gh[:]))
    if debug:
        ctx.on("sync", [gate_ev], lambda e: e.dma_start(out=gate_o[:], in_=gg[:]))
    gr_ev = gate_ev
    for tt in range(NTT):
        tev = ctx.on("tensor", [gr_ev, cload], lambda e, tt=tt:
                     e.transpose(pB[0:1, 0:128], gg[:, tt:tt + 1], identf[:]))
        gr_ev = ctx.on("vector", [tev], lambda e, tt=tt:
                       e.tensor_copy(grow[:, 128 * tt:128 * (tt + 1)], pB[0:1, 0:128]))
    gb_mm = None
    for c in range(4):
        gb_mm = ctx.on("tensor", [gr_ev, gb_mm], lambda e, c=c: e.matmul(
            pB[:, 0:512], ones1_f[:],
            grow[:, 512 * c:512 * (c + 1)], start=True, stop=True))
        gb_mm = ctx.on("vector", [gb_mm], lambda e, c=c: e.tensor_copy(
            gate_bch[:, 512 * c:512 * (c + 1)], pB[:, 0:512]))
    gatebc_ev = gb_mm

    vals_ev = None
    psD_free = [gb_mm, gb_mm]
    pO_free = None
    kq_ld = None
    for h in range(n_heads):
        kq_ld = ctx.on("sync", [kq_ld, vals_ev], lambda e, h=h: [
            e.dma_start(out=qh[:, 0:T], in_=qT_d[256 * h:256 * h + 128, :]),
            e.dma_start(out=qh[:, T:2 * T], in_=qT_d[256 * h + 128:256 * h + 256, :]),
            e.dma_start(out=ko[:, 0:T], in_=ko_d[256 * h:256 * h + 128, :]),
            e.dma_start(out=ko[:, T:2 * T], in_=ko_d[256 * h + 128:256 * h + 256, :]),
        ])
        p1ev = None
        if not do_pass1:
            p1ev = ctx.on("vector", [kq_ld], lambda e: e.memset(m_run[:], 0.0))
        for qt in (range(NTT) if do_pass1 else []):
            oc_rel = qt // 4
            def mk_diag(e, qt=qt, oc=oc_rel):
                mm = None
                for ct in range(2):
                    mm = e.matmul(pA[:, 0:512],
                                  qh[:, T * ct + 128 * qt:T * ct + 128 * (qt + 1)],
                                  ko[:, T * ct + 512 * oc:T * ct + 512 * (oc + 1)],
                                  start=(ct == 0), stop=(ct == 1))
                return mm
            dmm = ctx.on("tensor", [kq_ld, p1ev, vals_ev], mk_diag)
            aev = ctx.on("vector", [dmm, cload], lambda e, qt=qt: e.tensor_add(
                S1[:, 0:512], pA[:, 0:512], tri1[:, 512 * (qt % 4):512 * (qt % 4 + 1)]))
            p1ev = ctx.on("vector", [aev], lambda e, qt=qt: e.reduce_max(
                m_run[:, qt:qt + 1], S1[:, 0:512], axis=AX.X))
        kl1 = None
        p1bank = [p1ev, p1ev]
        for fc in (range(16) if do_pass1 else []):
            if standalone:
                kl1 = ctx.on("sync", [kl1, p1ev], lambda e, h=h, fc=fc: [
                    e.dma_start(out=kf[fc % 2][:, 0:512], in_=kf_d[256 * h:256 * h + 128, 512 * fc:512 * (fc + 1)]),
                    e.dma_start(out=kf[fc % 2][:, 1024:1536], in_=kf_d[256 * h + 128:256 * h + 256, 512 * fc:512 * (fc + 1)]),
                ])
            else:
                blk, loc = fc // 4, 512 * (fc % 4)
                kl1 = ctx.on("sync", [kl1, p1ev], lambda e, h=h, fc=fc, blk=blk, loc=loc: [
                    e.dma_start(out=kf[fc % 2][:, 0:512], in_=kf_d[blk, 256 * h:256 * h + 128, loc:loc + 512]),
                    e.dma_start(out=kf[fc % 2][:, 1024:1536], in_=kf_d[blk, 256 * h + 128:256 * h + 256, loc:loc + 512]),
                ])
            for qt in range(NTT):
                bank = qt % 2
                def mk_p1(e, qt=qt, fc=fc, bank=bank):
                    mm = None
                    for ct in range(2):
                        mm = e.matmul(pA[:, 512 * bank:512 * (bank + 1)],
                                      qh[:, T * ct + 128 * qt:T * ct + 128 * (qt + 1)],
                                      kf[fc % 2][:, 1024 * ct:1024 * ct + 512],
                                      start=(ct == 0), stop=(ct == 1))
                    return mm
                mm1 = ctx.on("tensor", [kl1, p1bank[qt % 2]], mk_p1)
                red = ctx.on("vector", [mm1], lambda e, qt=qt: e.reduce_max(
                    m_bc[:, 2 + qt % 2:3 + qt % 2], pA[:, 512 * (qt % 2):512 * (qt % 2 + 1)], axis=AX.X))
                p1bank[qt % 2] = red
                p1ev = ctx.on("vector", [red], lambda e, qt=qt, fc=fc: e.scalar_tensor_tensor(
                    m_run[:, qt:qt + 1], m_bc[:, 2 + qt % 2:3 + qt % 2], offs1[:, 16 * qt + fc:16 * qt + fc + 1],
                    m_run[:, qt:qt + 1], op0=ALU.add, op1=ALU.max))
        if debug and h == 0:
            ctx.on("sync", [p1ev], lambda e: e.dma_start(out=mrun_o[:], in_=m_run[:]))
        if not do_pass2:
            vals_ev = ctx.on("vector", [p1ev], lambda e: e.memset(valsT[:], 0.01))
            continue
        for qc in range(4):
            mr_ev = [p1ev, vals_ev]
            for i in range(4):
                tev = ctx.on("tensor", [mr_ev, cload], lambda e, i=i, qc=qc:
                             e.transpose(pB[0:1, 0:128], m_run[:, 4 * qc + i:4 * qc + i + 1], identf[:]))
                mr_ev = ctx.on("vector", [tev], lambda e, i=i: e.tensor_copy(
                    mrow[:, 128 * i:128 * (i + 1)], pB[0:1, 0:128]))
            bmm = ctx.on("tensor", [mr_ev], lambda e: e.matmul(
                pB[:, 0:512], ones1_f[:], mrow[:], start=True, stop=True))
            mbc_ev = ctx.on("vector", [bmm], lambda e: e.tensor_copy(m_bc[:], pB[:, 0:512]))

            ldk = None
            ldv = None
            pv_mm = None
            pv_par = [None, None]
            exp_par = [None, None]
            first = True
            for fb in range(nfb if nfb is not None else (68 if do_diag else 64)):
                diag = fb >= 64
                if not diag:
                    if fb % 8 == 0:
                        if standalone:
                            ldk = ctx.on("sync", [ldk, mbc_ev if fb == 0 else None,
                                          psD_free[0], psD_free[1]],
                                         lambda e, h=h, fb=fb: [
                                e.dma_start(out=kf[(fb // 8) % 2][:, 0:1024],
                                            in_=kf_d[256 * h:256 * h + 128, 128 * fb:128 * fb + 1024]),
                                e.dma_start(out=kf[(fb // 8) % 2][:, 1024:2048],
                                            in_=kf_d[256 * h + 128:256 * h + 256, 128 * fb:128 * fb + 1024]),
                            ])
                        else:
                            blk, loc = fb // 16, 128 * (fb % 16)
                            ldk = ctx.on("sync", [ldk, mbc_ev if fb == 0 else None,
                                          psD_free[0], psD_free[1]],
                                         lambda e, h=h, fb=fb, blk=blk, loc=loc: [
                                e.dma_start(out=kf[(fb // 8) % 2][:, 0:1024],
                                            in_=kf_d[blk, 256 * h:256 * h + 128, loc:loc + 1024]),
                                e.dma_start(out=kf[(fb // 8) % 2][:, 1024:2048],
                                            in_=kf_d[blk, 256 * h + 128:256 * h + 256, loc:loc + 1024]),
                            ])
                        if standalone:
                            ldv = ctx.on("gpsimd", [ldv, mbc_ev if fb == 0 else None, pv_mm],
                                         lambda e, h=h, fb=fb: [
                                e.dma_start(out=vt[(fb // 8) % 2][:, 0:1024], in_=vf_d[2 * h, fb // 8]),
                                e.dma_start(out=vt[(fb // 8) % 2][:, 1024:2048], in_=vf_d[2 * h + 1, fb // 8]),
                            ])
                        else:
                            g = fb // 8
                            blk, gl = g // 2, g % 2
                            ldv = ctx.on("gpsimd", [ldv, mbc_ev if fb == 0 else None, pv_mm],
                                         lambda e, h=h, fb=fb, blk=blk, gl=gl: [
                                e.dma_start(out=vt[(fb // 8) % 2][:, 0:1024], in_=vf_d[blk, 2 * h, gl]),
                                e.dma_start(out=vt[(fb // 8) % 2][:, 1024:2048], in_=vf_d[blk, 2 * h + 1, gl]),
                            ])
                    kbuf = kf[(fb // 8) % 2]
                    vbuf = vt[(fb // 8) % 2]
                    koff = 128 * (fb % 8)
                    def mk_sc(e, qc=qc, kbuf=kbuf, koff=koff, bank=fb % 2):
                        mm = None
                        for ct in range(2):
                            mm = e.matmul(pA[:, 512 * bank:512 * (bank + 1)],
                                          kbuf[:, 1024 * ct + koff:1024 * ct + koff + 128],
                                          qh[:, T * ct + 512 * qc:T * ct + 512 * (qc + 1)],
                                          start=(ct == 0), stop=(ct == 1))
                        return mm
                else:
                    fd = fb - 64
                    def mk_sc(e, qc=qc, fd=fd, bank=fb % 2):
                        mm = None
                        for ct in range(2):
                            mm = e.matmul(pA[:, 512 * bank:512 * (bank + 1)],
                                          ko[:, T * ct + 512 * qc + 128 * fd:T * ct + 512 * qc + 128 * (fd + 1)],
                                          qh[:, T * ct + 512 * qc:T * ct + 512 * (qc + 1)],
                                          start=(ct == 0), stop=(ct == 1))
                        return mm
                par = fb % 2
                sc_mm = ctx.on("tensor", [ldk if not diag else kq_ld, mbc_ev, psD_free[par]], mk_sc)
                sub_ev = ctx.on("vector", [sc_mm, exp_par[par]], lambda e, par=par: e.tensor_sub(
                    S1[:, 512 * par:512 * (par + 1)], pA[:, 512 * par:512 * (par + 1)], m_bc[:]))
                psD_free[par] = sub_ev
                if diag:
                    fd = fb - 64
                    sub_ev = ctx.on("vector", [sub_ev, cload], lambda e, fd=fd, par=par: e.tensor_add(
                        S1[:, 512 * par:512 * (par + 1)], S1[:, 512 * par:512 * (par + 1)],
                        tri2[:, 512 * fd:512 * (fd + 1)]))
                    exp_ev = ctx.on("scalar", [sub_ev, pv_par[par]], lambda e, par=par: e.activation(
                        Pt[:, 512 * par:512 * (par + 1)], S1[:, 512 * par:512 * (par + 1)], AF.Exp))
                else:
                    exp_ev = ctx.on("scalar", [sub_ev, cload, pv_par[par]], lambda e, qc=qc, fb=fb, par=par: e.activation(
                        Pt[:, 512 * par:512 * (par + 1)], S1[:, 512 * par:512 * (par + 1)],
                        AF.Exp, bias=sup2[:, 64 * qc + fb:64 * qc + fb + 1]))
                exp_par[par] = exp_ev

                def mk_pv(e, fb=fb, h=h, first=first, diag=diag):
                    if not do_pv:
                        return e.matmul(pL[0:1, 0:512], onesc_h[:], Pt[:], start=first, stop=(fb == (67 if do_diag else 63)))
                    if diag:
                        fd = fb - 64
                        v0 = vt[0][:, 0:128]
                        v1 = vt[0][:, 1024:1152]
                    else:
                        vbuf = vt[(fb // 8) % 2]
                        v0 = vbuf[:, 128 * (fb % 8):128 * (fb % 8 + 1)]
                        v1 = vbuf[:, 1024 + 128 * (fb % 8):1024 + 128 * (fb % 8 + 1)]
                    st_ = first if do_accum else True
                    last_fb = (nfb - 1) if nfb is not None else (67 if do_diag else 63)
                    sp_ = (fb == last_fb) if do_accum else True
                    Ppar = Pt[:, 512 * (fb % 2):512 * (fb % 2 + 1)]
                    mm = e.matmul(pO[:, 0:512], v0, Ppar, start=st_, stop=sp_)
                    mm = e.matmul(pO[:, 512:1024], v1, Ppar, start=st_, stop=sp_)
                    if do_l:
                        mm = e.matmul(pL[0:1, 0:512], onesc_h[:], Ppar, start=st_, stop=sp_)
                    return mm
                if diag:
                    fd = fb - 64
                    if standalone:
                        ldvo = ctx.on("gpsimd", [pv_mm, exp_ev], lambda e, h=h, qc=qc, fd=fd: [
                            e.dma_start(out=vt[0][:, 0:128], in_=vo_d[2 * h, 4 * qc + fd]),
                            e.dma_start(out=vt[0][:, 1024:1152], in_=vo_d[2 * h + 1, 4 * qc + fd]),
                        ])
                    else:
                        ti = 4 * qc + fd
                        ldvo = ctx.on("gpsimd", [pv_mm, exp_ev], lambda e, h=h, ti=ti: [
                            e.dma_start(out=vt[0][:, 0:128],
                                        in_=vo_d[2 * h, ti // 8, :, 128 * (ti % 8):128 * (ti % 8 + 1)]),
                            e.dma_start(out=vt[0][:, 1024:1152],
                                        in_=vo_d[2 * h + 1, ti // 8, :, 128 * (ti % 8):128 * (ti % 8 + 1)]),
                        ])
                    pv_mm = ctx.on("tensor", [exp_ev, ldvo, consts], mk_pv)
                else:
                    pv_mm = ctx.on("tensor", [exp_ev, ldv, consts, pO_free if first else None], mk_pv)
                pv_par[par] = pv_mm
                first = False
            if not do_fin:
                vals_ev = ctx.on("vector", [pv_mm], lambda e, h=h, qc=qc: e.memset(
                    valsT[:, T * (2 * h) + 512 * qc:T * (2 * h) + 512 * (qc + 1)], 0.01))
                vals_ev = ctx.on("vector", [vals_ev], lambda e, h=h, qc=qc: e.memset(
                    valsT[:, T * (2 * h + 1) + 512 * qc:T * (2 * h + 1) + 512 * (qc + 1)], 0.01))
                pO_free = vals_ev
                continue
            lv_ev = ctx.on("vector", [pv_mm], lambda e: e.tensor_copy(lrow[:], pL[0:1, 0:512]))
            if debug and h == 0 and qc == 0:
                ctx.on("sync", [lv_ev], lambda e: e.dma_start(out=l_o[:], in_=lrow[:]))
            if fin_mode == "nolnexp":
                rl_ev = ctx.on("vector", [lv_ev], lambda e: e.tensor_scalar(
                    lrow[:], lrow[:], 1.0, None, op0=ALU.mult))
            elif fin_mode == "recipdve":
                rl_ev = ctx.on("vector", [lv_ev], lambda e: e.reciprocal(lrow[:], lrow[:]))
            else:
                rl_ev = ctx.on("scalar", [lv_ev], lambda e: e.activation(lrow[:], lrow[:], AF.Ln))
                rl_ev = ctx.on("scalar", [rl_ev], lambda e: e.activation(lrow[:], lrow[:], AF.Exp, scale=-1.0))
            rb_mm = ctx.on("tensor", [rl_ev], lambda e: e.matmul(
                pB[:, 0:512], ones1_f[:], lrow[:], start=True, stop=True))
            rb_ev = ctx.on("vector", [rb_mm], lambda e: e.tensor_copy(S1[:, 0:512], pB[:, 0:512]))
            if fin_mode == "nomul":
                v0_ev = ctx.on("vector", [rb_ev], lambda e, h=h, qc=qc: e.memset(
                    valsT[:, T * (2 * h) + 512 * qc:T * (2 * h) + 512 * (qc + 1)], 0.01))
                v1_ev = ctx.on("vector", [v0_ev], lambda e, h=h, qc=qc: e.memset(
                    valsT[:, T * (2 * h + 1) + 512 * qc:T * (2 * h + 1) + 512 * (qc + 1)], 0.01))
            else:
                v0_ev = ctx.on("vector", [rb_ev], lambda e, h=h, qc=qc: e.tensor_mul(
                    valsT[:, T * (2 * h) + 512 * qc:T * (2 * h) + 512 * (qc + 1)],
                    pO[:, 0:512], S1[:, 0:512]))
                v1_ev = ctx.on("vector", [v0_ev], lambda e, h=h, qc=qc: e.tensor_mul(
                    valsT[:, T * (2 * h + 1) + 512 * qc:T * (2 * h + 1) + 512 * (qc + 1)],
                    pO[:, 512:1024], S1[:, 0:512]))
            pO_free = v1_ev
            vals_ev = v1_ev

    if debug:
        ctx.on("sync", [vals_ev], lambda e: [
            e.dma_start(out=vals_o[128 * ct:128 * (ct + 1), :], in_=valsT[:, T * ct:T * (ct + 1)])
            for ct in range(16)])

    st_ev = None
    s1free = [[vals_ev], [vals_ev]]
    wl_free = [None, None]
    ld_free = None
    ev_free = None
    for ot in range(32):
        i = ot % 2
        wl = ctx.on("gpsimd", [wl_free[i]], lambda e, ot=ot, i=i: e.dma_start(out=wo_sb[i][:], in_=wo1_p[ot]))
        ldx = ctx.on("sync", [ld_free], lambda e, ot=ot: [
            e.dma_start(out=a3b[:], in_=aux3_d[128 * ot:128 * (ot + 1), :]),
            e.dma_start(out=xnb[:], in_=xn1_d[128 * ot:128 * (ot + 1), :]),
            e.dma_start(out=x1b[:], in_=x1_d[128 * ot:128 * (ot + 1), :]),
        ])
        halves_ev = []
        for half in range(2):
            def mk_s1(e, ot=ot, i=i, half=half):
                mm = None
                for ct in range(16):
                    for c in range(2):
                        mm = e.matmul(pA[:, 512 * c:512 * (c + 1)],
                                      wo_sb[i][:, 128 * ct:128 * (ct + 1)],
                                      valsT[:, T * ct + 1024 * half + 512 * c:T * ct + 1024 * half + 512 * (c + 1)],
                                      start=(ct == 0), stop=(ct == 15))
                return mm
            mm_ev = ctx.on("tensor", [wl, vals_ev, ev_free if half == 0 and ot > 0 else None,
                                      halves_ev[0] if half == 1 else None], mk_s1)
            e1 = ctx.on("vector", [mm_ev, cload, st_ev], lambda e, ot=ot, half=half: e.tensor_scalar(
                S2[:, 1024 * half:1024 * (half + 1)], pA[:, 0:1024], bo1[:, ot:ot + 1], None, op0=ALU.add))
            halves_ev.append(e1)
        s1_ev = [halves_ev[0], halves_ev[1]]
        ev1 = ctx.on("vector", [ldx, gatebc_ev, st_ev], lambda e: e.tensor_mul(t1h[:], a3b[:], gate_bch[:]))
        ev2 = ctx.on("vector", [ev1], lambda e: e.tensor_add(t1h[:], xnb[:], t1h[:]))
        ev3 = ctx.on("scalar", [ev2], lambda e: e.activation(g1f[:], t1h[:], AF.Silu))
        dsg = None
        if debug:
            dsg = ctx.on("sync", [ev3, s1_ev], lambda e, ot=ot: [
                e.dma_start(out=s1_o[128 * ot:128 * (ot + 1), :], in_=S2[:]),
                e.dma_start(out=g1_o[128 * ot:128 * (ot + 1), :], in_=g1f[:]),
            ])
        ev4 = ctx.on("vector", [ev3, s1_ev, dsg if debug else None], lambda e: e.tensor_mul(S2[:], S2[:], g1f[:]))
        ev5 = ctx.on("vector", [ev4], lambda e: e.tensor_add(S2[:], S2[:], x1b[:]))
        ev_free = ev5
        ld_free = ev5
        if debug:
            ev5 = [ev5, dsg]
        st_ev = ctx.on("sync", [ev5], lambda e, ot=ot: e.dma_start(
            out=out_o[128 * ot:128 * (ot + 1), :], in_=S2[:]))

    ctx.wait_only("gpsimd", [st_ev])
    if standalone:
        es.close()
    return nc


def _prep_l4(inp):
    if "wo1_p" in _PC:
        return
    _PC["wo1_p"] = pack_lhsT(np.ascontiguousarray(inp["L1_attn_w_o"].T))
    _PC["bo1_c"] = np.ascontiguousarray(inp["L1_attn_b_o"].reshape(32, 128).T)
    _PC["gw1_c"] = np.ascontiguousarray(np.broadcast_to(
        np.asarray(inp["L1_ffn_gate_w1"], np.float32).reshape(1, 1), (128, 1)))
    _PC["gb_c"] = np.ascontiguousarray(np.broadcast_to(
        np.asarray(inp["L1_ffn_gate_bias"], np.float32).reshape(1, 1), (128, 1)))
    p = np.arange(128)[:, None]
    cc = np.arange(512)[None, :]
    tri1 = np.zeros((4, 128, 512), np.float32)
    tri2 = np.zeros((4, 128, 512), np.float32)
    for r in range(4):
        tri1[r] = np.where(cc <= 128 * r + p, 0.0, -1e9)
        tri2[r] = np.where(128 * r + p <= cc, 0.0, -1e9)
    _PC["tri1"] = tri1
    _PC["tri2"] = tri2


def l4_suppr(core):
    j = core % 4
    offs1 = np.zeros((128, 256), np.float32)
    for qt in range(16):
        oc = 4 * j + qt // 4
        for fc in range(16):
            offs1[:, 16 * qt + fc] = 0.0 if fc < oc else -1e9
    sup2 = np.zeros((128, 256), np.float32)
    for qc in range(4):
        own = 4 * (4 * j + qc)
        for fb in range(64):
            sup2[:, 64 * qc + fb] = 0.0 if fb < own else -1e9
    return offs1, sup2


def l4_in_map(inp, core, r2, r3, kfull_b, vfull_b, lgp8):
    _prep_l4(inp)
    offs1, sup2 = l4_suppr(core)
    return {
        "qT": r3["qT"], "kT": r3["kT"], "vt": r3["vt"],
        "k_full": kfull_b, "v_full2": vfull_b,
        "offs1": offs1, "sup2": sup2,
        "tri1": _PC["tri1"], "tri2": _PC["tri2"],
        "aux3T": r3["aux3T"], "xn1T": r2["xn1T"], "x1T": r2["x1T"],
        "lgv": r2["lgv"], "lgp8": lgp8, "learned": r3["learned"],
        "gw1_c": _PC["gw1_c"], "gb_c": _PC["gb_c"],
        "wo1_p": _PC["wo1_p"], "bo1_c": _PC["bo1_c"],
        "ident": _PC["ident"],
    }



def build_fused():
    nc = bass.Bass("TRN2", target_bir_lowering=False)
    es = ExitStack()
    ctx = Ctx(nc, es)
    I = lambda n, s, d: nc.dram_tensor(n, s, d, kind="Internal").ap()

    qphi_d = I("i_qphi", [4096, T], BF16)
    rot_d = I("i_rot", [4096, T], BF16)
    w2r_d = I("i_w2r", [4096, 4096], BF16)
    kvin = I("i_kvin", [1, 132096], F32)
    kvout = I("i_kvout", [1, 132096], F32)
    x1i = I("i_x1", [4096, T], F32)
    xn1i = I("i_xn1", [4096, T], BF16)
    lgvd = I("i_lgv", [128, 16], F32)
    lgpin = I("i_lgpin", [1, 16], F32)
    lgpout = I("i_lgpout", [1, 16], F32)
    lrni = I("i_lrn", [128, 16], F32)
    qdi = I("i_q", [2048, T], BF16)
    kin = I("i_kin", [1, 4194304], BF16)
    kout = I("i_kout", [4, 4194304], BF16)
    vin = I("i_vin", [1, 4194304], BF16)
    vout = I("i_vout", [4, 4194304], BF16)
    aux3i = I("i_aux3", [4096, T], BF16)

    kvin8 = kvin.rearrange("a (h p c) -> (a h) p c", h=8, p=128)
    kvout8 = kvout.rearrange("a (h p c) -> (a h) p c", h=8, p=128)
    kin2048 = kin.rearrange("a (r t) -> (a r) t", r=2048)
    kout3 = kout.rearrange("b (r t) -> b r t", r=2048)
    vin_v2c = vin.rearrange("a (ct g p c) -> (a ct) g p c", ct=16, g=2, p=128)
    vout_g = vout.rearrange("b (ct g p c) -> b ct g p c", ct=16, g=2, p=128)

    a_dT = nc.dram_tensor("a_dT", [2048, 4096], I32, kind="ExternalInput").ap()
    a_sc = nc.dram_tensor("a_sc", [2048, 4096], F32, kind="ExternalInput").ap()

    G = [[0, 1, 2, 3], [4, 5, 6, 7]]
    G8 = [[0, 1, 2, 3, 4, 5, 6, 7]]

    F0 = FusedEnv(nc, ctx, "a_", {"dT": a_dT, "sc": a_sc,
                                  "w2lo": w2r_d[0:2048, :], "w2hi": w2r_d[2048:4096, :]})
    build_launch2a(F=F0); F0.close(); ctx.barrier()

    F1 = FusedEnv(nc, ctx, "b_", {"kvp": kvin8, "qphiT": qphi_d, "rotT": rot_d})
    build_launch1(F=F1); F1.close(); ctx.barrier()

    ar_ev = ctx.on("gpsimd", [], lambda e: e.collective_compute(
        "AllReduce", ALU.add, replica_groups=G, ins=[kvin[:]], outs=[kvout[:]]),
        step_override=1)

    F2 = FusedEnv(nc, ctx, "c_", {"rotT": rot_d, "qphiT": qphi_d, "kv_red": kvout8,
                                  "w2rows": w2r_d, "x1T": x1i, "xn1T": xn1i,
                                  "lgv": lgvd, "lgp": lgpin})
    F2.collective_evs = {"kv_ar": ar_ev}
    build_launch2b(F=F2); F2.close(); ctx.barrier()

    lgp_ev = ctx.on("gpsimd", [], lambda e: e.collective_compute(
        "AllReduce", ALU.add, replica_groups=G8, ins=[lgpin[:]], outs=[lgpout[:]]),
        step_override=1)

    F3 = FusedEnv(nc, ctx, "d_", {"xn1T": xn1i, "qT": qdi, "kT": kin2048,
                                  "v2c": vin_v2c, "aux3T": aux3i, "learned": lrni})
    build_launch3(F=F3); F3.close(); ctx.barrier()

    ag1 = ctx.on("gpsimd", [], lambda e: e.collective_compute(
        "AllGather", ALU.bypass, replica_groups=G, ins=[kin[:]], outs=[kout[:]]),
        step_override=1)
    ag2 = ctx.on("gpsimd", [ag1], lambda e: e.collective_compute(
        "AllGather", ALU.bypass, replica_groups=G, ins=[vin[:]], outs=[vout[:]]),
        step_override=1)
    ctx.wait_only("sync", [ag1, ag2, lgp_ev])
    ctx.wait_only("gpsimd", [ag1, ag2, lgp_ev])

    F4 = FusedEnv(nc, ctx, "e_", {"qT": qdi, "kT": kin2048, "v2c": vin_v2c,
                                  "k_gath": kout3, "v_gath": vout_g,
                                  "aux3T": aux3i, "xn1T": xn1i, "x1T": x1i,
                                  "lgv": lgvd, "lgp_out": lgpout, "learned": lrni})
    build_launch4(F=F4); F4.close()

    es.close()
    return nc


def fused_in_map(inp, core):
    _prep_l1(inp); _prep_l2(inp); _prep_l3(inp); _prep_l4(inp)
    b, j = core // 4, core % 4
    x = inp["L0_attn_hidden_states"][b, T * j:T * (j + 1), :]
    xT = np.ascontiguousarray(x.T)
    m = {"a_dT": _PC["dTfull"], "a_sc": _PC["scbig"]}
    for k, v in l1_in_map(inp, core).items():
        m["b_" + k] = v
    m.update({"c_wo_p": _PC["wo_p"], "c_fb_c": _PC["fb_c"], "c_bo_c": _PC["bo_c"],
              "c_xT": xT, "c_ident": _PC["ident"]})
    d3 = l3_in_map(inp, core, None)
    d3.pop("xn1T")
    for k, v in d3.items():
        m["d_" + k] = v
    offs1, sup2 = l4_suppr(core)
    m.update({"e_offs1": offs1, "e_sup2": sup2, "e_tri1": _PC["tri1"], "e_tri2": _PC["tri2"],
              "e_gw1_c": _PC["gw1_c"], "e_gb_c": _PC["gb_c"], "e_wo1_p": _PC["wo1_p"],
              "e_bo1_c": _PC["bo1_c"], "e_ident": _PC["ident"]})
    return m


def kernel(**inputs):
    _PC.clear()
    inp = {k: np.asarray(v) for k, v in inputs.items()}
    nc = build_fused()
    res = run_bass_kernel_spmd(nc, [fused_in_map(inp, c) for c in range(8)],
                               core_ids=list(range(8)))
    out = np.empty((2, 8192, 4096), np.float32)
    for c in range(8):
        b, j = c // 4, c % 4
        out[b, T * j:T * (j + 1), :] = res.results[c]["e_outT"].T
    return out


def kernel_unfused(**inputs):
    _PC.clear()
    inp = {k: np.asarray(v) for k, v in inputs.items()}

    res1 = run_bass_kernel_spmd(build_launch1(), [l1_in_map(inp, c) for c in range(8)],
                                core_ids=list(range(8)))
    res2a = run_bass_kernel_spmd(build_launch2a(), [l2a_in_map(inp, c) for c in range(8)],
                                 core_ids=list(range(8)))
    w2_p = assemble_w2(res2a)
    res2b = run_bass_kernel_spmd(build_launch2b(), [l2b_in_map(inp, c, res1, w2_p) for c in range(8)],
                                 core_ids=list(range(8)))
    res3 = run_bass_kernel_spmd(build_launch3(),
                                [l3_in_map(inp, c, res2b.results[c]["xn1T"]) for c in range(8)],
                                core_ids=list(range(8)))
    r3 = [res3.results[c] for c in range(8)]
    r2 = [res2b.results[c] for c in range(8)]
    kfull = [np.ascontiguousarray(np.concatenate([r3[4 * b + j]["kT"] for j in range(4)], axis=1))
             for b in range(2)]
    vfull = []
    for b in range(2):
        vf = np.concatenate([r3[4 * b + j]["vt"] for j in range(4)], axis=1)
        vfull.append(np.ascontiguousarray(
            vf.reshape(16, 8, 8, 128, 128).transpose(0, 1, 3, 2, 4).reshape(16, 8, 128, 1024)))
    lgp8 = np.ascontiguousarray(np.concatenate([r2[c]["lgp"][:, 0:1] for c in range(8)], axis=1))
    res4 = run_bass_kernel_spmd(build_launch4(),
                                [l4_in_map(inp, c, r2[c], r3[c], kfull[c // 4], vfull[c // 4], lgp8)
                                 for c in range(8)],
                                core_ids=list(range(8)))
    out = np.empty((2, 8192, 4096), np.float32)
    for c in range(8):
        b, j = c // 4, c % 4
        out[b, T * j:T * (j + 1), :] = res4.results[c]["outT"].T
    return out


# version 04 (final)
# speedup vs baseline: 1.4002x; 1.3902x over previous
import numpy as np
import ml_dtypes
from contextlib import ExitStack
import concourse.bass as bass
import concourse.mybir as mybir
from concourse.bass_utils import run_bass_kernel_spmd

F32 = mybir.dt.float32
BF16 = mybir.dt.float16
RBF16 = mybir.dt.bfloat16
I32 = mybir.dt.int32
AF = mybir.ActivationFunctionType
ALU = mybir.AluOpType
AX = mybir.AxisListType

BF = np.float16
T = 2048
NFT = 32
NTT = 16

ENGINES = ("sync", "gpsimd", "vector", "scalar", "tensor")
DMA_ENG = {"sync", "gpsimd"}


class Ctx:

    def __init__(self, nc, es):
        self.nc = nc
        self.es = es
        self.block = es.enter_context(nc.Block())
        self.sem = {n: es.enter_context(nc.semaphore("sem_" + n)) for n in ENGINES}
        self.cnt = {n: 0 for n in ENGINES}

    @staticmethod
    def _flat(deps):
        out = []
        for d in deps:
            if d is None:
                continue
            if isinstance(d, list):
                out.extend(x for x in d if x is not None)
            else:
                out.append(d)
        best = {}
        for (s, c) in out:
            best[s] = max(best.get(s, 0), c)
        return sorted(best.items())

    def barrier(self):
        snap = dict(self.cnt)
        for e in ENGINES:
            deps = [(o, snap[o]) for o in ENGINES if o != e]
            self.on(e, deps, lambda eng: eng.nop(), step_override=1)

    def on(self, eng_name, deps, emit, signal=True, step_override=None):
        step = step_override if step_override is not None else (16 if eng_name in DMA_ENG else 1)
        sem = self.sem[eng_name]
        deps = self._flat(deps)

        def body(eng):
            for (s, c) in deps:
                if c > 0:
                    eng.wait_ge(self.sem[s], c)
            ret = emit(eng)
            if signal:
                if isinstance(ret, list):
                    for ins in ret:
                        ins.then_inc(sem, step)
                        self.cnt[eng_name] += step
                else:
                    ret.then_inc(sem, step)
                    self.cnt[eng_name] += step

        getattr(self.block, eng_name)(body)
        return (eng_name, self.cnt[eng_name])

    def wait_only(self, eng_name, deps):
        deps = self._flat(deps)

        def body(eng):
            for (s, c) in deps:
                if c > 0:
                    eng.wait_ge(self.sem[s], c)

        getattr(self.block, eng_name)(body)


class FusedEnv:

    def __init__(self, nc, ctx, prefix, overrides):
        self.nc = nc
        self.ctx = ctx
        self.prefix = prefix
        self.overrides = overrides
        self.es = ExitStack()

    def dram(self, name, shape, dtype, kind):
        if name in self.overrides:
            return self.overrides[name]
        return self.nc.dram_tensor(self.prefix + name, shape, dtype, kind=kind).ap()

    def close(self):
        self.es.close()



def pack_lhsT(WT):
    Fd, Od = WT.shape
    return np.ascontiguousarray(
        WT.reshape(Fd // 128, 128, Od // 128, 128).transpose(2, 1, 0, 3).reshape(Od // 128, 128, Fd)
    )



def build_launch1(debug=False, F=None):
    standalone = F is None
    if standalone:
        nc = bass.Bass("TRN2", target_bir_lowering=False)
        D = lambda name, shape, dtype, kind: nc.dram_tensor(name, shape, dtype, kind=kind).ap()
    else:
        nc = F.nc
        D = F.dram
    xT = D("xT", [4096, T], F32, "ExternalInput")
    wq_p = D("wq_p", [32, 128, 4096], F32, "ExternalInput")
    wk_p = D("wk_p", [8, 128, 4096], F32, "ExternalInput")
    wv_p = D("wv_p", [8, 128, 4096], F32, "ExternalInput")
    had_p = D("had_p", [32, 128, 4096], F32, "ExternalInput")
    bq_c = D("bq_c", [128, 32], F32, "ExternalInput")
    bkv_c = D("bkv_c", [128, 16], F32, "ExternalInput")
    ident_d = D("ident", [128, 128], BF16, "ExternalInput")

    kvp_o = D("kvp", [8, 128, 129], F32, "ExternalOutput")
    qphi_o = D("qphiT", [4096, T], BF16, "ExternalOutput")
    rot_o = D("rotT", [4096, T], BF16, "ExternalOutput")
    if debug:
        xn_o = D("xn0T", [4096, T], BF16, "ExternalOutput")

    es = ExitStack() if standalone else F.es
    _pfx = "" if standalone else F.prefix
    S = lambda n, *a: es.enter_context(nc.sbuf_tensor(_pfx + n, *a))
    P = lambda n, *a: es.enter_context(nc.psum_tensor(_pfx + n, *a))
    xb = S("xb", [128, NFT * T], BF16)
    _xf0 = S("xf0", [128, T], F32)
    xf = [_xf0, _xf0]
    wbuf = [S(f"wbuf{i}", [128, 4096], BF16) for i in range(2)]
    abc = S("abc", [128, T], F32)
    bbc = S("bbc", [128, T], F32)
    stt = S("stt", [128, 16], F32)
    st2 = S("st2", [128, 16], F32)
    st3 = S("st3", [128, 16], F32)
    arow = S("arow", [1, 128], F32)
    brow = S("brow", [1, 128], F32)
    onesc_bf = S("onesc_bf", [128, 1], BF16)
    epsb = S("epsb", [128, 1], F32)
    ones1_f = S("ones1_f", [1, 128], F32)
    ident = S("identS", [128, 128], BF16)
    identf = S("identfS", [128, 128], F32)
    biasq = S("biasq", [128, 32], F32)
    biaskv = S("biaskv", [128, 16], F32)
    kcol = S("kcol", [128, T], BF16)
    vcol = S("vcol", [128, T], BF16)
    outt = kcol
    pa = S("pa", [128, T], F32)
    pex = S("pex", [128, T], BF16)
    prl = S("prl", [128, T], BF16)
    sq = pex
    xc = prl
    ktm = S("ktm", [128, 128], BF16)
    vtm = S("vtm", [128, 128], BF16)
    kvh = abc

    pacc = P("pacc", [128, T], F32)
    pkv = P("pkv", [128, 512], F32)
    ptrk = P("ptrk", [128, 128], BF16)
    ptrv = P("ptrv", [128, 128], BF16)
    pst = P("pst", [128, 64], F32)

    ctx = Ctx(nc, es) if standalone else F.ctx

    def phi_evac(psum_ap, out_ap, dep_pe, bias_col=None, extra_out_dep=None, sl=slice(0, T)):
        pa_, pex_, prl_ = pa[:, sl], pex[:, sl], prl[:, sl]
        if bias_col is not None:
            e1 = ctx.on("vector", [dep_pe, cload], lambda e: e.tensor_scalar(
                pa_, psum_ap, bias_col, 0.0, op0=ALU.add, op1=ALU.min))
            e3 = ctx.on("vector", [dep_pe, cload], lambda e: e.tensor_scalar(
                prl_, psum_ap, bias_col, 0.0, op0=ALU.add, op1=ALU.max))
        else:
            e1 = ctx.on("vector", [dep_pe], lambda e: e.tensor_scalar(
                pa_, psum_ap, 0.0, None, op0=ALU.min))
            e3 = ctx.on("vector", [dep_pe], lambda e: e.tensor_scalar(
                prl_, psum_ap, 0.0, None, op0=ALU.max))
        e2 = ctx.on("scalar", [e1],
                    lambda e: e.activation(pex_, pa_, AF.Exp))
        e4 = ctx.on("vector", [e2, e3, extra_out_dep],
                    lambda e: e.tensor_add(out_ap, pex_, prl_))
        return e4, [e1, e3, e4]

    consts = ctx.on("vector", [], lambda e: [
        e.memset(onesc_bf[:], 1.0),
        e.memset(ones1_f[:], 1.0),
        e.memset(epsb[:], 1e-5),
    ][-1])
    cload = ctx.on("gpsimd", [], lambda e: [
        e.dma_start(out=ident[:], in_=ident_d[:]),
        e.dma_start(out=biasq[:], in_=bq_c[:]),
        e.dma_start(out=biaskv[:], in_=bkv_c[:]),
    ])
    cload = ctx.on("vector", [cload], lambda e: e.tensor_copy(identf[:], ident[:]))

    xf_free = None
    stats_ev = None
    for ft in range(NFT):
        lev = ctx.on("sync", [xf_free],
                     lambda e, ft=ft: e.dma_start(out=xf[ft % 2][:], in_=xT[128 * ft:128 * (ft + 1), :]))
        cast_ev = ctx.on("vector", [lev, stats_ev],
                         lambda e, ft=ft: e.tensor_copy(xc[:], xf[ft % 2][:]))
        sqr_ev = ctx.on("scalar", [lev, stats_ev],
                        lambda e, ft=ft: e.activation(sq[:], xf[ft % 2][:], AF.Square))
        xf_free = [cast_ev, sqr_ev]

        def mk_stats(e, ft=ft):
            mm = None
            for tt in range(NTT):
                mm = e.matmul(pst[:, tt:tt + 1], xc[:, 128 * tt:128 * (tt + 1)], onesc_bf[:],
                              start=(ft == 0 and tt == 0), stop=(ft == NFT - 1))
            for tt in range(NTT):
                mm = e.matmul(pst[:, 16 + tt:17 + tt], sq[:, 128 * tt:128 * (tt + 1)], onesc_bf[:],
                              start=False, stop=(ft == NFT - 1))
            return mm
        stats_ev = ctx.on("tensor", [consts, cast_ev, sqr_ev], mk_stats)

    ev = ctx.on("vector", [stats_ev], lambda e: [
        e.tensor_scalar(stt[:], pst[:, 0:16], 1.0 / 4096.0, None, op0=ALU.mult),
        e.tensor_scalar(st2[:], pst[:, 16:32], 1.0 / 4096.0, None, op0=ALU.mult),
    ][-1])
    ev = ctx.on("scalar", [ev], lambda e: e.activation(st3[:], stt[:], AF.Square))
    ev = ctx.on("vector", [ev], lambda e: e.tensor_sub(st2[:], st2[:], st3[:]))
    ev = ctx.on("scalar", [ev], lambda e: [
        e.activation(st3[:], st2[:], AF.Ln, bias=epsb[:]),
        e.activation(st3[:], st3[:], AF.Exp, scale=-0.5),
    ][-1])
    rows_ev = ctx.on("vector", [ev], lambda e:
        e.tensor_mul(stt[:], stt[:], st3[:]))

    prow = pkv
    bc_ev = None
    for src_col, rowbuf in [(st3, arow), (stt, brow)]:
        for tt in range(NTT):
            tev = ctx.on("tensor", [rows_ev, cload, bc_ev],
                         lambda e, tt=tt, s=src_col: e.transpose(prow[0:1, 0:128], s[:, tt:tt + 1], identf[:]))
            cpe = ctx.on("vector", [tev],
                         lambda e, r=rowbuf: e.tensor_copy(r[:], prow[0:1, 0:128]))
            bc_ev = ctx.on("tensor", [cpe],
                           lambda e, tt=tt, r=rowbuf: e.matmul(
                               pacc[:, 128 * tt:128 * (tt + 1)], ones1_f[:], r[:],
                               start=(tt % 4 == 0), stop=True))
        if src_col is st3:
            abc_ev = ctx.on("vector", [bc_ev], lambda e: e.tensor_copy(abc[:], pacc[:]))
            bc_ev = abc_ev
        else:
            bbc_ev = ctx.on("vector", [bc_ev], lambda e: e.tensor_copy(bbc[:], pacc[:]))

    norm_ev = [None] * NFT
    for ft in range(NFT):
        lev = ctx.on("sync", [norm_ev[ft - 1] if ft >= 1 else [stats_ev] + xf_free],
                     lambda e, ft=ft: e.dma_start(out=xf[ft % 2][:], in_=xT[128 * ft:128 * (ft + 1), :]))
        norm_ev[ft] = ctx.on("vector", [lev, bbc_ev], lambda e, ft=ft: [
            e.tensor_mul(pa[:], xf[ft % 2][:], abc[:]),
            e.tensor_sub(xb[:, T * ft:T * (ft + 1)], pa[:], bbc[:]),
        ][-1])
    xn_ev = norm_ev[-1]

    dbg_ev = None
    if debug:
        dbg_ev = ctx.on("sync", [xn_ev], lambda e: [
            e.dma_start(out=xn_o[128 * ft:128 * (ft + 1), :], in_=xb[:, T * ft:T * (ft + 1)])
            for ft in range(NFT)])

    kv_evac_ev = None
    kv_st = None
    pacc_free = [xn_ev]
    tt_mm_ev = None
    vmm_ev = None
    for h in range(8):
        wload_ev = ctx.on("gpsimd", [vmm_ev], lambda e, h=h: [
            e.dma_start(out=wbuf[0][:], in_=wk_p[h]),
            e.dma_start(out=wbuf[1][:], in_=wv_p[h]),
        ])

        def mk_kmm(e, h=h):
            mm = None
            for ft in range(NFT):
                for c in range(4):
                    mm = e.matmul(pacc[:, 512 * c:512 * (c + 1)],
                                  wbuf[0][:, 128 * ft:128 * (ft + 1)],
                                  xb[:, T * ft + 512 * c:T * ft + 512 * (c + 1)],
                                  start=(ft == 0), stop=(ft == NFT - 1))
            return mm
        kmm_ev = ctx.on("tensor", [xn_ev, wload_ev, pacc_free, consts, cload], mk_kmm)
        kphi_ev, pacc_readers = phi_evac(pacc[:], kcol[:], kmm_ev, bias_col=biaskv[:, h:h + 1])

        def mk_vmm(e, h=h):
            mm = None
            for ft in range(NFT):
                for c in range(4):
                    mm = e.matmul(pacc[:, 512 * c:512 * (c + 1)],
                                  wbuf[1][:, 128 * ft:128 * (ft + 1)],
                                  xb[:, T * ft + 512 * c:T * ft + 512 * (c + 1)],
                                  start=(ft == 0), stop=(ft == NFT - 1))
            return mm
        vmm_ev = ctx.on("tensor", pacc_readers, mk_vmm)
        vcol_ev = ctx.on("vector", [vmm_ev], lambda e, h=h: e.tensor_scalar(
            vcol[:], pacc[:], biaskv[:, 8 + h:9 + h], None, op0=ALU.add))
        pacc_free = [vcol_ev]

        cp_ev = None
        for tt in range(NTT):
            trev = ctx.on("tensor", [kphi_ev, vcol_ev, cp_ev], lambda e, tt=tt: [
                e.transpose(ptrk[:], kcol[:, 128 * tt:128 * (tt + 1)], ident[:]),
                e.transpose(ptrv[:], vcol[:, 128 * tt:128 * (tt + 1)], ident[:]),
            ][-1])
            cp_ev = ctx.on("vector", [trev], lambda e, tt=tt: [
                e.tensor_copy(ktm[:], ptrk[:]),
                e.tensor_copy(vtm[:], ptrv[:]),
            ][-1])
            tt_mm_ev = ctx.on("tensor", [cp_ev, kv_evac_ev], lambda e, tt=tt: [
                e.matmul(pkv[:, 0:128], ktm[:], vtm[:], start=(tt == 0), stop=(tt == NTT - 1)),
                e.matmul(pkv[:, 128:129], ktm[:], onesc_bf[:], start=False, stop=(tt == NTT - 1)),
            ][-1])
        kv_evac_ev = ctx.on("vector", [tt_mm_ev, kv_st], lambda e, h=h:
            e.tensor_copy(kvh[:, 0:129], pkv[:, 0:129]))
        kv_st = ctx.on("sync", [kv_evac_ev], lambda e, h=h: e.dma_start(out=kvp_o[h], in_=kvh[:, 0:129]))

    store_ev = None
    outt_free = None
    wb_free = [vmm_ev, vmm_ev]
    hfree = [pacc_free, pacc_free]
    for phase, (w_p, out_d, has_phi) in enumerate(
            [(wq_p, qphi_o, True), (had_p, rot_o, False)]):
        for ot in range(32):
            i = ot % 2
            wl = ctx.on("gpsimd", [wb_free[i]],
                        lambda e, w_p=w_p, ot=ot, i=i: e.dma_start(out=wbuf[i][:], in_=w_p[ot]))
            half_mm = [None, None]
            for hf in range(2):
                def mk_mm(e, ot=ot, i=i, hf=hf):
                    mm = None
                    for ft in range(NFT):
                        for c in (2 * hf, 2 * hf + 1):
                            mm = e.matmul(pacc[:, 512 * c:512 * (c + 1)],
                                          wbuf[i][:, 128 * ft:128 * (ft + 1)],
                                          xb[:, T * ft + 512 * c:T * ft + 512 * (c + 1)],
                                          start=(ft == 0), stop=(ft == NFT - 1))
                    return mm
                half_mm[hf] = ctx.on("tensor", [wl] + hfree[hf], mk_mm)
            wb_free[i] = half_mm[1]
            store_ev = None
            for hf in range(2):
                sl = slice(1024 * hf, 1024 * (hf + 1))
                psl = pacc[:, sl]
                if has_phi:
                    evac_ev, rd = phi_evac(psl, outt[:, sl], half_mm[hf],
                                           bias_col=biasq[:, ot:ot + 1],
                                           extra_out_dep=outt_free, sl=sl)
                    hfree[hf] = rd
                else:
                    evac_ev = ctx.on("vector", [half_mm[hf], outt_free],
                                     lambda e, psl=psl, sl=sl: e.tensor_copy(outt[:, sl], psl))
                    hfree[hf] = [evac_ev]
                store_ev = ctx.on("sync", [evac_ev],
                                  lambda e, ot=ot, out_d=out_d, sl=sl:
                                  e.dma_start(out=out_d[128 * ot:128 * (ot + 1), sl], in_=outt[:, sl]))
            outt_free = store_ev
    pacc_free = hfree[0] + hfree[1]

    ctx.wait_only("gpsimd", [store_ev, kv_st, dbg_ev])

    if standalone:
        es.close()
    return nc



_PC = {}


def _prep_l1(inp):
    if "wq_p" in _PC:
        return
    _PC["wq_p"] = pack_lhsT(np.ascontiguousarray(inp["L0_attn_Wq"].T))
    _PC["wk_p"] = pack_lhsT(np.ascontiguousarray(inp["L0_attn_Wk"].T))
    _PC["wv_p"] = pack_lhsT(np.ascontiguousarray(inp["L0_attn_Wv"].T))
    had = inp["L0_ffn_hadamard"]
    hperm = np.concatenate([had[:, 0::2], had[:, 1::2]], axis=1)
    _PC["had_p"] = pack_lhsT(np.ascontiguousarray(hperm))
    _PC["ident"] = np.eye(128, dtype=BF)
    _PC["bq_c"] = np.ascontiguousarray(inp["L0_attn_bq"].reshape(32, 128).T)
    _PC["bkv_c"] = np.ascontiguousarray(
        np.concatenate([inp["L0_attn_bk"], inp["L0_attn_bv"]]).reshape(16, 128).T)


def l1_in_map(inp, core):
    _prep_l1(inp)
    b, j = core // 4, core % 4
    x = inp["L0_attn_hidden_states"][b, T * j:T * (j + 1), :]
    return {
        "xT": np.ascontiguousarray(x.T),
        "wq_p": _PC["wq_p"], "wk_p": _PC["wk_p"], "wv_p": _PC["wv_p"],
        "had_p": _PC["had_p"], "bq_c": _PC["bq_c"], "bkv_c": _PC["bkv_c"],
        "ident": _PC["ident"],
    }



def build_launch2a(F=None):
    standalone = F is None
    if standalone:
        nc = bass.Bass("TRN2", target_bir_lowering=False)
        D = lambda name, shape, dtype, kind: nc.dram_tensor(name, shape, dtype, kind=kind).ap()
    else:
        nc = F.nc
        D = F.dram
    dT = D("dT", [256, 4096], I32, "ExternalInput")
    sc = D("sc", [256, 4096], F32, "ExternalInput")
    lo_o = D("w2lo", [256, 4096], BF16, "ExternalOutput")
    hi_o = D("w2hi", [256, 4096], BF16, "ExternalOutput")

    es = ExitStack() if standalone else F.es
    _pfx = "" if standalone else F.prefix
    S = lambda n, *a: es.enter_context(nc.sbuf_tensor(_pfx + n, *a))
    dint = S("dint", [128, 4096], I32)
    t1i = S("t1i", [128, 4096], I32)
    scb = S("scb", [128, 4096], F32)
    nibf = S("nibf", [128, 4096], F32)
    wlo = S("wlo", [128, 4096], BF16)
    whi = S("whi", [128, 4096], BF16)
    ctx = Ctx(nc, es) if standalone else F.ctx

    st_ev = None
    for j in range(2 if standalone else 16):
        ld = ctx.on("sync", [st_ev], lambda e, j=j: [
            e.dma_start(out=dint[:], in_=dT[128 * j:128 * (j + 1), :]),
            e.dma_start(out=scb[:], in_=sc[128 * j:128 * (j + 1), :]),
        ])
        lo_ev = ctx.on("vector", [ld], lambda e: [
            e.tensor_scalar(t1i[:], dint[:], 15, None, op0=ALU.bitwise_and),
            e.tensor_scalar(nibf[:], t1i[:], 8.0, None, op0=ALU.subtract),
            e.tensor_mul(wlo[:], nibf[:], scb[:]),
        ][-1])
        hi_ev = ctx.on("vector", [lo_ev], lambda e: [
            e.tensor_scalar(t1i[:], dint[:], 4, 15, op0=ALU.logical_shift_right, op1=ALU.bitwise_and),
            e.tensor_scalar(nibf[:], t1i[:], 8.0, None, op0=ALU.subtract),
            e.tensor_mul(whi[:], nibf[:], scb[:]),
        ][-1])
        st_ev = ctx.on("sync", [lo_ev, hi_ev], lambda e, j=j: [
            e.dma_start(out=lo_o[128 * j:128 * (j + 1), :], in_=wlo[:]),
            e.dma_start(out=hi_o[128 * j:128 * (j + 1), :], in_=whi[:]),
        ])
    ctx.wait_only("gpsimd", [st_ev])
    if standalone:
        es.close()
    return nc



def build_launch2b(debug=False, F=None):
    standalone = F is None
    if standalone:
        nc = bass.Bass("TRN2", target_bir_lowering=False)
        D = lambda name, shape, dtype, kind: nc.dram_tensor(name, shape, dtype, kind=kind).ap()
    else:
        nc = F.nc
        D = F.dram
    rot_d = D("rotT", [4096, T], BF16, "ExternalInput")
    qphi_d = D("qphiT", [4096, T], BF16, "ExternalInput")
    if standalone:
        kvp4 = D("kvp4", [4, 8, 128, 129], F32, "ExternalInput")
    else:
        kvp4 = F.overrides["kv_red"]
    if standalone:
        w2_p = D("w2_p", [32, 128, 4096], BF16, "ExternalInput")
    else:
        w2_p = F.overrides["w2rows"]
    wo_p = D("wo_p", [32, 128, 4096], F32, "ExternalInput")
    fb_c = D("fb_c", [128, 32], F32, "ExternalInput")
    bo_c = D("bo_c", [128, 32], F32, "ExternalInput")
    xT = D("xT", [4096, T], F32, "ExternalInput")
    ident_d = D("ident", [128, 128], BF16, "ExternalInput")

    x1_o = D("x1T", [4096, T], F32, "ExternalOutput")
    xn1_o = D("xn1T", [4096, T], BF16, "ExternalOutput")
    lgv_o = D("lgv", [128, 16], F32, "ExternalOutput")
    lgp_o = D("lgp", [1, 16], F32, "ExternalOutput")
    if debug:
        g0_o = D("g0T", [4096, T], F32, "ExternalOutput")
        pst_dbg = D("pst_dbg", [128, 64], F32, "ExternalOutput")
        fin_dbg = D("fin_dbg", [128, 48], F32, "ExternalOutput")
        abc_dbg = D("abc_dbg", [128, 64], F32, "ExternalOutput")
        s0_o = D("s0T", [4096, T], F32, "ExternalOutput")
        attn_o = D("attnT", [4096, T], BF16, "ExternalOutput")
        kvs_o = D("kvsum_o", [128, 1032], F32, "ExternalOutput")
        rn_o = D("rnorm_o", [128, 8], F32, "ExternalOutput")

    g0_d = D("g0_d", [4096, T], F32, "Internal")

    es = ExitStack() if standalone else F.es
    _pfx = "" if standalone else F.prefix
    S = lambda n, *a: es.enter_context(nc.sbuf_tensor(_pfx + n, *a))
    P = lambda n, *a: es.enter_context(nc.psum_tensor(_pfx + n, *a))
    xb2 = S("xb2", [128, NFT * T], BF16)
    wbuf = [S(f"wbuf{i}", [128, 4096], BF16) for i in range(2)]
    xload = S("xload", [128, T], F32)
    gld = S("gld", [128, T], F32)
    sg = S("sg", [128, T], F32)
    x1c = S("x1c", [128, T], BF16)
    sqb = S("sqb", [128, T], RBF16)
    kvsum = S("kvsum", [128, 1032], F32)
    kvt2 = S("kvt2", [128, 1032], F32)
    kvbf = S("kvbf", [128, 1024], BF16)
    rnorm = S("rnorm", [128, 8], F32)
    fbias = S("fbias", [128, 32], F32)
    bobias = S("bobias", [128, 32], F32)
    stt = S("stt", [128, 16], F32)
    st2 = S("st2", [128, 16], F32)
    st3 = S("st3", [128, 16], F32)
    arow = S("arow", [1, 128], F32)
    onesc_bf = S("onesc_bf", [128, 1], BF16)
    onesc_bb = S("onesc_bb", [128, 1], RBF16)
    onesc_f = S("onesc_f", [128, 1], F32)
    ones1_f = S("ones1_f", [1, 128], F32)
    epsb = S("epsb", [128, 1], F32)
    oneb = S("oneb", [128, 1], F32)
    ident = S("identS", [128, 128], BF16)
    identf = S("identfS", [128, 128], F32)
    abc = S("abc", [128, T], F32)
    bbc = S("bbc", [128, T], F32)

    pacc = P("pacc", [128, T], F32)
    pst = P("pst", [128, 64], F32)
    ppr = P("ppr", [1, 128], F32)
    psc = P("psc", [1, 64], F32)

    ctx = Ctx(nc, es) if standalone else F.ctx

    consts = ctx.on("vector", [], lambda e: [
        e.memset(onesc_bf[:], 1.0),
        e.memset(onesc_bb[:], 1.0),
        e.memset(onesc_f[:], 1.0),
        e.memset(ones1_f[:], 1.0),
        e.memset(epsb[:], 1e-5),
        e.memset(oneb[:], 1.0),
    ][-1])
    cload = ctx.on("gpsimd", [], lambda e: [
        e.dma_start(out=ident[:], in_=ident_d[:]),
        e.dma_start(out=fbias[:], in_=fb_c[:]),
        e.dma_start(out=bobias[:], in_=bo_c[:]),
    ])
    cload = ctx.on("vector", [cload], lambda e: e.tensor_copy(identf[:], ident[:]))

    rl_ev = ctx.on("sync", [], lambda e: [
        e.dma_start(out=xb2[:, T * ft:T * (ft + 1)], in_=rot_d[128 * ft:128 * (ft + 1), :])
        for ft in range(NFT)])

    pacc_free = None
    bfree = [None, None]
    g_st = None
    wl_free = [None, None]
    for ot in range(32):
        i = ot % 2
        if standalone:
            wl = ctx.on("sync", [wl_free[i]], lambda e, ot=ot, i=i: e.dma_start(out=wbuf[i][:], in_=w2_p[ot]))
        else:
            wl = ctx.on("sync", [wl_free[i]], lambda e, ot=ot, i=i: e.dma_start(
                out=wbuf[i][:].rearrange("p (rt c) -> p rt c", rt=32),
                in_=w2_p.rearrange("(rt p) o -> p rt o", p=128)[:, :, 128 * ot:128 * (ot + 1)]))

        half_mm = [None, None]
        for hf in range(2):
            def mk_mm(e, ot=ot, i=i, hf=hf):
                mm = None
                for ft in range(NFT):
                    for c in (2 * hf, 2 * hf + 1):
                        mm = e.matmul(pacc[:, 512 * c:512 * (c + 1)],
                                      wbuf[i][:, 128 * ft:128 * (ft + 1)],
                                      xb2[:, T * ft + 512 * c:T * ft + 512 * (c + 1)],
                                      start=(ft == 0), stop=(ft == NFT - 1))
                return mm
            half_mm[hf] = ctx.on("tensor", [wl, rl_ev, bfree[hf]], mk_mm)
        wl_free[i] = half_mm[1]
        for hf in range(2):
            sl = slice(1024 * hf, 1024 * (hf + 1))
            e1 = ctx.on("vector", [half_mm[hf], cload, g_st], lambda e, ot=ot, sl=sl: e.tensor_scalar(
                sg[:, sl], pacc[:, sl], fbias[:, ot:ot + 1], None, op0=ALU.add))
            bfree[hf] = [e1]
            e2 = ctx.on("scalar", [e1, g_st], lambda e, sl=sl: e.activation(gld[:, sl], sg[:, sl], AF.Silu))
            g_st = ctx.on("sync", [e2], lambda e, ot=ot, sl=sl:
                          e.dma_start(out=g0_d[128 * ot:128 * (ot + 1), sl], in_=gld[:, sl]))
        if debug:
            g_st = ctx.on("sync", [g_st], lambda e, ot=ot:
                          e.dma_start(out=g0_o[128 * ot:128 * (ot + 1), :], in_=gld[:]))

    pacc_free = (bfree[0] or []) + (bfree[1] or [])
    if standalone:
        kr = ctx.on("sync", [], lambda e: e.dma_start(out=kvsum[:].rearrange("p (h c) -> p h c", h=8), in_=kvp4[0].rearrange("h p c -> p h c")))
        kr2 = ctx.on("sync", [kr], lambda e: e.dma_start(out=kvt2[:].rearrange("p (h c) -> p h c", h=8), in_=kvp4[1].rearrange("h p c -> p h c")))
        s_ev = ctx.on("vector", [kr2], lambda e: e.tensor_add(kvsum[:], kvsum[:], kvt2[:]))
        kr3 = ctx.on("sync", [s_ev], lambda e: e.dma_start(out=kvt2[:].rearrange("p (h c) -> p h c", h=8), in_=kvp4[2].rearrange("h p c -> p h c")))
        s_ev = ctx.on("vector", [kr3], lambda e: e.tensor_add(kvsum[:], kvsum[:], kvt2[:]))
        kr4 = ctx.on("sync", [s_ev], lambda e: e.dma_start(out=kvt2[:].rearrange("p (h c) -> p h c", h=8), in_=kvp4[3].rearrange("h p c -> p h c")))
        s_ev = ctx.on("vector", [kr4], lambda e: e.tensor_add(kvsum[:], kvsum[:], kvt2[:]))
    else:
        s_ev = ctx.on("sync", [F.collective_evs["kv_ar"]], lambda e: e.dma_start(
            out=kvsum[:].rearrange("p (h c) -> p h c", h=8),
            in_=kvp4.rearrange("h p c -> p h c")))
    kbf_ev = ctx.on("vector", [s_ev], lambda e: [
        e.tensor_copy(kvbf[:, 128 * h:128 * (h + 1)], kvsum[:, 129 * h:129 * h + 128])
        for h in range(8)][-1])
    rn_ev = ctx.on("vector", [s_ev], lambda e: [
        e.tensor_scalar(rnorm[:, h:h + 1], kvsum[:, 129 * h + 128:129 * h + 129],
                        1e-10, None, op0=ALU.add)
        for h in range(8)][-1])
    rn_ev = ctx.on("vector", [rn_ev], lambda e: e.reciprocal(rnorm[:], rnorm[:]))

    if debug:
        dkv = ctx.on("sync", [kbf_ev, rn_ev], lambda e: [
            e.dma_start(out=kvs_o[:], in_=kvsum[:]),
            e.dma_start(out=rn_o[:], in_=rnorm[:]),
        ])

    attn_done = None
    q_free = [wl_free[0], wl_free[1]]
    for qh in range(32):
        i = qh % 2
        ql = ctx.on("sync", [q_free[i]], lambda e, qh=qh, i=i: e.dma_start(
            out=wbuf[i][:, 0:T], in_=qphi_d[128 * qh:128 * (qh + 1), :]))

        def mk_attn(e, qh=qh, i=i):
            kvh = qh // 4
            mm = None
            for c in range(4):
                mm = e.matmul(pacc[:, 512 * c:512 * (c + 1)],
                              kvbf[:, 128 * kvh:128 * (kvh + 1)],
                              wbuf[i][:, 512 * c:512 * (c + 1)],
                              start=True, stop=True)
            return mm
        mm_ev = ctx.on("tensor", [ql, kbf_ev, pacc_free], mk_attn)
        q_free[i] = mm_ev
        ev = ctx.on("vector", [mm_ev, rn_ev], lambda e, qh=qh: e.tensor_scalar(
            xb2[:, T * qh:T * (qh + 1)], pacc[:], rnorm[:, qh // 4:qh // 4 + 1], None, op0=ALU.mult))
        pacc_free = ev
        attn_done = ev

    if debug:
        attn_done = ctx.on("sync", [attn_done], lambda e: [
            e.dma_start(out=attn_o[128 * f:128 * (f + 1), :], in_=xb2[:, T * f:T * (f + 1)])
            for f in range(NFT)])

    dfree = [[attn_done], [attn_done]]
    wl_free = [attn_done, attn_done]
    xl_ev = None
    stats_ev = None
    x1st = None
    for ot in range(32):
        i = ot % 2
        wl = ctx.on("gpsimd", [wl_free[i]], lambda e, ot=ot, i=i: e.dma_start(out=wbuf[i][:], in_=wo_p[ot]))
        xl = ctx.on("sync", [xl_ev], lambda e, ot=ot: [
            e.dma_start(out=xload[:], in_=xT[128 * ot:128 * (ot + 1), :]),
            e.dma_start(out=gld[:], in_=g0_d[128 * ot:128 * (ot + 1), :]),
        ])

        half_mm = [None, None]
        for hf in range(2):
            def mk_mm(e, ot=ot, i=i, hf=hf):
                mm = None
                for ft in range(NFT):
                    for c in (2 * hf, 2 * hf + 1):
                        mm = e.matmul(pacc[:, 512 * c:512 * (c + 1)],
                                      wbuf[i][:, 128 * ft:128 * (ft + 1)],
                                      xb2[:, T * ft + 512 * c:T * ft + 512 * (c + 1)],
                                      start=(ft == 0), stop=(ft == NFT - 1))
                return mm
            half_mm[hf] = ctx.on("tensor", [wl, attn_done, dfree[hf]], mk_mm)
        mm_ev = half_mm[1]
        wl_free[i] = mm_ev
        e1h = []
        for hf in range(2):
            sl = slice(1024 * hf, 1024 * (hf + 1))
            eh = ctx.on("vector", [half_mm[hf], cload, x1st], lambda e, ot=ot, sl=sl: e.tensor_scalar(
                sg[:, sl], pacc[:, sl], bobias[:, ot:ot + 1], None, op0=ALU.add))
            dfree[hf] = [eh]
            e1h.append(eh)
        e1 = e1h[1]
        if debug:
            sdbg = ctx.on("sync", [e1], lambda e, ot=ot:
                          e.dma_start(out=s0_o[128 * ot:128 * (ot + 1), :], in_=sg[:]))
            e1 = [e1, sdbg]
        e2 = ctx.on("vector", [e1h[0], e1h[1], xl], lambda e: [
            e.tensor_mul(gld[:], sg[:], gld[:]),
            e.tensor_add(xload[:], gld[:], xload[:]),
        ][-1])
        xl_ev = e2
        x1st = ctx.on("sync", [e2], lambda e, ot=ot:
                      e.dma_start(out=x1_o[128 * ot:128 * (ot + 1), :], in_=xload[:]))
        cst = ctx.on("vector", [e2], lambda e: e.tensor_copy(x1c[:], xload[:]))
        sqe = ctx.on("scalar", [e2, stats_ev], lambda e: e.activation(sqb[:], xload[:], AF.Square))
        xl_ev = [x1st, cst, sqe]

        def mk_stats(e, ot=ot):
            mm = None
            for tt in range(NTT):
                mm = e.matmul(pst[:, tt:tt + 1], x1c[:, 128 * tt:128 * (tt + 1)], onesc_bf[:],
                              start=(ot == 0 and tt == 0), stop=(ot == NFT - 1))
            for tt in range(NTT):
                mm = e.matmul(pst[:, 16 + tt:17 + tt], sqb[:, 128 * tt:128 * (tt + 1)], onesc_bb[:],
                              start=False, stop=(ot == NFT - 1))
            return mm
        stats_ev = ctx.on("tensor", [consts, cst, sqe], mk_stats)

    if debug:
        dps = ctx.on("vector", [stats_ev], lambda e: e.tensor_copy(sg[:, 0:64], pst[:]))
        ctx.on("sync", [dps], lambda e: e.dma_start(out=pst_dbg[:], in_=sg[:, 0:64]))
    ev = ctx.on("vector", [stats_ev], lambda e: [
        e.tensor_scalar(stt[:], pst[:, 0:16], 1.0 / 4096.0, None, op0=ALU.mult),
        e.tensor_scalar(st2[:], pst[:, 16:32], 1.0 / 4096.0, None, op0=ALU.mult),
    ][-1])
    ev = ctx.on("scalar", [ev], lambda e: e.activation(st3[:], stt[:], AF.Square))
    ev = ctx.on("vector", [ev], lambda e: e.tensor_sub(st2[:], st2[:], st3[:]))
    ev = ctx.on("scalar", [ev, consts], lambda e: [
        e.activation(st3[:], st2[:], AF.Ln, bias=epsb[:]),
        e.activation(st3[:], st3[:], AF.Exp, scale=-0.5),
    ][-1])
    rows_ev = ctx.on("vector", [ev], lambda e: e.tensor_mul(stt[:], stt[:], st3[:]))
    if debug:
        dfn = ctx.on("vector", [rows_ev], lambda e: [
            e.tensor_copy(sg[:, 0:16], stt[:]),
            e.tensor_copy(sg[:, 16:32], st2[:]),
            e.tensor_copy(sg[:, 32:48], st3[:]),
        ][-1])
        ctx.on("sync", [dfn], lambda e: e.dma_start(out=fin_dbg[:], in_=sg[:, 0:48]))

    bc_ev = None
    abc_ev = bbc_ev = None
    for src_col, dst in [(st3, abc), (stt, bbc)]:
        for tt in range(NTT):
            tev = ctx.on("tensor", [rows_ev, cload, bc_ev],
                         lambda e, tt=tt, s=src_col: e.transpose(ppr[0:1, 0:128], s[:, tt:tt + 1], identf[:]))
            cpe = ctx.on("vector", [tev], lambda e: e.tensor_copy(arow[:], ppr[0:1, 0:128]))
            bc_ev = ctx.on("tensor", [cpe], lambda e, tt=tt: e.matmul(
                pacc[:, 128 * tt:128 * (tt + 1)], ones1_f[:], arow[:],
                start=(tt % 4 == 0), stop=True))
        cpev = ctx.on("vector", [bc_ev], lambda e, dst=dst: e.tensor_copy(dst[:], pacc[:]))
        if src_col is st3:
            abc_ev = cpev
        else:
            bbc_ev = cpev
        bc_ev = cpev

    if debug:
        dab = ctx.on("vector", [abc_ev, bbc_ev], lambda e: [
            e.tensor_copy(sg[:, 0:32], abc[:, 0:32]),
            e.tensor_copy(sg[:, 32:64], bbc[:, 0:32]),
        ][-1])
        ctx.on("sync", [dab], lambda e: e.dma_start(out=abc_dbg[:], in_=sg[:, 0:64]))

    vstats_ev = None
    nx_ev = None
    xn_st = None
    for ft in range(NFT):
        lx = ctx.on("sync", [nx_ev], lambda e, ft=ft: e.dma_start(out=xload[:], in_=x1_o[128 * ft:128 * (ft + 1), :]))
        ne = ctx.on("vector", [lx, abc_ev, bbc_ev, xn_st], lambda e, ft=ft: [
            e.tensor_mul(sg[:], xload[:], abc[:]),
            e.tensor_sub(x1c[:], sg[:], bbc[:]),
        ][-1])
        nx_ev = ne
        xn_st = ctx.on("sync", [ne], lambda e, ft=ft:
                       e.dma_start(out=xn1_o[128 * ft:128 * (ft + 1), :], in_=x1c[:]))
        sqe = ctx.on("scalar", [ne, vstats_ev], lambda e: e.activation(sqb[:], x1c[:], AF.Square))

        def mk_vstats(e, ft=ft):
            mm = None
            for tt in range(NTT):
                mm = e.matmul(pst[:, 32 + tt:33 + tt], x1c[:, 128 * tt:128 * (tt + 1)], onesc_bf[:],
                              start=(ft == 0 and tt == 0), stop=(ft == NFT - 1))
            for tt in range(NTT):
                mm = e.matmul(pst[:, 48 + tt:49 + tt], sqb[:, 128 * tt:128 * (tt + 1)], onesc_bb[:],
                              start=False, stop=(ft == NFT - 1))
            return mm
        vstats_ev = ctx.on("tensor", [sqe, ne], mk_vstats)
        nx_ev = [nx_ev, sqe, vstats_ev]

    ev = ctx.on("vector", [vstats_ev], lambda e: [
        e.tensor_scalar(stt[:], pst[:, 32:48], 1.0 / 4096.0, None, op0=ALU.mult),
        e.tensor_scalar(st2[:], pst[:, 48:64], 1.0 / 4096.0, None, op0=ALU.mult),
    ][-1])
    ev = ctx.on("scalar", [ev], lambda e: e.activation(st3[:], stt[:], AF.Square))
    ev = ctx.on("vector", [ev], lambda e: e.tensor_sub(st2[:], st2[:], st3[:]))
    lg_ev = ctx.on("scalar", [ev, consts], lambda e: e.activation(st3[:], st2[:], AF.Ln, bias=oneb[:]))
    lg_st = ctx.on("sync", [lg_ev], lambda e: e.dma_start(out=lgv_o[:], in_=st3[:]))
    ev = ctx.on("vector", [lg_ev], lambda e: e.reduce_sum(stt[:, 0:1], st3[:], axis=AX.X))
    ev = ctx.on("tensor", [ev], lambda e: e.matmul(psc[0:1, 0:1], stt[:, 0:1], onesc_f[:],
                                                   start=True, stop=True))
    ev = ctx.on("vector", [ev], lambda e: e.memset(st2[0:1, 1:16], 0.0))
    ev = ctx.on("vector", [ev], lambda e: e.tensor_copy(st2[0:1, 0:1], psc[0:1, 0:1]))
    lp_st = ctx.on("sync", [ev], lambda e: e.dma_start(out=lgp_o[:], in_=st2[0:1, 0:16]))

    ctx.wait_only("gpsimd", [xn_st, lg_st, lp_st, x1st])
    if standalone:
        es.close()
    return nc


def _prep_l2(inp):
    if "w2ref" in _PC:
        return
    _PC["dTfull"] = np.ascontiguousarray(inp["L0_ffn_weight_data"].T)
    _PC["scbig"] = np.ascontiguousarray(np.repeat(inp["L0_ffn_weight_scales"].T, 64, axis=0))
    _PC["wo_p"] = pack_lhsT(np.ascontiguousarray(inp["L0_attn_Wo"].T))
    _PC["fb_c"] = np.ascontiguousarray(inp["L0_ffn_bias"].reshape(32, 128).T)
    _PC["bo_c"] = np.ascontiguousarray(inp["L0_attn_bo"].reshape(32, 128).T)
    _PC["w2ref"] = True


def l2a_in_map(inp, core):
    _prep_l2(inp)
    return {"dT": _PC["dTfull"][256 * core:256 * (core + 1)],
            "sc": _PC["scbig"][256 * core:256 * (core + 1)]}


def assemble_w2(res2a):
    w2 = np.empty((4096, 4096), dtype=BF)
    for c in range(8):
        w2[256 * c:256 * (c + 1)] = res2a.results[c]["w2lo"]
        w2[2048 + 256 * c:2048 + 256 * (c + 1)] = res2a.results[c]["w2hi"]
    return pack_lhsT(w2)


def l2b_in_map(inp, core, res1, w2_p):
    _prep_l2(inp)
    b, j = core // 4, core % 4
    x = inp["L0_attn_hidden_states"][b, T * j:T * (j + 1), :]
    kvp4 = np.stack([res1.results[4 * b + g]["kvp"] for g in range(4)])
    return {
        "rotT": res1.results[core]["rotT"],
        "qphiT": res1.results[core]["qphiT"],
        "kvp4": np.ascontiguousarray(kvp4),
        "w2_p": w2_p,
        "wo_p": _PC["wo_p"], "fb_c": _PC["fb_c"], "bo_c": _PC["bo_c"],
        "xT": np.ascontiguousarray(x.T),
        "ident": _PC["ident"],
    }



def build_launch3(debug=False, F=None):
    standalone = F is None
    if standalone:
        nc = bass.Bass("TRN2", target_bir_lowering=False)
        D = lambda name, shape, dtype, kind: nc.dram_tensor(name, shape, dtype, kind=kind).ap()
    else:
        nc = F.nc
        D = F.dram
    xn1_d = D("xn1T", [4096, T], BF16, "ExternalInput")
    wqkv_p = D("wqkv_p", [48, 128, 4096], F32, "ExternalInput")
    bqkv_c = D("bqkv_c", [128, 48], F32, "ExternalInput")
    wu1_p = D("wu1_p", [8, 128, 4096], F32, "ExternalInput")
    bu1_c = D("bu1_c", [128, 8], F32, "ExternalInput")
    wqr_pk = D("wqr_pk", [128, 2048], F32, "ExternalInput")
    wr_s = D("wr_s", [64, 48], F32, "ExternalInput")
    akT_d = D("akT", [64, 1024], F32, "ExternalInput")
    lr_d = D("logrel", [1, 1024], F32, "ExternalInput")
    av_p = D("av_p", [32, 128, 1024], F32, "ExternalInput")
    wvd_p = D("wvd_p", [2, 128, 4096], F32, "ExternalInput")
    wvu_p = D("wvu_p", [32, 128, 256], F32, "ExternalInput")
    wu2_s = D("wu2_s", [128, 8], F32, "ExternalInput")
    bu2_c = D("bu2_c", [128, 1], F32, "ExternalInput")
    ident_d = D("ident", [128, 128], BF16, "ExternalInput")

    q_o = D("qT", [2048, T], BF16, "ExternalOutput")
    k_o = D("kT", [2048, T], BF16, "ExternalOutput")
    if standalone:
        v_o = D("vt", [16, 16, 128, 128], BF16, "ExternalOutput")
    else:
        v_o = F.overrides["v2c"]
    aux3_o = D("aux3T", [4096, T], BF16, "ExternalOutput")
    lrn_o = D("learned", [128, 16], F32, "ExternalOutput")
    if debug:
        wrt_o = D("wrt", [16, 128, 1024], BF16, "ExternalOutput")
        auxo_o = D("auxoT", [4096, T], BF16, "ExternalOutput")
        rs_dbg = D("rs_dbg", [128, 1024], F32, "ExternalOutput")
        t8_dbg = D("t8_dbg", [128, 8], F32, "ExternalOutput")
        mk_dbg = D("mk_dbg", [128, 1024], F32, "ExternalOutput")
        ms_dbg = D("ms_dbg", [128, 1024], F32, "ExternalOutput")
        ex_dbg = D("ex_dbg", [128, 1024], F32, "ExternalOutput")
        ls_dbg = D("ls_dbg", [128, 2], F32, "ExternalOutput")
        u2_dbg = D("u2_dbg", [128, 16], F32, "ExternalOutput")
        as_dbg = D("as_dbg", [128, 1024], F32, "ExternalOutput")
        m1_dbg = D("m1_dbg", [128, T], BF16, "ExternalOutput")
        q_dbg = D("q_dbg", [64, T], BF16, "ExternalOutput")

    aux_d = D("aux_d", [4096, T], BF16, "Internal")
    m1_d = D("m1_d", [1024, T], BF16, "Internal")

    es = ExitStack() if standalone else F.es
    _pfx = "" if standalone else F.prefix
    S = lambda n, *a: es.enter_context(nc.sbuf_tensor(_pfx + n, *a))
    P = lambda n, *a: es.enter_context(nc.psum_tensor(_pfx + n, *a))
    xb = S("xb", [128, NFT * T], BF16)
    wbuf = [S(f"wbuf{i}", [128, 4096], BF16) for i in range(2)]
    wT = S("wT", [128, 8192], BF16)
    S1 = S("S1", [128, T], F32)
    S2 = S("S2", [128, T], F32)
    S3 = S("S3", [128, T], F32)
    qkvrow = S("qkvrow", [128, T], BF16)
    queryT = S("queryT", [64, T], BF16)
    q8T = S("q8T", [64, T], BF16)
    rqT = S("rqT", [48, T], BF16)
    rkT = S("rkT", [48, 1024], BF16)
    akT = S("akTs", [64, 1024], BF16)
    lrow = S("lrow", [1, 1024], BF16)
    wrow = None
    wrs = S("wrs", [64, 48], BF16)
    wu2c = S("wu2c", [128, 8], BF16)
    bqkv = S("bqkv", [128, 48], F32)
    bu1 = S("bu1", [128, 8], F32)
    bu2 = S("bu2", [128, 1], F32)
    st8 = S("st8", [128, 8], F32)
    rmx = S("rmx", [128, 1], F32)
    lsm = S("lsm", [128, 1], F32)
    rl = S("rl", [128, 1], F32)
    lrn = S("lrn", [128, 16], F32)
    vtm = S("vtm", [128, 128], BF16)
    ident = S("identS", [128, 128], BF16)
    ones1h = S("ones1h", [1, 128], BF16)
    wvul = S("wvul", [128, 256], BF16)

    wrow = qkvrow[:, 1024:2048]
    wqr = wT[:, 0:2048]

    pacc = P("pacc", [128, T], F32)
    prt = P("prt", [128, 1024], F32)
    pst = P("pst", [128, 16], F32)
    ptrk = P("ptrk", [128, 128], BF16)

    ctx = Ctx(nc, es) if standalone else F.ctx

    consts = ctx.on("vector", [], lambda e: e.memset(ones1h[:], 1.0))
    cload = ctx.on("gpsimd", [], lambda e: [
        e.dma_start(out=ident[:], in_=ident_d[:]),
        e.dma_start(out=bqkv[:], in_=bqkv_c[:]),
        e.dma_start(out=bu1[:], in_=bu1_c[:]),
        e.dma_start(out=bu2[:], in_=bu2_c[:]),
        e.dma_start(out=wrs[:], in_=wr_s[:]),
        e.dma_start(out=wqr, in_=wqr_pk[:]),
        e.dma_start(out=akT[:], in_=akT_d[:]),
        e.dma_start(out=lrow[:], in_=lr_d[:]),
        e.dma_start(out=wu2c[:], in_=wu2_s[:]),
    ])
    xl_ev = ctx.on("sync", [], lambda e: [
        e.dma_start(out=xb[:, T * ft:T * (ft + 1)], in_=xn1_d[128 * ft:128 * (ft + 1), :])
        for ft in range(NFT)])

    pacc_free = None
    efree = [None, None]
    wl_free = [None, None]
    row_free = None
    tr_free = None
    for ot in range(48):
        i = ot % 2
        wl = ctx.on("gpsimd", [wl_free[i]], lambda e, ot=ot, i=i: e.dma_start(out=wbuf[i][:], in_=wqkv_p[ot]))

        half_mm = [None, None]
        for hf in range(2):
            def mk_mm(e, ot=ot, i=i, hf=hf):
                mm = None
                for ft in range(NFT):
                    for c in (2 * hf, 2 * hf + 1):
                        mm = e.matmul(pacc[:, 512 * c:512 * (c + 1)],
                                      wbuf[i][:, 128 * ft:128 * (ft + 1)],
                                      xb[:, T * ft + 512 * c:T * ft + 512 * (c + 1)],
                                      start=(ft == 0), stop=(ft == NFT - 1))
                return mm
            half_mm[hf] = ctx.on("tensor", [wl, xl_ev, efree[hf]], mk_mm)
        mm_ev = half_mm[1]
        wl_free[i] = mm_ev
        evh = []
        for hf in range(2):
            sl = slice(1024 * hf, 1024 * (hf + 1))
            eh = ctx.on("vector", [half_mm[hf], cload, row_free], lambda e, ot=ot, sl=sl: e.tensor_scalar(
                qkvrow[:, sl], pacc[:, sl], bqkv[:, ot:ot + 1], None, op0=ALU.add))
            efree[hf] = [eh]
            evh.append(eh)
        ev = evh[1]
        pacc_free = ev
        if ot < 32:
            out_d = q_o if ot < 16 else k_o
            row = ot % 16
            for hf in range(2):
                sl = slice(1024 * hf, 1024 * (hf + 1))
                row_free = ctx.on("sync", [evh[hf]], lambda e, out_d=out_d, row=row, sl=sl:
                                  e.dma_start(out=out_d[128 * row:128 * (row + 1), sl], in_=qkvrow[:, sl]))
        else:
            ct = ot - 32
            vst = None
            for tt in range(NTT):
                trev = ctx.on("tensor", [ev, tr_free], lambda e, tt=tt:
                              e.transpose(ptrk[:], qkvrow[:, 128 * tt:128 * (tt + 1)], ident[:]))
                cpe = ctx.on("vector", [trev, vst], lambda e: e.tensor_copy(vtm[:], ptrk[:]))
                tr_free = cpe
                if standalone:
                    vst = ctx.on("sync", [cpe], lambda e, ct=ct, tt=tt:
                                 e.dma_start(out=v_o[ct, tt], in_=vtm[:]))
                else:
                    vst = ctx.on("sync", [cpe], lambda e, ct=ct, tt=tt:
                                 e.dma_start(out=v_o[ct, tt // 8, :, 128 * (tt % 8):128 * (tt % 8 + 1)],
                                             in_=vtm[:]))
            row_free = [vst, tr_free]

    def mk_q(e):
        mm = None
        for ft in range(NFT):
            for c in range(4):
                mm = e.matmul(pacc[0:64, 512 * c:512 * (c + 1)],
                              wqr[:, 64 * ft:64 * (ft + 1)],
                              xb[:, T * ft + 512 * c:T * ft + 512 * (c + 1)],
                              start=(ft == 0), stop=(ft == NFT - 1))
        return mm
    qm_ev = ctx.on("tensor", [cload, xl_ev, pacc_free], mk_q)
    qv_ev = ctx.on("vector", [qm_ev, row_free], lambda e: e.tensor_copy(queryT[:], pacc[0:64, :]))
    q8_ev = ctx.on("vector", [qv_ev], lambda e: e.tensor_scalar(q8T[:], pacc[0:64, :], 0.125, None, op0=ALU.mult))
    pacc_free = q8_ev
    if debug:
        ctx.on("sync", [qv_ev], lambda e: e.dma_start(out=q_dbg[:], in_=queryT[:]))

    def mk_rq(e):
        mm = None
        for c in range(4):
            mm = e.matmul(pacc[0:48, 512 * c:512 * (c + 1)], wrs[:], queryT[:, 512 * c:512 * (c + 1)],
                          start=True, stop=True)
        return mm
    ev = ctx.on("tensor", [qv_ev, pacc_free], mk_rq)
    rq_ev = ctx.on("vector", [ev], lambda e: e.tensor_copy(rqT[:], pacc[0:48, :]))
    pacc_free = rq_ev

    def mk_rk(e):
        mm = None
        for c in range(2):
            mm = e.matmul(pacc[0:48, 512 * c:512 * (c + 1)], wrs[:], akT[:, 512 * c:512 * (c + 1)],
                          start=True, stop=True)
        return mm
    ev = ctx.on("tensor", [rq_ev, cload], mk_rk)
    rk_ev = ctx.on("vector", [ev], lambda e: e.tensor_scalar(
        rkT[:], pacc[0:48, 0:1024], 0.14433756729740643, None, op0=ALU.mult))
    pacc_free = rk_ev

    aux_st = None
    avl_free = [wl_free[0], wl_free[1]]
    for half in range(2):
        wt_ev = None
        for ti in range(8):
            tt = 8 * half + ti

            def mk_rs(e, tt=tt):
                mm = None
                for c in range(2):
                    mm = e.matmul(prt[:, 512 * c:512 * (c + 1)],
                                  rqT[:, 128 * tt:128 * (tt + 1)],
                                  rkT[:, 512 * c:512 * (c + 1)], start=True, stop=True)
                for c in range(2):
                    mm = e.matmul(prt[:, 512 * c:512 * (c + 1)],
                                  ones1h[:], lrow[0:1, 512 * c:512 * (c + 1)],
                                  start=False, stop=True)
                return mm
            rs_ev = ctx.on("tensor", [rk_ev, q8_ev, wt_ev, consts], mk_rs)
            if debug and tt == 0:
                dttt = ctx.on("vector", [rs_ev], lambda e: e.tensor_copy(S2[:, 0:1024], prt[:]))
                ctx.on("sync", [dttt], lambda e: e.dma_start(out=rs_dbg[:], in_=S2[:, 0:1024]))
            mx_ev = ctx.on("vector", [rs_ev], lambda e: e.max(out=st8[:], in_=prt[:]))
            mk_ev = ctx.on("vector", [mx_ev], lambda e: e.tensor_scalar(
                S1[:, 0:1024], prt[:], st8[:, 7:8], None, op0=ALU.is_lt))
            if debug and tt == 0:
                ctx.on("sync", [mk_ev], lambda e: [
                    e.dma_start(out=t8_dbg[:], in_=st8[:]),
                    e.dma_start(out=mk_dbg[:], in_=S1[:, 0:1024]),
                ])

            def mk_as(e, tt=tt):
                mm = None
                for c in range(2):
                    mm = e.matmul(prt[:, 512 * c:512 * (c + 1)],
                                  q8T[:, 128 * tt:128 * (tt + 1)],
                                  akT[:, 512 * c:512 * (c + 1)], start=True, stop=True)
                for c in range(2):
                    mm = e.matmul(prt[:, 512 * c:512 * (c + 1)],
                                  ones1h[:], lrow[0:1, 512 * c:512 * (c + 1)],
                                  start=False, stop=True)
                return mm
            as_ev = ctx.on("tensor", [mk_ev, q8_ev, consts], mk_as)
            if debug and tt == 0:
                dtt2 = ctx.on("vector", [as_ev], lambda e: e.tensor_copy(S3[:, 0:1024], prt[:]))
                ctx.on("sync", [dtt2], lambda e: e.dma_start(out=as_dbg[:], in_=S3[:, 0:1024]))
            msk_ev = ctx.on("vector", [as_ev], lambda e: e.scalar_tensor_tensor(
                S2[:, 0:1024], S1[:, 0:1024], -60000.0, prt[:], op0=ALU.mult, op1=ALU.add))
            mxr_ev = ctx.on("vector", [msk_ev], lambda e: e.reduce_max(
                rmx[:], S2[:, 0:1024], axis=AX.X, negate=True))
            if debug and tt == 0:
                ctx.on("sync", [mxr_ev], lambda e: e.dma_start(out=ms_dbg[:], in_=S2[:, 0:1024]))
            exp_ev = ctx.on("scalar", [mxr_ev], lambda e: e.activation(
                S3[:, 0:1024], S2[:, 0:1024], AF.Exp, bias=rmx[:], accum_out=lsm[:]))
            if debug and tt == 0:
                dbb = ctx.on("vector", [exp_ev], lambda e: [
                    e.tensor_copy(st8[:, 0:1], rmx[:]),
                    e.tensor_copy(st8[:, 1:2], lsm[:]),
                ][-1])
                ctx.on("sync", [dbb, exp_ev], lambda e: [
                    e.dma_start(out=ex_dbg[:], in_=S3[:, 0:1024]),
                    e.dma_start(out=ls_dbg[:], in_=st8[:, 0:2]),
                ])
            rl_ev = ctx.on("scalar", [exp_ev], lambda e: [
                e.activation(rl[:], lsm[:], AF.Ln),
                e.activation(rl[:], rl[:], AF.Exp, scale=-1.0),
            ][-1])
            wr_ev = ctx.on("vector", [rl_ev], lambda e: e.tensor_scalar(
                wrow, S3[:, 0:1024], rl[:, 0:1], None, op0=ALU.mult))
            if debug:
                ctx.on("sync", [wr_ev], lambda e, tt=tt: e.dma_start(out=wrt_o[tt], in_=wrow))
            cp_ev = None
            for nt in range(8):
                trev = ctx.on("tensor", [wr_ev, cp_ev], lambda e, nt=nt:
                              e.transpose(ptrk[:], wrow[:, 128 * nt:128 * (nt + 1)], ident[:]))
                cp_ev = ctx.on("vector", [trev], lambda e, nt=nt, ti=ti: e.tensor_copy(
                    wT[:, 1024 * nt + 128 * ti:1024 * nt + 128 * (ti + 1)], ptrk[:]))
            wt_ev = cp_ev
        for ot in range(32):
            i = ot % 2
            avl = ctx.on("gpsimd", [avl_free[i]], lambda e, ot=ot, i=i: e.dma_start(
                out=wbuf[i][:, 0:1024], in_=av_p[ot]))

            def mk_aux(e, ot=ot, i=i, half=half):
                mm = None
                for nt in range(8):
                    for c in range(2):
                        mm = e.matmul(pacc[:, 512 * c:512 * (c + 1)],
                                      wbuf[i][:, 128 * nt:128 * (nt + 1)],
                                      wT[:, 1024 * nt + 512 * c:1024 * nt + 512 * (c + 1)],
                                      start=(nt == 0), stop=(nt == 7))
                return mm
            am_ev = ctx.on("tensor", [avl, wt_ev, pacc_free], mk_aux)
            avl_free[i] = am_ev
            ax_ev = ctx.on("vector", [am_ev, aux_st], lambda e: e.tensor_copy(qkvrow[:, 0:1024], pacc[:, 0:1024]))
            pacc_free = ax_ev
            aux_st = ctx.on("sync", [ax_ev], lambda e, ot=ot, half=half: e.dma_start(
                out=aux_d[128 * ot:128 * (ot + 1), 1024 * half:1024 * (half + 1)], in_=qkvrow[:, 0:1024]))
            if debug:
                aux_st = ctx.on("sync", [aux_st], lambda e, ot=ot, half=half: e.dma_start(
                    out=auxo_o[128 * ot:128 * (ot + 1), 1024 * half:1024 * (half + 1)], in_=qkvrow[:, 0:1024]))

    vb_ev = None
    ab_free = aux_st
    for vt_i in range(2):
        wl = ctx.on("gpsimd", [wl_free[0] if vt_i == 0 else vb_ev], lambda e, vt_i=vt_i:
                    e.dma_start(out=wbuf[0][:], in_=wvd_p[vt_i]))
        mm_last = None
        for ot in range(32):
            al = ctx.on("sync", [ab_free], lambda e, ot=ot: e.dma_start(
                out=wbuf[1][:, 0:2048], in_=aux_d[128 * ot:128 * (ot + 1), :]))

            def mk_vb(e, ot=ot, vt_i=vt_i):
                mm = None
                for c in range(4):
                    mm = e.matmul(pacc[:, 512 * c:512 * (c + 1)],
                                  wbuf[0][:, 128 * ot:128 * (ot + 1)],
                                  wbuf[1][:, 512 * c:512 * (c + 1)],
                                  start=(ot == 0), stop=(ot == 31))
                return mm
            mm_last = ctx.on("tensor", [al, wl, pacc_free if ot == 0 else None, mm_last], mk_vb)
            ab_free = mm_last
        vb_ev = ctx.on("vector", [mm_last], lambda e, vt_i=vt_i: e.tensor_copy(
            wT[:, 2048 * vt_i:2048 * (vt_i + 1)], pacc[:]))
        pacc_free = vb_ev

    a3_st = None
    wl2 = None
    for ot in range(32):
        wl2 = ctx.on("gpsimd", [vb_ev if ot == 0 else None, wl2], lambda e, ot=ot:
                     e.dma_start(out=wvul[:], in_=wvu_p[ot]))
        def mk_a3(e, ot=ot):
            mm = None
            for vt_i in range(2):
                for c in range(4):
                    mm = e.matmul(pacc[:, 512 * c:512 * (c + 1)],
                                  wvul[:, 128 * vt_i:128 * (vt_i + 1)],
                                  wT[:, 2048 * vt_i + 512 * c:2048 * vt_i + 512 * (c + 1)],
                                  start=(vt_i == 0), stop=(vt_i == 1))
            return mm
        mm_ev = ctx.on("tensor", [wl2, vb_ev, pacc_free], mk_a3)
        wl2 = mm_ev
        a3_ev = ctx.on("vector", [mm_ev, a3_st], lambda e: e.tensor_copy(qkvrow[:], pacc[:]))
        pacc_free = a3_ev
        a3_st = ctx.on("sync", [a3_ev], lambda e, ot=ot: e.dma_start(
            out=aux3_o[128 * ot:128 * (ot + 1), :], in_=qkvrow[:]))

    C1 = 0.7978845608028654
    C2 = C1 * 0.044715
    m1_st = None
    for ut in range(8):
        i = ut % 2
        wl = ctx.on("gpsimd", [wl_free[i]], lambda e, ut=ut, i=i:
                    e.dma_start(out=wbuf[i][:], in_=wu1_p[ut]))

        def mk_m1(e, ut=ut, i=i):
            mm = None
            for ft in range(NFT):
                for c in range(4):
                    mm = e.matmul(pacc[:, 512 * c:512 * (c + 1)],
                                  wbuf[i][:, 128 * ft:128 * (ft + 1)],
                                  xb[:, T * ft + 512 * c:T * ft + 512 * (c + 1)],
                                  start=(ft == 0), stop=(ft == NFT - 1))
            return mm
        mm_ev = ctx.on("tensor", [wl, pacc_free], mk_m1)
        wl_free[i] = mm_ev
        e1 = ctx.on("vector", [mm_ev, cload], lambda e, ut=ut: e.tensor_scalar(
            S1[:], pacc[:], bu1[:, ut:ut + 1], None, op0=ALU.add))
        pacc_free = e1
        e2 = ctx.on("scalar", [e1], lambda e: e.activation(S2[:], S1[:], AF.Square))
        e3 = ctx.on("vector", [e2], lambda e: e.tensor_scalar(S2[:], S2[:], C2, C1, op0=ALU.mult, op1=ALU.add))
        e4 = ctx.on("vector", [e3], lambda e: e.tensor_mul(S2[:], S1[:], S2[:]))
        e5 = ctx.on("scalar", [e4], lambda e: e.activation(S3[:], S2[:], AF.Tanh))
        e6 = ctx.on("vector", [e5], lambda e: e.tensor_scalar(S3[:], S3[:], 0.5, 0.5, op0=ALU.mult, op1=ALU.add))
        e7 = ctx.on("vector", [e6, m1_st], lambda e: e.tensor_mul(qkvrow[:], S1[:], S3[:]))
        m1_st = ctx.on("sync", [e7], lambda e, ut=ut: e.dma_start(
            out=m1_d[128 * ut:128 * (ut + 1), :], in_=qkvrow[:]))
        if debug and ut == 0:
            m1_st = ctx.on("sync", [m1_st], lambda e: e.dma_start(out=m1_dbg[:], in_=qkvrow[:]))

    u2_mm = None
    ml_ev = None
    for ut in range(8):
        ml_ev = ctx.on("sync", [(m1_st if ut == 0 else None),
                                (wl_free[0] if ut == 0 else None), u2_mm],
                       lambda e, ut=ut: e.dma_start(
            out=wbuf[0][:, 0:2048], in_=m1_d[128 * ut:128 * (ut + 1), :]))

        def mk_u2(e, ut=ut):
            mm = None
            for tt in range(NTT):
                mm = e.matmul(pst[:, tt:tt + 1], wbuf[0][:, 128 * tt:128 * (tt + 1)],
                              wu2c[:, ut:ut + 1], start=(ut == 0 and tt == 0), stop=(ut == 7))
            return mm
        u2_mm = ctx.on("tensor", [ml_ev, cload], mk_u2)
    if debug:
        du2 = ctx.on("vector", [u2_mm], lambda e: e.tensor_copy(S1[:, 0:16], pst[:, 0:16]))
        ctx.on("sync", [du2], lambda e: e.dma_start(out=u2_dbg[:], in_=S1[:, 0:16]))
    lr_ev = ctx.on("vector", [u2_mm], lambda e: e.tensor_scalar(
        lrn[:], pst[:, 0:16], bu2[:, 0:1], None, op0=ALU.add))
    lr_st = ctx.on("sync", [lr_ev], lambda e: e.dma_start(out=lrn_o[:], in_=lrn[:]))

    ctx.wait_only("gpsimd", [lr_st, a3_st, aux_st, m1_st])
    if standalone:
        es.close()
    return nc


def _prep_l3(inp):
    if "wqkv_p" in _PC:
        return
    _PC["wqkv_p"] = pack_lhsT(np.ascontiguousarray(inp["L1_attn_w_qkv"].T))
    _PC["bqkv_c"] = np.ascontiguousarray(inp["L1_attn_b_qkv"].reshape(48, 128).T)
    _PC["wu1_p"] = pack_lhsT(np.ascontiguousarray(inp["L1_ffn_Wu1"].T))
    _PC["bu1_c"] = np.ascontiguousarray(inp["L1_ffn_bu1"].reshape(8, 128).T)
    _PC["wqr_pk"] = np.ascontiguousarray(
        inp["L1_ffn_Wq"].T.reshape(32, 128, 64).transpose(1, 0, 2).reshape(128, 2048))
    _PC["wr_s"] = np.ascontiguousarray(inp["L1_ffn_Wr"].T)
    _PC["akT"] = np.ascontiguousarray(inp["L1_ffn_aux_keys"].T)
    _PC["logrel"] = np.ascontiguousarray(inp["L1_ffn_log_rel"].reshape(1, 1024))
    _PC["av_p"] = pack_lhsT(np.ascontiguousarray(inp["L1_ffn_aux_values"]))
    _PC["wvd_p"] = pack_lhsT(np.ascontiguousarray(inp["L1_ffn_Wvd"].T))
    _PC["wvu_p"] = pack_lhsT(np.ascontiguousarray(inp["L1_ffn_Wvu"].T))
    _PC["wu2_s"] = np.ascontiguousarray(inp["L1_ffn_Wu2"][0].reshape(8, 128).T)
    _PC["bu2_c"] = np.ascontiguousarray(np.broadcast_to(
        inp["L1_ffn_bu2"].reshape(1, 1), (128, 1)))


def l3_in_map(inp, core, xn1T):
    _prep_l3(inp)
    return {
        "xn1T": xn1T,
        "wqkv_p": _PC["wqkv_p"], "bqkv_c": _PC["bqkv_c"],
        "wu1_p": _PC["wu1_p"], "bu1_c": _PC["bu1_c"],
        "wqr_pk": _PC["wqr_pk"], "wr_s": _PC["wr_s"], "akT": _PC["akT"],
        "logrel": _PC["logrel"], "av_p": _PC["av_p"], "wvd_p": _PC["wvd_p"],
        "wvu_p": _PC["wvu_p"], "wu2_s": _PC["wu2_s"], "bu2_c": _PC["bu2_c"],
        "ident": _PC["ident"],
    }



def build_launch4(debug=False, n_heads=8, do_pass1=True, do_pass2=True, do_pv=True, do_l=True, do_diag=True, do_exp=True, do_accum=True, do_fin=True, fin_mode='all', nfb=None, F=None):
    standalone = F is None
    if standalone:
        nc = bass.Bass("TRN2", target_bir_lowering=False)
        D = lambda name, shape, dtype, kind: nc.dram_tensor(name, shape, dtype, kind=kind).ap()
    else:
        nc = F.nc
        D = F.dram
    qT_d = D("qT", [2048, T], BF16, "ExternalInput")
    ko_d = D("kT", [2048, T], BF16, "ExternalInput")
    if standalone:
        vo_d = D("vt", [16, 16, 128, 128], BF16, "ExternalInput")
    else:
        vo_d = F.overrides["v2c"]
    if standalone:
        kf_d = D("k_full", [2048, 8192], BF16, "ExternalInput")
    else:
        kf_d = F.overrides["k_gath"]
    if standalone:
        vf_d = D("v_full2", [16, 8, 128, 1024], BF16, "ExternalInput")
    else:
        vf_d = F.overrides["v_gath"]
    offs1_d = D("offs1", [128, 256], F32, "ExternalInput")
    sup2_d = D("sup2", [128, 256], F32, "ExternalInput")
    tri1_d = D("tri1", [4, 128, 512], F32, "ExternalInput")
    tri2_d = D("tri2", [4, 128, 512], F32, "ExternalInput")
    aux3_d = D("aux3T", [4096, T], BF16, "ExternalInput")
    xn1_d = D("xn1T", [4096, T], BF16, "ExternalInput")
    x1_d = D("x1T", [4096, T], F32, "ExternalInput")
    lgv_d = D("lgv", [128, 16], F32, "ExternalInput")
    if standalone:
        lgp8_d = D("lgp8", [1, 8], F32, "ExternalInput")
    else:
        lgp8_d = F.overrides["lgp_out"]
    lrn_d = D("learned", [128, 16], F32, "ExternalInput")
    gw1_d = D("gw1_c", [128, 1], F32, "ExternalInput")
    gb_d = D("gb_c", [128, 1], F32, "ExternalInput")
    wo1_p = D("wo1_p", [32, 128, 2048], F32, "ExternalInput")
    bo1_c = D("bo1_c", [128, 32], F32, "ExternalInput")
    ident_d = D("ident", [128, 128], BF16, "ExternalInput")

    out_o = D("outT", [4096, T], F32, "ExternalOutput")
    if debug:
        vals_o = D("vals_dbg", [2048, T], BF16, "ExternalOutput")
        gate_o = D("gate_dbg", [128, 16], F32, "ExternalOutput")
        s1_o = D("s1T", [4096, T], F32, "ExternalOutput")
        g1_o = D("g1T", [4096, T], F32, "ExternalOutput")
        mrun_o = D("mrun_dbg", [128, 16], F32, "ExternalOutput")
        l_o = D("l_dbg", [1, 512], F32, "ExternalOutput")
        sc1_o = D("sc1_dbg", [1, 16], F32, "ExternalOutput")
        gg_o = D("gg_dbg", [128, 16], F32, "ExternalOutput")

    es = ExitStack() if standalone else F.es
    _pfx = "" if standalone else F.prefix
    S = lambda n, *a: es.enter_context(nc.sbuf_tensor(_pfx + n, *a))
    P = lambda n, *a: es.enter_context(nc.psum_tensor(_pfx + n, *a))
    valsT = S("valsT_sb", [128, 16 * T], BF16)
    qh = S("qh", [128, 2 * T], BF16)
    ko = S("ko", [128, 2 * T], BF16)
    kf = [S(f"kf{i}", [128, 2048], BF16) for i in range(2)]
    vt = [S(f"vt{i}", [128, 2048], BF16) for i in range(2)]
    m_run = S("m_run", [128, 16], F32)
    m_all2 = S("m_all2", [128, 272], F32)
    m_bc = S("m_bc", [128, 512], F32)
    S1 = S("S1f", [128, T], F32)
    Pt = S("Pt", [128, 2048], BF16)
    mrow = S("mrow", [1, 512], F32)
    lrow = S("lrow", [1, 512], F32)
    tri1 = S("tri1s", [128, 4 * 512], F32)
    tri2 = S("tri2s", [128, 4 * 512], F32)
    offs1 = S("offs1s", [128, 256], F32)
    sup2 = S("sup2s", [128, 256], F32)
    lgv = S("lgvs", [128, 16], F32)
    lrn = S("lrns", [128, 16], F32)
    gg = S("ggs", [128, 16], F32)
    gh = S("ghs", [128, 16], F32)
    grgm = S("grgm", [128, 1], F32)
    gw1 = S("gw1s", [128, 1], F32)
    gb = S("gbs", [128, 1], F32)
    grow = S("grow", [1, T], F32)
    gate_bch = S("gate_bch", [128, T], BF16)
    wo_sb = [S(f"wo_sb{i}", [128, 2048], BF16) for i in range(2)]
    a3b = S("a3b", [128, T], BF16)
    xnb = S("xnb", [128, T], BF16)
    t1h = S("t1h", [128, T], BF16)
    x1b = S("x1b", [128, T], F32)
    S2 = S("S2f", [128, T], F32)
    g1f = S("g1f", [128, T], F32)
    onesc_h = S("onesc_h", [128, 1], BF16)
    ones1_f = S("ones1_f", [1, 128], F32)
    nones1_f = S("nones1_f", [1, 128], F32)
    epsb6 = S("epsb6", [1, 1], F32)
    bo1 = S("bo1s", [128, 32], F32)
    sc1 = S("sc1", [1, 16], F32)
    ident = S("identS", [128, 128], BF16)
    identf = S("identfS", [128, 128], F32)

    pA = P("pA", [128, 1024], F32)
    pO = P("pO", [128, 1024], F32)
    pL = P("pL", [128, 512], F32)
    pB = P("pB", [128, 512], F32)

    ctx = Ctx(nc, es) if standalone else F.ctx

    def mk_consts(e):
        e.memset(onesc_h[:], 1.0)
        e.memset(ones1_f[:], 1.0)
        e.memset(nones1_f[:], -1.0)
        last = e.memset(epsb6[:], 1e-6)
        if n_heads < 8:
            last = e.memset(valsT[:], 0.0)
        return last
    consts = ctx.on("vector", [], mk_consts)
    cload = ctx.on("gpsimd", [], lambda e: [
        e.dma_start(out=ident[:], in_=ident_d[:]),
        e.dma_start(out=tri1[:].rearrange("p (r c) -> p r c", r=4), in_=tri1_d[:].rearrange("r p c -> p r c")),
        e.dma_start(out=tri2[:].rearrange("p (r c) -> p r c", r=4), in_=tri2_d[:].rearrange("r p c -> p r c")),
        e.dma_start(out=offs1[:], in_=offs1_d[:]),
        e.dma_start(out=sup2[:], in_=sup2_d[:]),
        e.dma_start(out=lgv[:], in_=lgv_d[:]),
        e.dma_start(out=lrn[:], in_=lrn_d[:]),
        e.dma_start(out=gw1[:], in_=gw1_d[:]),
        e.dma_start(out=gb[:], in_=gb_d[:]),
        e.dma_start(out=bo1[:], in_=bo1_c[:]),
        e.dma_start(out=sc1[0:1, 0:8], in_=(lgp8_d[:] if standalone else lgp8_d[0:1, 0:8])),
    ])
    cload = ctx.on("vector", [cload], lambda e: e.tensor_copy(identf[:], ident[:]))

    if standalone:
        ev = ctx.on("vector", [cload], lambda e: e.reduce_sum(sc1[0:1, 8:9], sc1[0:1, 0:8], axis=AX.X))
    else:
        ev = ctx.on("vector", [cload], lambda e: e.tensor_copy(sc1[0:1, 8:9], sc1[0:1, 0:1]))
    ev = ctx.on("vector", [ev], lambda e: e.tensor_scalar(
        sc1[0:1, 9:10], sc1[0:1, 8:9], 1.0 / 16384.0, None, op0=ALU.mult))
    ev = ctx.on("vector", [ev], lambda e: e.tensor_scalar(
        sc1[0:1, 10:11], sc1[0:1, 9:10], epsb6[0:1, 0:1], None, op0=ALU.add))
    ev = ctx.on("scalar", [ev], lambda e: e.activation(sc1[0:1, 11:12], sc1[0:1, 10:11], AF.Ln))
    ev = ctx.on("scalar", [ev], lambda e: e.activation(sc1[0:1, 12:13], sc1[0:1, 11:12], AF.Exp, scale=-1.0))
    ev = ctx.on("tensor", [ev, consts], lambda e: e.matmul(
        pB[:, 0:1], ones1_f[:], sc1[0:1, 12:13], start=True, stop=True))
    ev = ctx.on("vector", [ev], lambda e: e.tensor_copy(grgm[:], pB[:, 0:1]))
    ev = ctx.on("vector", [ev], lambda e: e.tensor_scalar(
        gg[:], lgv[:], grgm[:, 0:1], 0.5, op0=ALU.mult, op1=ALU.mult))
    sg_ev = ctx.on("scalar", [cload], lambda e: e.activation(gh[:], lrn[:], AF.Sigmoid))
    ev = ctx.on("vector", [ev, sg_ev], lambda e: e.scalar_tensor_tensor(
        gg[:], gh[:], 2.5, gg[:], op0=ALU.mult, op1=ALU.add))
    ev = ctx.on("vector", [ev], lambda e: e.tensor_scalar(
        gg[:], gg[:], 0.0, 5.0, op0=ALU.max, op1=ALU.min))
    ev = ctx.on("vector", [ev], lambda e: e.tensor_scalar(
        gg[:], gg[:], 0.5, 1.0 / 1.5, op0=ALU.subtract, op1=ALU.mult))
    ev = ctx.on("vector", [ev], lambda e: e.tensor_scalar(
        gg[:], gg[:], 0.0, 1.0, op0=ALU.max, op1=ALU.min))
    if debug:
        dgg = ctx.on("sync", [ev], lambda e: e.dma_start(out=gg_o[:], in_=gg[:]))
        ev = [ev, dgg]
    ev = ctx.on("scalar", [ev], lambda e: e.activation(gh[:], gg[:], AF.Sigmoid,
                                                       bias=gb[:, 0:1], scale=gw1[:, 0:1]))
    if debug:
        dsc = ctx.on("sync", [ev], lambda e: e.dma_start(out=sc1_o[:], in_=sc1[:]))
    gate_ev = ctx.on("vector", [ev], lambda e: e.tensor_scalar(
        gg[:], gh[:], 0.05, None, op0=ALU.is_ge))
    gate_ev = ctx.on("vector", [gate_ev], lambda e: e.tensor_mul(gg[:], gg[:], gh[:]))
    if debug:
        ctx.on("sync", [gate_ev], lambda e: e.dma_start(out=gate_o[:], in_=gg[:]))
    gr_ev = gate_ev
    for tt in range(NTT):
        tev = ctx.on("tensor", [gr_ev, cload], lambda e, tt=tt:
                     e.transpose(pB[0:1, 0:128], gg[:, tt:tt + 1], identf[:]))
        gr_ev = ctx.on("vector", [tev], lambda e, tt=tt:
                       e.tensor_copy(grow[:, 128 * tt:128 * (tt + 1)], pB[0:1, 0:128]))
    gb_mm = None
    for c in range(4):
        gb_mm = ctx.on("tensor", [gr_ev, gb_mm], lambda e, c=c: e.matmul(
            pB[:, 0:512], ones1_f[:],
            grow[:, 512 * c:512 * (c + 1)], start=True, stop=True))
        gb_mm = ctx.on("vector", [gb_mm], lambda e, c=c: e.tensor_copy(
            gate_bch[:, 512 * c:512 * (c + 1)], pB[:, 0:512]))
    gatebc_ev = gb_mm

    vals_ev = None
    stt_prev = gb_mm
    exp_hist = [None, None]
    pv_hist = [None, None]
    pv_latest = None
    pO_free = None
    kq_ld = None
    ldk = None
    ldv = None
    for h in range(n_heads):
        head_stt0 = stt_prev
        kq_ld = ctx.on("sync", [kq_ld, vals_ev], lambda e, h=h: [
            e.dma_start(out=qh[:, 0:T], in_=qT_d[256 * h:256 * h + 128, :]),
            e.dma_start(out=qh[:, T:2 * T], in_=qT_d[256 * h + 128:256 * h + 256, :]),
            e.dma_start(out=ko[:, 0:T], in_=ko_d[256 * h:256 * h + 128, :]),
            e.dma_start(out=ko[:, T:2 * T], in_=ko_d[256 * h + 128:256 * h + 256, :]),
        ])
        p1ev = None
        if not do_pass1:
            p1ev = ctx.on("vector", [kq_ld], lambda e: e.memset(m_run[:], 0.0))
        dguard = [head_stt0, head_stt0]
        red_hist = [None, None]
        for qt in (range(NTT) if do_pass1 else []):
            oc_rel = qt // 4
            def mk_diag(e, qt=qt, oc=oc_rel):
                mm = None
                for ct in range(2):
                    mm = e.matmul(pA[:, 512 * (qt % 2):512 * (qt % 2 + 1)],
                                  qh[:, T * ct + 128 * qt:T * ct + 128 * (qt + 1)],
                                  ko[:, T * ct + 512 * oc:T * ct + 512 * (oc + 1)],
                                  start=(ct == 0), stop=(ct == 1))
                return mm
            dmm = ctx.on("tensor", [kq_ld, dguard[qt % 2], vals_ev], mk_diag)
            aev = ctx.on("vector", [dmm, cload, exp_hist[0], red_hist[qt % 2]],
                         lambda e, qt=qt: e.tensor_add(
                S1[:, 512 * (qt % 2):512 * (qt % 2 + 1)], pA[:, 512 * (qt % 2):512 * (qt % 2 + 1)],
                tri1[:, 512 * (qt % 4):512 * (qt % 4 + 1)]))
            dguard[qt % 2] = aev
            red_hist[qt % 2] = ctx.on("vector", [aev], lambda e, qt=qt: e.reduce_max(
                m_all2[:, 17 * qt + 16:17 * qt + 17],
                S1[:, 512 * (qt % 2):512 * (qt % 2 + 1)], axis=AX.X))
        kl1 = None
        p1bank = [dguard[0], dguard[1]]
        p1last = None
        for fc in (range(16) if do_pass1 else []):
            kf_guard = head_stt0 if fc < 2 else None
            if standalone:
                kl1 = ctx.on("sync", [kl1, p1last, kf_guard], lambda e, h=h, fc=fc: [
                    e.dma_start(out=kf[fc % 2][:, 0:512], in_=kf_d[256 * h:256 * h + 128, 512 * fc:512 * (fc + 1)]),
                    e.dma_start(out=kf[fc % 2][:, 1024:1536], in_=kf_d[256 * h + 128:256 * h + 256, 512 * fc:512 * (fc + 1)]),
                ])
            else:
                blk, loc = fc // 4, 512 * (fc % 4)
                kl1 = ctx.on("sync", [kl1, p1last, kf_guard], lambda e, h=h, fc=fc, blk=blk, loc=loc: [
                    e.dma_start(out=kf[fc % 2][:, 0:512], in_=kf_d[blk, 256 * h:256 * h + 128, loc:loc + 512]),
                    e.dma_start(out=kf[fc % 2][:, 1024:1536], in_=kf_d[blk, 256 * h + 128:256 * h + 256, loc:loc + 512]),
                ])
            for qt in range(NTT):
                bank = qt % 2
                def mk_p1(e, qt=qt, fc=fc, bank=bank):
                    mm = None
                    for ct in range(2):
                        mm = e.matmul(pA[:, 512 * bank:512 * (bank + 1)],
                                      qh[:, T * ct + 128 * qt:T * ct + 128 * (qt + 1)],
                                      kf[fc % 2][:, 1024 * ct:1024 * ct + 512],
                                      start=(ct == 0), stop=(ct == 1))
                    return mm
                mm1 = ctx.on("tensor", [kl1, p1bank[qt % 2]], mk_p1)
                red = ctx.on("vector", [mm1], lambda e, qt=qt, fc=fc: e.reduce_max(
                    m_all2[:, 17 * qt + fc:17 * qt + fc + 1],
                    pA[:, 512 * (qt % 2):512 * (qt % 2 + 1)], axis=AX.X))
                p1bank[qt % 2] = red
                p1last = red
        if do_pass1:
            for qt in range(NTT):
                ea = ctx.on("vector", [p1bank[0], p1bank[1], cload, vals_ev],
                            lambda e, qt=qt: e.tensor_add(
                    m_all2[:, 17 * qt:17 * qt + 16], m_all2[:, 17 * qt:17 * qt + 16],
                    offs1[:, 16 * qt:16 * qt + 16]))
                p1ev = ctx.on("vector", [ea], lambda e, qt=qt: e.reduce_max(
                    m_run[:, qt:qt + 1], m_all2[:, 17 * qt:17 * qt + 17], axis=AX.X))
        if debug and h == 0:
            ctx.on("sync", [p1ev], lambda e: e.dma_start(out=mrun_o[:], in_=m_run[:]))
        if not do_pass2:
            vals_ev = ctx.on("vector", [p1ev], lambda e: e.memset(valsT[:], 0.01))
            continue
        for qc in range(4):
            mr_ev = [p1ev, vals_ev]
            for i in range(4):
                tev = ctx.on("tensor", [mr_ev, cload], lambda e, i=i, qc=qc:
                             e.transpose(pB[0:1, 0:128], m_run[:, 4 * qc + i:4 * qc + i + 1], identf[:]))
                mr_ev = ctx.on("vector", [tev], lambda e, i=i: e.tensor_copy(
                    mrow[:, 128 * i:128 * (i + 1)], pB[0:1, 0:128]))
            bmm = ctx.on("tensor", [mr_ev], lambda e: e.matmul(
                pB[:, 0:512], nones1_f[:], mrow[:], start=True, stop=True))
            mbc_ev = ctx.on("vector", [bmm], lambda e: e.tensor_copy(m_bc[:], pB[:, 0:512]))

            assert nfb is None or nfb % 2 == 0
            n_all = nfb if nfb is not None else (68 if do_diag else 64)
            prev = None
            for pi, fb0 in enumerate(range(0, n_all, 2)):
                diag = fb0 >= 64
                pp = pi % 2
                if not diag and fb0 % 8 == 0:
                    g = fb0 // 8
                    if standalone:
                        ldk = ctx.on("sync", [ldk, mbc_ev if pi == 0 else None, stt_prev],
                                     lambda e, h=h, g=g: [
                            e.dma_start(out=kf[g % 2][:, 0:1024],
                                        in_=kf_d[256 * h:256 * h + 128, 1024 * g:1024 * (g + 1)]),
                            e.dma_start(out=kf[g % 2][:, 1024:2048],
                                        in_=kf_d[256 * h + 128:256 * h + 256, 1024 * g:1024 * (g + 1)]),
                        ])
                        ldv = ctx.on("gpsimd", [ldv, mbc_ev if pi == 0 else None, pv_latest],
                                     lambda e, h=h, g=g: [
                            e.dma_start(out=vt[g % 2][:, 0:1024], in_=vf_d[2 * h, g]),
                            e.dma_start(out=vt[g % 2][:, 1024:2048], in_=vf_d[2 * h + 1, g]),
                        ])
                    else:
                        blk, gl = g // 2, g % 2
                        ldk = ctx.on("sync", [ldk, mbc_ev if pi == 0 else None, stt_prev],
                                     lambda e, h=h, g=g, blk=blk, gl=gl: [
                            e.dma_start(out=kf[g % 2][:, 0:1024],
                                        in_=kf_d[blk, 256 * h:256 * h + 128, 1024 * gl:1024 * (gl + 1)]),
                            e.dma_start(out=kf[g % 2][:, 1024:2048],
                                        in_=kf_d[blk, 256 * h + 128:256 * h + 256, 1024 * gl:1024 * (gl + 1)]),
                        ])
                        ldv = ctx.on("gpsimd", [ldv, mbc_ev if pi == 0 else None, pv_latest],
                                     lambda e, h=h, g=g, blk=blk, gl=gl: [
                            e.dma_start(out=vt[g % 2][:, 0:1024], in_=vf_d[blk, 2 * h, gl]),
                            e.dma_start(out=vt[g % 2][:, 1024:2048], in_=vf_d[blk, 2 * h + 1, gl]),
                        ])
                if diag:
                    fd0 = fb0 - 64
                    if standalone:
                        ldv = ctx.on("gpsimd", [ldv, pv_latest], lambda e, h=h, qc=qc, fd0=fd0: [
                            d for k in range(2) for d in (
                                e.dma_start(out=vt[0][:, 128 * (fd0 + k):128 * (fd0 + k + 1)],
                                            in_=vo_d[2 * h, 4 * qc + fd0 + k]),
                                e.dma_start(out=vt[0][:, 1024 + 128 * (fd0 + k):1024 + 128 * (fd0 + k + 1)],
                                            in_=vo_d[2 * h + 1, 4 * qc + fd0 + k]),
                            )])
                    else:
                        ldv = ctx.on("gpsimd", [ldv, pv_latest], lambda e, h=h, qc=qc, fd0=fd0: [
                            d for k in range(2) for d in (
                                e.dma_start(out=vt[0][:, 128 * (fd0 + k):128 * (fd0 + k + 1)],
                                            in_=vo_d[2 * h, (4 * qc + fd0 + k) // 8, :,
                                                     128 * ((4 * qc + fd0 + k) % 8):128 * ((4 * qc + fd0 + k) % 8 + 1)]),
                                e.dma_start(out=vt[0][:, 1024 + 128 * (fd0 + k):1024 + 128 * (fd0 + k + 1)],
                                            in_=vo_d[2 * h + 1, (4 * qc + fd0 + k) // 8, :,
                                                     128 * ((4 * qc + fd0 + k) % 8):128 * ((4 * qc + fd0 + k) % 8 + 1)]),
                            )])
                    def mk_sc(e, qc=qc, fd0=fd0):
                        mm = None
                        for k in range(2):
                            for ct in range(2):
                                mm = e.matmul(pA[:, 512 * k:512 * (k + 1)],
                                              ko[:, T * ct + 512 * qc + 128 * (fd0 + k):T * ct + 512 * qc + 128 * (fd0 + k + 1)],
                                              qh[:, T * ct + 512 * qc:T * ct + 512 * (qc + 1)],
                                              start=(ct == 0), stop=(ct == 1))
                        return mm
                    sc_ev = ctx.on("tensor", [kq_ld, mbc_ev, stt_prev], mk_sc)
                    vbuf_pair = vt[0]
                else:
                    kbuf = kf[(fb0 // 8) % 2]
                    def mk_sc(e, qc=qc, fb0=fb0, kbuf=kbuf):
                        mm = None
                        for k in range(2):
                            koff = 128 * ((fb0 + k) % 8)
                            for ct in range(2):
                                mm = e.matmul(pA[:, 512 * k:512 * (k + 1)],
                                              kbuf[:, 1024 * ct + koff:1024 * ct + koff + 128],
                                              qh[:, T * ct + 512 * qc:T * ct + 512 * (qc + 1)],
                                              start=(ct == 0), stop=(ct == 1))
                        return mm
                    sc_ev = ctx.on("tensor", [ldk, mbc_ev, stt_prev], mk_sc)
                    vbuf_pair = vt[(fb0 // 8) % 2]
                if prev is not None:
                    pv_ev = ctx.on("tensor", prev["deps"], prev["mk_pv"])
                    pv_hist[prev["pp"]] = pv_ev
                    pv_latest = pv_ev
                def mk_stt(e, qc=qc, fb0=fb0, pp=pp, diag=diag):
                    op = None
                    for k in range(2):
                        sc_s = 0.0 if diag else sup2[:, 64 * qc + fb0 + k:64 * qc + fb0 + k + 1]
                        op = e.scalar_tensor_tensor(
                            S1[:, 1024 * pp + 512 * k:1024 * pp + 512 * (k + 1)],
                            pA[:, 512 * k:512 * (k + 1)], sc_s, m_bc[:],
                            op0=ALU.add, op1=ALU.add)
                    return op
                stt_ev = ctx.on("vector", [sc_ev, exp_hist[pp], cload], mk_stt)
                if diag:
                    fd0 = fb0 - 64
                    stt_ev = ctx.on("vector", [stt_ev], lambda e, fd0=fd0, pp=pp: [
                        e.tensor_add(S1[:, 1024 * pp + 512 * k:1024 * pp + 512 * (k + 1)],
                                     S1[:, 1024 * pp + 512 * k:1024 * pp + 512 * (k + 1)],
                                     tri2[:, 512 * (fd0 + k):512 * (fd0 + k + 1)])
                        for k in range(2)][-1])
                stt_prev = stt_ev
                exp_ev = ctx.on("scalar", [stt_ev, pv_hist[pp]], lambda e, pp=pp: e.activation(
                    Pt[:, 1024 * pp:1024 * (pp + 1)], S1[:, 1024 * pp:1024 * (pp + 1)], AF.Exp))
                exp_hist[pp] = exp_ev

                def mk_pv(e, fb0=fb0, pp=pp, diag=diag, vbuf=vbuf_pair, n_all=n_all):
                    mm = None
                    for k in range(2):
                        fb = fb0 + k
                        if diag:
                            fd = fb - 64
                            v0 = vbuf[:, 128 * fd:128 * (fd + 1)]
                            v1 = vbuf[:, 1024 + 128 * fd:1024 + 128 * (fd + 1)]
                        else:
                            v0 = vbuf[:, 128 * (fb % 8):128 * (fb % 8 + 1)]
                            v1 = vbuf[:, 1024 + 128 * (fb % 8):1024 + 128 * (fb % 8 + 1)]
                        st_ = (fb == 0) if do_accum else True
                        sp_ = (fb == n_all - 1) if do_accum else True
                        Pp = Pt[:, 1024 * pp + 512 * k:1024 * pp + 512 * (k + 1)]
                        if do_pv:
                            mm = e.matmul(pO[:, 0:512], v0, Pp, start=st_, stop=sp_)
                            mm = e.matmul(pO[:, 512:1024], v1, Pp, start=st_, stop=sp_)
                        if do_l or not do_pv:
                            mm = e.matmul(pL[0:1, 0:512], onesc_h[:], Pp, start=st_, stop=sp_)
                    return mm
                prev = {"pp": pp, "mk_pv": mk_pv,
                        "deps": [exp_ev, ldv, consts, pO_free if pi == 0 else None]}
            pv_ev = ctx.on("tensor", prev["deps"], prev["mk_pv"])
            pv_hist[prev["pp"]] = pv_ev
            pv_latest = pv_ev
            prev = None
            if not do_fin:
                vals_ev = ctx.on("vector", [pv_latest], lambda e, h=h, qc=qc: e.memset(
                    valsT[:, T * (2 * h) + 512 * qc:T * (2 * h) + 512 * (qc + 1)], 0.01))
                vals_ev = ctx.on("vector", [vals_ev], lambda e, h=h, qc=qc: e.memset(
                    valsT[:, T * (2 * h + 1) + 512 * qc:T * (2 * h + 1) + 512 * (qc + 1)], 0.01))
                pO_free = vals_ev
                continue
            lv_ev = ctx.on("vector", [pv_latest], lambda e: e.tensor_copy(lrow[:], pL[0:1, 0:512]))
            if debug and h == 0 and qc == 0:
                ctx.on("sync", [lv_ev], lambda e: e.dma_start(out=l_o[:], in_=lrow[:]))
            if fin_mode == "nolnexp":
                rl_ev = ctx.on("vector", [lv_ev], lambda e: e.tensor_scalar(
                    lrow[:], lrow[:], 1.0, None, op0=ALU.mult))
            elif fin_mode == "recipdve":
                rl_ev = ctx.on("vector", [lv_ev], lambda e: e.reciprocal(lrow[:], lrow[:]))
            else:
                rl_ev = ctx.on("scalar", [lv_ev], lambda e: [
                    e.activation(lrow[:], lrow[:], AF.Ln),
                    e.activation(lrow[:], lrow[:], AF.Exp, scale=-1.0),
                ][-1])
            rb_mm = ctx.on("tensor", [rl_ev], lambda e: e.matmul(
                pB[:, 0:512], ones1_f[:], lrow[:], start=True, stop=True))
            rb_ev = ctx.on("vector", [rb_mm, exp_hist[0]], lambda e: e.tensor_copy(S1[:, 0:512], pB[:, 0:512]))
            if fin_mode == "nomul":
                v0_ev = ctx.on("vector", [rb_ev], lambda e, h=h, qc=qc: e.memset(
                    valsT[:, T * (2 * h) + 512 * qc:T * (2 * h) + 512 * (qc + 1)], 0.01))
                v1_ev = ctx.on("vector", [v0_ev], lambda e, h=h, qc=qc: e.memset(
                    valsT[:, T * (2 * h + 1) + 512 * qc:T * (2 * h + 1) + 512 * (qc + 1)], 0.01))
            else:
                v0_ev = ctx.on("vector", [rb_ev], lambda e, h=h, qc=qc: e.tensor_mul(
                    valsT[:, T * (2 * h) + 512 * qc:T * (2 * h) + 512 * (qc + 1)],
                    pO[:, 0:512], S1[:, 0:512]))
                v1_ev = ctx.on("vector", [v0_ev], lambda e, h=h, qc=qc: e.tensor_mul(
                    valsT[:, T * (2 * h + 1) + 512 * qc:T * (2 * h + 1) + 512 * (qc + 1)],
                    pO[:, 512:1024], S1[:, 0:512]))
            pO_free = v1_ev
            vals_ev = v1_ev

    if debug:
        ctx.on("sync", [vals_ev], lambda e: [
            e.dma_start(out=vals_o[128 * ct:128 * (ct + 1), :], in_=valsT[:, T * ct:T * (ct + 1)])
            for ct in range(16)])

    st_ev = None
    s1free = [[vals_ev], [vals_ev]]
    wl_free = [None, None]
    ld_free = None
    ev_free = None
    for ot in range(32):
        i = ot % 2
        wl = ctx.on("gpsimd", [wl_free[i]], lambda e, ot=ot, i=i: e.dma_start(out=wo_sb[i][:], in_=wo1_p[ot]))
        ldx = ctx.on("sync", [ld_free], lambda e, ot=ot: [
            e.dma_start(out=a3b[:], in_=aux3_d[128 * ot:128 * (ot + 1), :]),
            e.dma_start(out=xnb[:], in_=xn1_d[128 * ot:128 * (ot + 1), :]),
            e.dma_start(out=x1b[:], in_=x1_d[128 * ot:128 * (ot + 1), :]),
        ])
        halves_ev = []
        for half in range(2):
            def mk_s1(e, ot=ot, i=i, half=half):
                mm = None
                for ct in range(16):
                    for c in range(2):
                        mm = e.matmul(pA[:, 512 * c:512 * (c + 1)],
                                      wo_sb[i][:, 128 * ct:128 * (ct + 1)],
                                      valsT[:, T * ct + 1024 * half + 512 * c:T * ct + 1024 * half + 512 * (c + 1)],
                                      start=(ct == 0), stop=(ct == 15))
                return mm
            mm_ev = ctx.on("tensor", [wl, vals_ev, ev_free if half == 0 and ot > 0 else None,
                                      halves_ev[0] if half == 1 else None], mk_s1)
            e1 = ctx.on("vector", [mm_ev, cload, st_ev], lambda e, ot=ot, half=half: e.tensor_scalar(
                S2[:, 1024 * half:1024 * (half + 1)], pA[:, 0:1024], bo1[:, ot:ot + 1], None, op0=ALU.add))
            halves_ev.append(e1)
        s1_ev = [halves_ev[0], halves_ev[1]]
        ev1 = ctx.on("vector", [ldx, gatebc_ev, st_ev], lambda e: e.tensor_mul(t1h[:], a3b[:], gate_bch[:]))
        ev2 = ctx.on("vector", [ev1], lambda e: e.tensor_add(t1h[:], xnb[:], t1h[:]))
        ev3 = ctx.on("scalar", [ev2], lambda e: e.activation(g1f[:], t1h[:], AF.Silu))
        dsg = None
        if debug:
            dsg = ctx.on("sync", [ev3, s1_ev], lambda e, ot=ot: [
                e.dma_start(out=s1_o[128 * ot:128 * (ot + 1), :], in_=S2[:]),
                e.dma_start(out=g1_o[128 * ot:128 * (ot + 1), :], in_=g1f[:]),
            ])
        ev4 = ctx.on("vector", [ev3, s1_ev, dsg if debug else None], lambda e: e.tensor_mul(S2[:], S2[:], g1f[:]))
        ev5 = ctx.on("vector", [ev4], lambda e: e.tensor_add(S2[:], S2[:], x1b[:]))
        ev_free = ev5
        ld_free = ev5
        if debug:
            ev5 = [ev5, dsg]
        st_ev = ctx.on("sync", [ev5], lambda e, ot=ot: e.dma_start(
            out=out_o[128 * ot:128 * (ot + 1), :], in_=S2[:]))

    ctx.wait_only("gpsimd", [st_ev])
    if standalone:
        es.close()
    return nc


def _prep_l4(inp):
    if "wo1_p" in _PC:
        return
    _PC["wo1_p"] = pack_lhsT(np.ascontiguousarray(inp["L1_attn_w_o"].T))
    _PC["bo1_c"] = np.ascontiguousarray(inp["L1_attn_b_o"].reshape(32, 128).T)
    _PC["gw1_c"] = np.ascontiguousarray(np.broadcast_to(
        np.asarray(inp["L1_ffn_gate_w1"], np.float32).reshape(1, 1), (128, 1)))
    _PC["gb_c"] = np.ascontiguousarray(np.broadcast_to(
        np.asarray(inp["L1_ffn_gate_bias"], np.float32).reshape(1, 1), (128, 1)))
    p = np.arange(128)[:, None]
    cc = np.arange(512)[None, :]
    tri1 = np.zeros((4, 128, 512), np.float32)
    tri2 = np.zeros((4, 128, 512), np.float32)
    for r in range(4):
        tri1[r] = np.where(cc <= 128 * r + p, 0.0, -1e9)
        tri2[r] = np.where(128 * r + p <= cc, 0.0, -1e9)
    _PC["tri1"] = tri1
    _PC["tri2"] = tri2


def l4_suppr(core):
    j = core % 4
    offs1 = np.zeros((128, 256), np.float32)
    for qt in range(16):
        oc = 4 * j + qt // 4
        for fc in range(16):
            offs1[:, 16 * qt + fc] = 0.0 if fc < oc else -1e9
    sup2 = np.zeros((128, 256), np.float32)
    for qc in range(4):
        own = 4 * (4 * j + qc)
        for fb in range(64):
            sup2[:, 64 * qc + fb] = 0.0 if fb < own else -1e9
    return offs1, sup2


def l4_in_map(inp, core, r2, r3, kfull_b, vfull_b, lgp8):
    _prep_l4(inp)
    offs1, sup2 = l4_suppr(core)
    return {
        "qT": r3["qT"], "kT": r3["kT"], "vt": r3["vt"],
        "k_full": kfull_b, "v_full2": vfull_b,
        "offs1": offs1, "sup2": sup2,
        "tri1": _PC["tri1"], "tri2": _PC["tri2"],
        "aux3T": r3["aux3T"], "xn1T": r2["xn1T"], "x1T": r2["x1T"],
        "lgv": r2["lgv"], "lgp8": lgp8, "learned": r3["learned"],
        "gw1_c": _PC["gw1_c"], "gb_c": _PC["gb_c"],
        "wo1_p": _PC["wo1_p"], "bo1_c": _PC["bo1_c"],
        "ident": _PC["ident"],
    }



def build_fused():
    nc = bass.Bass("TRN2", target_bir_lowering=False)
    es = ExitStack()
    ctx = Ctx(nc, es)
    I = lambda n, s, d: nc.dram_tensor(n, s, d, kind="Internal").ap()

    qphi_d = I("i_qphi", [4096, T], BF16)
    rot_d = I("i_rot", [4096, T], BF16)
    w2r_d = I("i_w2r", [4096, 4096], BF16)
    kvin = I("i_kvin", [1, 132096], F32)
    kvout = I("i_kvout", [1, 132096], F32)
    x1i = I("i_x1", [4096, T], F32)
    xn1i = I("i_xn1", [4096, T], BF16)
    lgvd = I("i_lgv", [128, 16], F32)
    lgpin = I("i_lgpin", [1, 16], F32)
    lgpout = I("i_lgpout", [1, 16], F32)
    lrni = I("i_lrn", [128, 16], F32)
    qdi = I("i_q", [2048, T], BF16)
    kin = I("i_kin", [1, 4194304], BF16)
    kout = I("i_kout", [4, 4194304], BF16)
    vin = I("i_vin", [1, 4194304], BF16)
    vout = I("i_vout", [4, 4194304], BF16)
    aux3i = I("i_aux3", [4096, T], BF16)

    kvin8 = kvin.rearrange("a (h p c) -> (a h) p c", h=8, p=128)
    kvout8 = kvout.rearrange("a (h p c) -> (a h) p c", h=8, p=128)
    kin2048 = kin.rearrange("a (r t) -> (a r) t", r=2048)
    kout3 = kout.rearrange("b (r t) -> b r t", r=2048)
    vin_v2c = vin.rearrange("a (ct g p c) -> (a ct) g p c", ct=16, g=2, p=128)
    vout_g = vout.rearrange("b (ct g p c) -> b ct g p c", ct=16, g=2, p=128)

    a_dT = nc.dram_tensor("a_dT", [2048, 4096], I32, kind="ExternalInput").ap()
    a_sc = nc.dram_tensor("a_sc", [2048, 4096], F32, kind="ExternalInput").ap()

    G = [[0, 1, 2, 3], [4, 5, 6, 7]]
    G8 = [[0, 1, 2, 3, 4, 5, 6, 7]]

    F0 = FusedEnv(nc, ctx, "a_", {"dT": a_dT, "sc": a_sc,
                                  "w2lo": w2r_d[0:2048, :], "w2hi": w2r_d[2048:4096, :]})
    build_launch2a(F=F0); F0.close(); ctx.barrier()

    F1 = FusedEnv(nc, ctx, "b_", {"kvp": kvin8, "qphiT": qphi_d, "rotT": rot_d})
    build_launch1(F=F1); F1.close(); ctx.barrier()

    ar_ev = ctx.on("gpsimd", [], lambda e: e.collective_compute(
        "AllReduce", ALU.add, replica_groups=G, ins=[kvin[:]], outs=[kvout[:]]),
        step_override=1)

    F2 = FusedEnv(nc, ctx, "c_", {"rotT": rot_d, "qphiT": qphi_d, "kv_red": kvout8,
                                  "w2rows": w2r_d, "x1T": x1i, "xn1T": xn1i,
                                  "lgv": lgvd, "lgp": lgpin})
    F2.collective_evs = {"kv_ar": ar_ev}
    build_launch2b(F=F2); F2.close(); ctx.barrier()

    lgp_ev = ctx.on("gpsimd", [], lambda e: e.collective_compute(
        "AllReduce", ALU.add, replica_groups=G8, ins=[lgpin[:]], outs=[lgpout[:]]),
        step_override=1)

    F3 = FusedEnv(nc, ctx, "d_", {"xn1T": xn1i, "qT": qdi, "kT": kin2048,
                                  "v2c": vin_v2c, "aux3T": aux3i, "learned": lrni})
    build_launch3(F=F3); F3.close(); ctx.barrier()

    ag1 = ctx.on("gpsimd", [], lambda e: e.collective_compute(
        "AllGather", ALU.bypass, replica_groups=G, ins=[kin[:]], outs=[kout[:]]),
        step_override=1)
    ag2 = ctx.on("gpsimd", [ag1], lambda e: e.collective_compute(
        "AllGather", ALU.bypass, replica_groups=G, ins=[vin[:]], outs=[vout[:]]),
        step_override=1)
    ctx.wait_only("sync", [ag1, ag2, lgp_ev])
    ctx.wait_only("gpsimd", [ag1, ag2, lgp_ev])

    F4 = FusedEnv(nc, ctx, "e_", {"qT": qdi, "kT": kin2048, "v2c": vin_v2c,
                                  "k_gath": kout3, "v_gath": vout_g,
                                  "aux3T": aux3i, "xn1T": xn1i, "x1T": x1i,
                                  "lgv": lgvd, "lgp_out": lgpout, "learned": lrni})
    build_launch4(F=F4); F4.close()

    es.close()
    return nc


def fused_in_map(inp, core):
    _prep_l1(inp); _prep_l2(inp); _prep_l3(inp); _prep_l4(inp)
    b, j = core // 4, core % 4
    x = inp["L0_attn_hidden_states"][b, T * j:T * (j + 1), :]
    xT = np.ascontiguousarray(x.T)
    m = {"a_dT": _PC["dTfull"], "a_sc": _PC["scbig"]}
    for k, v in l1_in_map(inp, core).items():
        m["b_" + k] = v
    m.update({"c_wo_p": _PC["wo_p"], "c_fb_c": _PC["fb_c"], "c_bo_c": _PC["bo_c"],
              "c_xT": xT, "c_ident": _PC["ident"]})
    d3 = l3_in_map(inp, core, None)
    d3.pop("xn1T")
    for k, v in d3.items():
        m["d_" + k] = v
    offs1, sup2 = l4_suppr(core)
    m.update({"e_offs1": offs1, "e_sup2": sup2, "e_tri1": _PC["tri1"], "e_tri2": _PC["tri2"],
              "e_gw1_c": _PC["gw1_c"], "e_gb_c": _PC["gb_c"], "e_wo1_p": _PC["wo1_p"],
              "e_bo1_c": _PC["bo1_c"], "e_ident": _PC["ident"]})
    return m


def kernel(**inputs):
    _PC.clear()
    inp = {k: np.asarray(v) for k, v in inputs.items()}
    nc = build_fused()
    res = run_bass_kernel_spmd(nc, [fused_in_map(inp, c) for c in range(8)],
                               core_ids=list(range(8)))
    out = np.empty((2, 8192, 4096), np.float32)
    for c in range(8):
        b, j = c // 4, c % 4
        out[b, T * j:T * (j + 1), :] = res.results[c]["e_outT"].T
    return out


def kernel_unfused(**inputs):
    _PC.clear()
    inp = {k: np.asarray(v) for k, v in inputs.items()}

    res1 = run_bass_kernel_spmd(build_launch1(), [l1_in_map(inp, c) for c in range(8)],
                                core_ids=list(range(8)))
    res2a = run_bass_kernel_spmd(build_launch2a(), [l2a_in_map(inp, c) for c in range(8)],
                                 core_ids=list(range(8)))
    w2_p = assemble_w2(res2a)
    res2b = run_bass_kernel_spmd(build_launch2b(), [l2b_in_map(inp, c, res1, w2_p) for c in range(8)],
                                 core_ids=list(range(8)))
    res3 = run_bass_kernel_spmd(build_launch3(),
                                [l3_in_map(inp, c, res2b.results[c]["xn1T"]) for c in range(8)],
                                core_ids=list(range(8)))
    r3 = [res3.results[c] for c in range(8)]
    r2 = [res2b.results[c] for c in range(8)]
    kfull = [np.ascontiguousarray(np.concatenate([r3[4 * b + j]["kT"] for j in range(4)], axis=1))
             for b in range(2)]
    vfull = []
    for b in range(2):
        vf = np.concatenate([r3[4 * b + j]["vt"] for j in range(4)], axis=1)
        vfull.append(np.ascontiguousarray(
            vf.reshape(16, 8, 8, 128, 128).transpose(0, 1, 3, 2, 4).reshape(16, 8, 128, 1024)))
    lgp8 = np.ascontiguousarray(np.concatenate([r2[c]["lgp"][:, 0:1] for c in range(8)], axis=1))
    res4 = run_bass_kernel_spmd(build_launch4(),
                                [l4_in_map(inp, c, r2[c], r3[c], kfull[c // 4], vfull[c // 4], lgp8)
                                 for c in range(8)],
                                core_ids=list(range(8)))
    out = np.empty((2, 8192, 4096), np.float32)
    for c in range(8):
        b, j = c // 4, c % 4
        out[b, T * j:T * (j + 1), :] = res4.results[c]["outT"].T
    return out
